# version 39
# baseline (speedup 1.0000x reference)
import contextlib
import numpy as np
import ml_dtypes
import concourse.bass as bass
import concourse.mybir as mybir
from concourse.bass_utils import run_bass_kernel_spmd

F32 = mybir.dt.float32
BF16 = mybir.dt.bfloat16
U8 = mybir.dt.uint8
AF = mybir.ActivationFunctionType
ALU = mybir.AluOpType
AX = mybir.AxisListType
BFDT = ml_dtypes.bfloat16

D = 1024
TOK = 2048
KV = 4096
NS = 16
IC = 8
H = 16
NE = 8
FF = 2048
EPS = 1e-5
SB_LIMIT = 207 * 1024

ENGINES = ("pe", "act", "dve", "pool", "sp")
QUEUES = {"qsp": "sp", "qact": "act", "qpool": "pool"}
NQSLOT = 8
QSLOT_NAMES = [f"{q}{i}" for q in QUEUES for i in range(NQSLOT)]
ISZ = {F32: 4, BF16: 2}


class Prog:

    def __init__(self):
        self.ops = []
        self.segs = {}

    def _access(self, space, lo, hi, opid, is_write):
        segs = self.segs.setdefault(space, [])
        deps = set()
        rest = []
        for seg in segs:
            slo, shi, w, rd = seg
            if shi <= lo or slo >= hi:
                rest.append(seg)
                continue
            if w >= 0:
                deps.add(w)
            if is_write:
                deps.update(rd)
            if slo < lo:
                rest.append([slo, lo, w, list(rd)])
            if shi > hi:
                rest.append([hi, shi, w, list(rd)])
            if not is_write:
                plo, phi = max(slo, lo), min(shi, hi)
                rest.append([plo, phi, w, rd + [opid]])
        if is_write:
            rest.append([lo, hi, opid, []])
        self.segs[space] = rest
        return deps

    def op(self, engine, emit, r=(), w=(), q=None, tag=""):
        i = len(self.ops)
        deps = set()
        for (sp, lo, hi) in r:
            deps |= self._access(sp, lo, hi, i, False)
        for (sp, lo, hi) in w:
            deps |= self._access(sp, lo, hi, i, True)
        deps.discard(i)
        self.ops.append({"eng": engine, "emit": emit, "deps": sorted(deps),
                         "q": q, "tag": tag, "event": None, "waits": None,
                         "prewait": None})
        return i

    def finalize(self):
        n = len(self.ops)
        needed = [False] * n
        for o in self.ops:
            for d in o["deps"]:
                if o["eng"] == "pe" and self.ops[d]["eng"] == "pe" \
                        and self.ops[d]["q"] is None and o["q"] is None:
                    continue
                needed[d] = True
        inc_count = {e: 0 for e in ENGINES}
        q_idx = {q: 0 for q in QUEUES}
        slot_last = {}
        for i, o in enumerate(self.ops):
            if o["q"] is not None:
                k = q_idx[o["q"]]
                q_idx[o["q"]] += 1
                slot, cyc = k % NQSLOT, k // NQSLOT
                sname = f"{o['q']}{slot}"
                o["event"] = (sname, 16 * (cyc + 1))
                o["prewait"] = (sname, 16 * cyc, slot_last.get(sname)) if cyc else None
                slot_last[sname] = i
            elif needed[i]:
                e = o["eng"]
                inc_count[e] += 1
                o["event"] = (e, inc_count[e])
        self.slot_total = {s: (self.ops[i]["event"][1]) for s, i in slot_last.items()}
        clock = {e: {} for e in ENGINES}
        opclock = [None] * n
        for i, o in enumerate(self.ops):
            E = o["eng"]
            ck = clock[E]
            waits = {}

            def need(s, v, src_op):
                if ck.get(s, 0) < v:
                    waits[s] = max(waits.get(s, 0), v)
                    ck[s] = v
                    dck = opclock[src_op] if src_op is not None else None
                    if dck:
                        for s2, v2 in dck.items():
                            if ck.get(s2, 0) < v2:
                                ck[s2] = v2

            pw = o.get("prewait")
            if pw:
                need(pw[0], pw[1], pw[2])
            for d in o["deps"]:
                dop = self.ops[d]
                if E == "pe" and dop["eng"] == "pe" and dop["q"] is None \
                        and o["q"] is None:
                    dck = opclock[d]
                    if dck:
                        for s2, v2 in dck.items():
                            if ck.get(s2, 0) < v2:
                                ck[s2] = v2
                    continue
                s, v = dop["event"]
                need(s, v, d)
            o["waits"] = sorted(waits.items())
            myck = dict(ck)
            if o["event"] is not None:
                s, v = o["event"]
                myck[s] = max(myck.get(s, 0), v)
            opclock[i] = myck

    def emit_engine(self, eng_name, e, sems):
        for o in self.ops:
            if o["eng"] != eng_name:
                continue
            for (s, v) in o["waits"]:
                e.wait_ge(sems[s], v)
            inst = o["emit"](e)
            if o["event"] is not None:
                s, _ = o["event"]
                inst.then_inc(sems[s], 16 if o["q"] is not None else 1)


class Alloc:
    def __init__(self, limit):
        self.free_segs = [(0, limit)]
        self.peak = 0

    def alloc(self, nbytes, align=64):
        nbytes = (nbytes + align - 1) // align * align
        for idx, (lo, hi) in enumerate(self.free_segs):
            lo_a = (lo + align - 1) // align * align
            if hi - lo_a >= nbytes:
                new = []
                if lo < lo_a:
                    new.append((lo, lo_a))
                if lo_a + nbytes < hi:
                    new.append((lo_a + nbytes, hi))
                self.free_segs[idx:idx + 1] = new
                self.peak = max(self.peak, lo_a + nbytes)
                return lo_a
        raise MemoryError(f"sbuf arena exhausted (want {nbytes})")

    def dealloc(self, off, nbytes, align=64):
        nbytes = (nbytes + align - 1) // align * align
        self.free_segs.append((off, off + nbytes))
        self.free_segs.sort()
        merged = []
        for seg in self.free_segs:
            if merged and merged[-1][1] >= seg[0]:
                merged[-1] = (merged[-1][0], max(merged[-1][1], seg[1]))
            else:
                merged.append(list(seg))
        self.free_segs = [tuple(s) for s in merged]


class Buf:
    def __init__(self, al, af32, abf, name, shape, dtype):
        self.name, self.shape, self.dtype = name, list(shape), dtype
        isz = ISZ[dtype]
        self.isz = isz
        per_part = int(np.prod(shape[1:])) * isz
        self.nbytes = per_part
        self.off = al.alloc(per_part)
        self.al = al
        base = af32 if dtype == F32 else abf
        flat = base.ap()[0:shape[0], self.off // isz: self.off // isz + int(np.prod(shape[1:]))]
        if len(shape) == 2:
            self.ap = flat
        elif len(shape) == 3:
            self.ap = flat.rearrange("p (a b) -> p a b", a=shape[1], b=shape[2])
        else:
            raise ValueError(shape)

    def spec(self, lo_elem=0, hi_elem=None):
        hi_elem = int(np.prod(self.shape[1:])) if hi_elem is None else hi_elem
        return ("sb", self.off + lo_elem * self.isz, self.off + hi_elem * self.isz)

    def spec3(self, a, e0, e1):
        inner = self.shape[2]
        return self.spec(a * inner + e0, a * inner + e1)

    def free(self):
        self.al.dealloc(self.off, self.nbytes)


def ps_spec(b0, b1=None):
    return ("ps", b0, (b0 + 1) if b1 is None else b1)


def build_nc():
    nc = bass.Bass("TRN2", target_bir_lowering=False)

    din = {}

    def dram_in(name, shape, dt=F32):
        din[name] = nc.dram_tensor(name, list(shape), dt, kind="ExternalInput").ap()

    dram_in("x0", [TOK, D]); dram_in("xsib", [TOK, D]); dram_in("xp3", [3, D])
    dram_in("cflag", [1, 1])
    dram_in("Wr", [D, 2]); dram_in("br", [2]); dram_in("step_bias", [3])
    dram_in("ln1_g", [D]); dram_in("ln1_b", [D]); dram_in("ln2_g", [D]); dram_in("ln2_b", [D])
    dram_in("ls1", [D]); dram_in("ls2", [D])
    dram_in("Wqkv", [D, 3 * D]); dram_in("Wout", [D, D]); dram_in("Wgate", [D, D])
    dram_in("Win", [D, 2 * D]); dram_in("conv_w", [D, 1, 4]); dram_in("conv_b", [D])
    dram_in("Wout_ssm", [D, D]); dram_in("Wg", [D, NE])
    dram_in("W1", [NE, D, FF]); dram_in("W2", [NE, FF, D]); dram_in("W3", [NE, D, FF])
    dram_in("G1R", [1, 32, 4, 32]); dram_in("G2R", [4, 16, 1, 16])
    dram_in("G1I", [1, 32, 4, 32]); dram_in("G2I", [4, 16, 1, 16])
    dram_in("bias_r", [512]); dram_in("bias_i", [512])
    dram_in("ident", [128, 128], BF16)
    dram_in("qsel", [32, 512]); dram_in("pselt", [32, 512])
    dram_in("rselt", [16, 128]); dram_in("ssel", [16, 512])

    out_d = nc.dram_tensor("out", [TOK, D], F32, kind="ExternalOutput").ap()
    sp_n0 = nc.dram_tensor("sp_n0", [TOK, D], F32, kind="Internal").ap()
    sp_gate = nc.dram_tensor("sp_gate", [TOK, D], F32, kind="Internal").ap()
    sp_ssm = nc.dram_tensor("sp_ssm", [TOK, D], F32, kind="Internal").ap()
    nm_scr = nc.dram_tensor("nm_scr", [H, 16, 128], BF16, kind="Internal").ap()
    wt_scr = nc.dram_tensor("wt_scr", [2, 128, 4, 512], BF16, kind="Internal").ap()
    sgz_scr = nc.dram_tensor("sgz_scr", [8, 128, TOK], BF16, kind="Internal").ap()

    st = contextlib.ExitStack()
    sems = {}
    for s in list(ENGINES) + QSLOT_NAMES:
        sems[s] = st.enter_context(nc.semaphore("sem_" + s))
    ARENA = st.enter_context(nc.sbuf_tensor("arena", [128, SB_LIMIT], U8))
    PSF = st.enter_context(nc.psum_tensor("psf", [128, 4096], F32))
    PSBV = PSF.bitcast(BF16)
    AF32 = ARENA.bitcast(F32)
    ABF = ARENA.bitcast(BF16)
    al = Alloc(SB_LIMIT)
    P = Prog()

    def B(name, shape, dtype):
        return Buf(al, AF32, ABF, name, shape, dtype)

    def psf_bank(b, p0=0, p1=128, c0=0, c1=512):
        return PSF.ap()[p0:p1, b * 512 + c0: b * 512 + c1]

    def psbb(b, p0=0, p1=128, c0=0, c1=1024):
        return PSBV.ap()[p0:p1, b * 1024 + c0: b * 1024 + c1]

    def dma(q, dst, src, r=(), w=(), tag="", slow=False):
        eng = QUEUES[q]
        if slow:
            def em(e, d=dst, s=src):
                return e.dma_start(d, s, allow_slow_non_contiguous=True)
        else:
            def em(e, d=dst, s=src):
                return e.dma_start(d, s)
        return P.op(eng, em, r=r, w=w, q=q, tag=tag)

    def bcast_load(q, dstbuf, src1d, tag="bc"):
        src = src1d.rearrange("(o n) -> o n", o=1).partition_broadcast(128)
        dma(q, dstbuf.ap, src, w=[dstbuf.spec()], tag=tag)

    def mm(out_ap, lhsT, rhs, start, stop, r, w, tag=""):
        return P.op("pe", lambda e, o=out_ap, l=lhsT, x=rhs, s=start, t=stop:
                    e.matmul(o, l, x, start=s, stop=t), r=r, w=w, tag=tag)

    const_pool = {}

    def get_const(v):
        if v not in const_pool:
            cb = Buf(al, AF32, ABF, f"const_{len(const_pool)}", [128, 1], F32)
            P.op("dve", lambda e, c=cb, vv=v: e.memset(c.ap, vv), r=[], w=[cb.spec()],
                 tag=f"cst_{v}")
            nc.const_aps.register(F32, float(v), cb.ap)
            const_pool[v] = cb
        return const_pool[v]

    def act(fn, out_ap, in_ap, r, w, bias=0.0, scale=1.0, accum=None, tag=""):
        rr = list(r)
        if isinstance(bias, float) and fn != AF.Copy:
            rr.append(get_const(bias).spec())
        ww = list(w) + ([] if accum is None else [])
        return P.op("act", lambda e, o=out_ap, i=in_ap, f=fn, b=bias, s=scale, a=accum:
                    e.activation(o, i, f, bias=b, scale=s, accum_out=a), r=rr, w=ww, tag=tag)

    def dve(fn, r, w, tag=""):
        return P.op("dve", fn, r=r, w=w, tag=tag)

    identb = B("identb", [128, 128], BF16)
    ones_f = B("ones_f", [128, 64], F32)
    convw = B("convw", [128, 8, 4], F32)
    convb = B("convb", [128, 8], F32)
    wr_s = B("wr_s", [128, 8, 2], BF16)
    wg_s = B("wg_s", [128, 8, 8], BF16)
    brt = B("brt", [128, 2], F32)
    sbt = B("sbt", [128, 3], F32)
    b1st = B("b1st", [128, 3], F32)
    cft = B("cft", [128, 1], F32)
    alpha = B("alpha", [128, 16], F32)
    wsum = B("wsum", [128, 16], F32)
    cwb = B("cwb", [128, 16, 8], F32)
    stt = B("stt", [128, 64], F32)
    st6 = B("st6", [128, 96], F32)
    hlt = B("hlt", [128, 3], F32)
    np3 = B("np3", [128, 8, 3], BF16)
    onesb = B("onesb", [128, 1], BF16)
    P.op("dve", lambda e: e.memset(onesb.ap, 1.0), r=[], w=[onesb.spec()], tag="onesb")
    for v_ in (0.0, float(EPS), float(EPS * EPS)):
        get_const(v_)

    def transp(out_ap, in_ap, r, w, tag="tr"):
        kdim = in_ap.shape[0]
        return P.op("pe", lambda e, o=out_ap, i=in_ap, d=identb.ap[0:kdim, 0:kdim]:
                    e.transpose(o, i, d), r=list(r) + [identb.spec()], w=w, tag=tag)

    dma("qsp", identb.ap, din["ident"], w=[identb.spec()], tag="ident")
    P.op("dve", lambda e: e.memset(ones_f.ap, 1.0), w=[ones_f.spec()], tag="ones")
    dma("qsp", convw.ap, din["conv_w"].rearrange("(g p) o k -> p g (o k)", p=128),
        w=[convw.spec()], tag="convw")
    dma("qsp", convb.ap, din["conv_b"].rearrange("(g p) -> p g", p=128),
        w=[convb.spec()], tag="convb", slow=True)
    dma("qpool", wr_s.ap, din["Wr"].rearrange("(g p) n -> p g n", p=128),
        w=[wr_s.spec()], tag="wr")
    dma("qpool", wg_s.ap, din["Wg"].rearrange("(g p) n -> p g n", p=128),
        w=[wg_s.spec()], tag="wgs")
    bcast_load("qsp", brt, din["br"])
    bcast_load("qsp", sbt, din["step_bias"])
    dma("qsp", cft.ap, din["cflag"].partition_broadcast(128), w=[cft.spec()], tag="cflag")
    dve(lambda e: e.tensor_scalar(b1st.ap, sbt.ap, brt.ap[:, 1:2], None, op0=ALU.add),
        r=[sbt.spec(), brt.spec()], w=[b1st.spec()], tag="b1st")

    def layer_norm(x_ap, x_spec, out_ap, out_spec, g=None, pdim=128, tag="ln", slot=0, aff_eng="dve"):
        sl = slice(0, pdim)
        c = 8 * slot
        s6 = 16 * slot

        def sc(a, b2):
            return stt.ap[sl, c + a:c + b2], stt.spec(c + a, c + b2)

        dve(lambda e: e.bn_stats(st6.ap[sl, s6:s6 + 6], x_ap[:, 0:512]),
            r=[x_spec], w=[st6.spec(s6, s6 + 6)], tag=tag + "b0")
        dve(lambda e: e.bn_stats(st6.ap[sl, s6 + 6:s6 + 12], x_ap[:, 512:1024]),
            r=[x_spec], w=[st6.spec(s6 + 6, s6 + 12)], tag=tag + "b1")
        dve(lambda e: e.bn_aggr(sc(4, 6)[0], st6.ap[sl, s6:s6 + 12]),
            r=[st6.spec(s6, s6 + 12)], w=[sc(4, 6)[1]], tag=tag + "ag")
        act(AF.Sqrt, sc(2, 3)[0], sc(5, 6)[0], r=[sc(5, 6)[1]],
            w=[sc(2, 3)[1]], bias=float(EPS), tag=tag + "sd")
        dve(lambda e: e.reciprocal(sc(1, 2)[0], sc(2, 3)[0]),
            r=[sc(2, 3)[1]], w=[sc(1, 2)[1]], tag=tag + "rs")
        if g is None:
            dve(lambda e: e.tensor_scalar(out_ap, x_ap, sc(4, 5)[0], sc(1, 2)[0],
                                          op0=ALU.subtract, op1=ALU.mult),
                r=[x_spec, sc(4, 5)[1], sc(1, 2)[1]], w=[out_spec], tag=tag + "ap")
        else:
            g_ap, g_spec, b_ap, b_spec, tmp_ap, tmp_spec = g
            dve(lambda e: e.tensor_scalar(tmp_ap, x_ap, sc(4, 5)[0], sc(1, 2)[0],
                                          op0=ALU.subtract, op1=ALU.mult),
                r=[x_spec, sc(4, 5)[1], sc(1, 2)[1]], w=[tmp_spec], tag=tag + "ap")
            P.op(aff_eng, lambda e: e.tensor_tensor(tmp_ap, tmp_ap, g_ap, op=ALU.mult),
                 r=[tmp_spec, g_spec], w=[tmp_spec], tag=tag + "gm")
            P.op(aff_eng, lambda e: e.tensor_tensor(out_ap, tmp_ap, b_ap, op=ALU.add),
                 r=[tmp_spec, b_spec], w=[out_spec], tag=tag + "ba")

    def layer_norm2_stages(x_ap, x_spec, n0_ap, n0_spec, nrm_ap, nrm_spec, g,
                           tag="ln2", slot=0):
        sl = slice(0, 128)
        c = 8 * slot
        s6 = 16 * slot

        def sc(a, b2):
            return stt.ap[sl, c + a:c + b2], stt.spec(c + a, c + b2)
        g_ap, g_spec, b_ap, b_spec, tmp_ap, tmp_spec = g
        return [
            lambda: dve(lambda e: e.bn_stats(st6.ap[sl, s6:s6 + 6], x_ap[:, 0:512]),
                        r=[x_spec], w=[st6.spec(s6, s6 + 6)], tag=tag + "b0"),
            lambda: dve(lambda e: e.bn_stats(st6.ap[sl, s6 + 6:s6 + 12], x_ap[:, 512:1024]),
                        r=[x_spec], w=[st6.spec(s6 + 6, s6 + 12)], tag=tag + "b1"),
            lambda: dve(lambda e: e.bn_aggr(sc(4, 6)[0], st6.ap[sl, s6:s6 + 12]),
                        r=[st6.spec(s6, s6 + 12)], w=[sc(4, 6)[1]], tag=tag + "ag"),
            lambda: act(AF.Sqrt, sc(2, 3)[0], sc(5, 6)[0], r=[sc(5, 6)[1]],
                        w=[sc(2, 3)[1]], bias=float(EPS), tag=tag + "sd"),
            lambda: act(AF.Sqrt, sc(3, 4)[0], sc(5, 6)[0], r=[sc(5, 6)[1]],
                        w=[sc(3, 4)[1]], bias=float(EPS * EPS), scale=float(1.0 + EPS),
                        tag=tag + "s2"),
            lambda: dve(lambda e: e.reciprocal(sc(1, 2)[0], sc(2, 3)[0]),
                        r=[sc(2, 3)[1]], w=[sc(1, 2)[1]], tag=tag + "rs"),
            lambda: dve(lambda e: e.reciprocal(sc(6, 7)[0], sc(3, 4)[0]),
                        r=[sc(3, 4)[1]], w=[sc(6, 7)[1]], tag=tag + "r2"),
            lambda: dve(lambda e: e.tensor_scalar(n0_ap, x_ap, sc(4, 5)[0], sc(1, 2)[0],
                                                  op0=ALU.subtract, op1=ALU.mult),
                        r=[x_spec, sc(4, 5)[1], sc(1, 2)[1]], w=[n0_spec], tag=tag + "ap"),
            lambda: dve(lambda e: e.tensor_scalar(tmp_ap, x_ap, sc(4, 5)[0], sc(6, 7)[0],
                                                  op0=ALU.subtract, op1=ALU.mult),
                        r=[x_spec, sc(4, 5)[1], sc(6, 7)[1]], w=[tmp_spec], tag=tag + "a2"),
            lambda: dve(lambda e: e.tensor_tensor(tmp_ap, tmp_ap, g_ap, op=ALU.mult),
                        r=[tmp_spec, g_spec], w=[tmp_spec], tag=tag + "gm"),
            lambda: dve(lambda e: e.tensor_tensor(nrm_ap, tmp_ap, b_ap, op=ALU.add),
                        r=[tmp_spec, b_spec], w=[nrm_spec], tag=tag + "ba"),
        ]

    def layer_norm2(x_ap, x_spec, n0_ap, n0_spec, nrm_ap, nrm_spec, g, pdim=128,
                    tag="ln2", slot=0, aff_eng="pool"):
        sl = slice(0, pdim)
        c = 8 * slot
        s6 = 16 * slot

        def sc(a, b2):
            return stt.ap[sl, c + a:c + b2], stt.spec(c + a, c + b2)

        dve(lambda e: e.bn_stats(st6.ap[sl, s6:s6 + 6], x_ap[:, 0:512]),
            r=[x_spec], w=[st6.spec(s6, s6 + 6)], tag=tag + "b0")
        dve(lambda e: e.bn_stats(st6.ap[sl, s6 + 6:s6 + 12], x_ap[:, 512:1024]),
            r=[x_spec], w=[st6.spec(s6 + 6, s6 + 12)], tag=tag + "b1")
        dve(lambda e: e.bn_aggr(sc(4, 6)[0], st6.ap[sl, s6:s6 + 12]),
            r=[st6.spec(s6, s6 + 12)], w=[sc(4, 6)[1]], tag=tag + "ag")
        act(AF.Sqrt, sc(2, 3)[0], sc(5, 6)[0], r=[sc(5, 6)[1]],
            w=[sc(2, 3)[1]], bias=float(EPS), tag=tag + "sd")
        act(AF.Sqrt, sc(3, 4)[0], sc(5, 6)[0], r=[sc(5, 6)[1]],
            w=[sc(3, 4)[1]], bias=float(EPS * EPS), scale=float(1.0 + EPS),
            tag=tag + "s2")
        dve(lambda e: e.reciprocal(sc(1, 2)[0], sc(2, 3)[0]),
            r=[sc(2, 3)[1]], w=[sc(1, 2)[1]], tag=tag + "rs")
        dve(lambda e: e.reciprocal(sc(6, 7)[0], sc(3, 4)[0]),
            r=[sc(3, 4)[1]], w=[sc(6, 7)[1]], tag=tag + "r2")
        dve(lambda e: e.tensor_scalar(n0_ap, x_ap, sc(4, 5)[0], sc(1, 2)[0],
                                      op0=ALU.subtract, op1=ALU.mult),
            r=[x_spec, sc(4, 5)[1], sc(1, 2)[1]], w=[n0_spec], tag=tag + "ap")
        g_ap, g_spec, b_ap, b_spec, tmp_ap, tmp_spec = g
        dve(lambda e: e.tensor_scalar(tmp_ap, x_ap, sc(4, 5)[0], sc(6, 7)[0],
                                      op0=ALU.subtract, op1=ALU.mult),
            r=[x_spec, sc(4, 5)[1], sc(6, 7)[1]], w=[tmp_spec], tag=tag + "a2")
        P.op(aff_eng, lambda e: e.tensor_tensor(tmp_ap, tmp_ap, g_ap, op=ALU.mult),
             r=[tmp_spec, g_spec], w=[tmp_spec], tag=tag + "gm")
        dve(lambda e: e.tensor_tensor(nrm_ap, tmp_ap, b_ap, op=ALU.add),
            r=[tmp_spec, b_spec], w=[nrm_spec], tag=tag + "ba")

    NT = B("NT", [128, IC, KV], BF16)
    XS = B("XS", [128, 8, 2052], BF16)
    zrs = [B(f"zr{i}", [128, 512], BF16) for i in range(2)]
    ln1g = B("ln1g", [128, D], F32)
    ln1b = B("ln1b", [128, D], F32)
    bcast_load("qsp", ln1g, din["ln1_g"])
    bcast_load("qsp", ln1b, din["ln1_b"])
    lntmp = B("lntmp", [128, D], F32)
    lntmp2 = B("lntmp2", [128, D], F32)
    x0p = B("x0p", [3, D], F32)
    nrmp = B("nrmp", [3, D], BF16)
    dma("qsp", x0p.ap, din["xp3"], w=[x0p.spec()], tag="xp3")
    layer_norm2(x0p.ap, x0p.spec(), lntmp.ap[0:3, :], lntmp.spec(),
                nrmp.ap, nrmp.spec(), pdim=3,
                g=(ln1g.ap[0:3, :], ln1g.spec(), ln1b.ap[0:3, :], ln1b.spec(),
                   lntmp2.ap[0:3, :], lntmp2.spec()), tag="p3")
    for ic in range(IC):
        tb_ = 6 + ic % 2
        transp(psbb(tb_, 0, 128, 0, 3), nrmp.ap[:, ic * 128:(ic + 1) * 128],
               r=[nrmp.spec(ic * 128, (ic + 1) * 128)], w=[ps_spec(tb_)], tag="p3t")
        dve(lambda e, i=ic, b=tb_: e.tensor_copy(np3.ap[:, i, :], psbb(b, 0, 128, 0, 3)),
            r=[ps_spec(tb_)], w=[np3.spec3(ic, 0, 3)], tag="p3e")

    x0p.free(); nrmp.free()
    x0r = [B(f"x0r{i}", [128, D], F32) for i in range(4)]
    n0r = [B(f"n0r{i}", [128, D], F32) for i in range(4)]
    nrmr = [B(f"nrm{i}", [128, D], BF16) for i in range(4)]
    lnt4 = [B(f"lnt4_{i}", [128, D], F32) for i in range(4)]
    gts = [B(f"gts{i}", [128, D], F32) for i in range(2)]
    rv = B("rv", [128, 16, 2], F32)
    winr = [B(f"winr{i}", [128, 8, 128], BF16) for i in range(2)]
    wgateS = B("wgateS", [128, 8, D], BF16)
    dma("qpool", wgateS.ap, din["Wgate"].rearrange("(g p) n -> p g n", p=128),
        w=[wgateS.spec()], tag="wgate")
    dve(lambda e: e.memset(XS.ap[:, :, 0:4], 0.0), r=[],
        w=[XS.spec3(c, 0, 4) for c in range(8)], tag="xspad")

    def p1_ln(c, comp_q):
        own = c < 4
        src = din["x0"] if own else din["xsib"]
        chains = []
        info = []
        for j in range(4):
            sg = c * 4 + j
            row0 = (c % 4) * 512 + j * 128
            xb = x0r[sg % 4]
            dma("qsp" if own else "qact", xb.ap, src[row0:row0 + 128, :],
                w=[xb.spec()], tag=f"x0_{sg}")
            n0b = n0r[sg % 4]
            nb = nrmr[sg % 4]
            ltb = lnt4[sg % 4]
            chains.append(layer_norm2_stages(
                xb.ap, xb.spec(), n0b.ap, n0b.spec(), nb.ap, nb.spec(),
                g=(ln1g.ap, ln1g.spec(), ln1b.ap, ln1b.spec(),
                   ltb.ap, ltb.spec()), tag=f"n_{sg}", slot=sg % 4))
            info.append((sg, row0, n0b, nb))
        for k in range(11):
            for ch in chains:
                ch[k]()
            if comp_q and k % 3 == 2:
                comp_q.pop(0)()
        for (sg, row0, n0b, nb) in info:
            if own:
                dma("qact", sp_n0[row0:row0 + 128, :], n0b.ap, r=[n0b.spec()],
                    w=[("hb_n0", row0, row0 + 128)], tag=f"spn0_{sg}")
            tcol = sg * 128 if own else 2048 + (sg - 16) * 128
            for ic in range(IC):
                tb_ = 6 + ic % 2
                transp(psbb(tb_, 0, 128, 0, 128), nb.ap[:, ic * 128:(ic + 1) * 128],
                       r=[nb.spec(ic * 128, (ic + 1) * 128)], w=[ps_spec(tb_)],
                       tag=f"tr_{sg}_{ic}")
                dve(lambda e, i=ic, t=tcol, b=tb_, nb2=nb: e.tensor_copy(
                    NT.ap[:, i, t:t + 128], psbb(b, 0, 128, 0, 128)),
                    r=[ps_spec(tb_)], w=[NT.spec3(ic, tcol, tcol + 128)], tag=f"te_{sg}_{ic}")
            for _ in range(2):
                if comp_q:
                    comp_q.pop(0)()

    def p1_compute(c):
        steps = []
        t0 = c * 512
        nt_r = [NT.spec3(i, t0, t0 + 512) for i in range(IC)]
        for cc in range(16):
            def step_cc(cc=cc):
                wb = winr[cc % 2]
                dma("qpool", wb.ap, din["Win"][:, cc * 128:(cc + 1) * 128]
                    .rearrange("(g p) n -> p g n", p=128), w=[wb.spec()],
                    tag=f"win_{c}_{cc}")
                bank = cc % 2
                for i in range(IC):
                    mm(psf_bank(bank), wb.ap[:, i, :], NT.ap[:, i, t0:t0 + 512],
                       i == 0, i == IC - 1, r=[wb.spec3(i, 0, 128), nt_r[i]],
                       w=[ps_spec(bank)], tag=f"xz_{c}_{cc}")
                if cc < 8:
                    dve(lambda e, ccc=cc, b=bank, tt=t0: e.tensor_copy(
                        XS.ap[:, ccc, 4 + tt: 4 + tt + 512], psf_bank(b)),
                        r=[ps_spec(bank)], w=[XS.spec3(cc, 4 + t0, 4 + t0 + 512)],
                        tag=f"xse_{c}_{cc}")
                else:
                    zr = zrs[cc % 2]
                    dve(lambda e, b=bank, z=zr: e.tensor_copy(z.ap, psf_bank(b)),
                        r=[ps_spec(bank)], w=[zr.spec()], tag=f"sgz_{c}_{cc}")
                    dma("qact", sgz_scr[cc - 8][:, t0:t0 + 512], zr.ap,
                        r=[zr.spec()], w=[("hb_z", (cc - 8) * 4 + t0 // 512,
                                          (cc - 8) * 4 + t0 // 512 + 1)],
                        tag=f"zs_{c}_{cc}")
                if c == 0 and cc < 8:
                    for i in range(IC):
                        mm(psf_bank(4, 0, 128, 0, 3), wb.ap[:, i, :], np3.ap[:, i, :],
                           i == 0, i == IC - 1,
                           r=[wb.spec3(i, 0, 128), np3.spec3(i, 0, 3)],
                           w=[ps_spec(4)], tag=f"ctx_{cc}")
                    dve(lambda e, ccc=cc: e.tensor_scalar(
                        XS.ap[:, ccc, 1:4], psf_bank(4, 0, 128, 0, 3), cft.ap, None,
                        op0=ALU.mult), r=[ps_spec(4), cft.spec()],
                        w=[XS.spec3(cc, 1, 4)], tag=f"ctxe_{cc}")
            steps.append(step_cc)
        for j in range(4):
            def step_j(j=j):
                sg2 = c * 4 + j
                tl = sg2 * 128
                for nh in range(2):
                    for i in range(IC):
                        mm(psf_bank(2 + nh), NT.ap[:, i, tl:tl + 128],
                           wgateS.ap[:, i, nh * 512:(nh + 1) * 512], i == 0, i == IC - 1,
                           r=[NT.spec3(i, tl, tl + 128),
                              wgateS.spec3(i, nh * 512, (nh + 1) * 512)],
                           w=[ps_spec(2 + nh)], tag=f"gate_{sg2}_{nh}")
                gb = gts[sg2 % 2]
                act(AF.Sigmoid, gb.ap, PSF.ap()[:, 2 * 512: 4 * 512],
                    r=[ps_spec(2), ps_spec(3)], w=[gb.spec()], tag=f"gsig_{sg2}")
                dma("qact", sp_gate[tl:tl + 128, :], gb.ap, r=[gb.spec()],
                    w=[("hb_gate", tl, tl + 128)], tag=f"gsp_{sg2}")
                for i in range(IC):
                    mm(psf_bank(4, 0, 128, 0, 2), NT.ap[:, i, tl:tl + 128],
                       wr_s.ap[:, i, :], i == 0, i == IC - 1,
                       r=[NT.spec3(i, tl, tl + 128), wr_s.spec3(i, 0, 2)],
                       w=[ps_spec(4)], tag=f"r_{sg2}")
                dve(lambda e, ss=sg2: e.tensor_copy(rv.ap[:, ss, :],
                                                    psf_bank(4, 0, 128, 0, 2)),
                    r=[ps_spec(4)], w=[rv.spec3(sg2, 0, 2)], tag=f"rv_{sg2}")
            steps.append(step_j)
        return steps

    def p1_halt(sg2):
        if True:
            if True:
                act(AF.Sigmoid, alpha.ap[:, sg2:sg2 + 1], rv.ap[:, sg2, 0:1],
                    r=[rv.spec3(sg2, 0, 2), brt.spec()], w=[alpha.spec(sg2, sg2 + 1)],
                    bias=brt.ap[:, 0:1], tag=f"al_{sg2}")
                for k in range(3):
                    act(AF.Sigmoid, hlt.ap[:, k:k + 1], rv.ap[:, sg2, 1:2],
                        r=[rv.spec3(sg2, 0, 2), b1st.spec()], w=[hlt.spec(k, k + 1)],
                        bias=b1st.ap[:, k:k + 1], tag=f"h_{sg2}_{k}")
                ca, cb_, cc_, cd = 32, 33, 34, 35
                dve(lambda e: e.tensor_scalar(stt.ap[:, ca:ca + 1], hlt.ap[:, 0:1],
                                              -1.0, 1.0, op0=ALU.mult, op1=ALU.add),
                    r=[hlt.spec(0, 1)], w=[stt.spec(ca, ca + 1)], tag=f"w0_{sg2}")
                dve(lambda e: e.tensor_scalar(stt.ap[:, ca:ca + 1], stt.ap[:, ca:ca + 1],
                                              0.0, None, op0=ALU.max),
                    r=[stt.spec(ca, ca + 1)], w=[stt.spec(ca, ca + 1)], tag=f"w1_{sg2}")
                dve(lambda e: e.tensor_tensor(stt.ap[:, cb_:cb_ + 1], hlt.ap[:, 1:2],
                                              stt.ap[:, ca:ca + 1], op=ALU.mult),
                    r=[hlt.spec(1, 2), stt.spec(ca, ca + 1)],
                    w=[stt.spec(cb_, cb_ + 1)], tag=f"w2_{sg2}")
                dve(lambda e: e.tensor_tensor(stt.ap[:, cc_:cc_ + 1], stt.ap[:, ca:ca + 1],
                                              stt.ap[:, cb_:cb_ + 1], op=ALU.subtract),
                    r=[stt.spec(ca, cb_ + 1)], w=[stt.spec(cc_, cc_ + 1)], tag=f"w3_{sg2}")
                dve(lambda e: e.tensor_scalar(stt.ap[:, cc_:cc_ + 1], stt.ap[:, cc_:cc_ + 1],
                                              0.0, None, op0=ALU.max),
                    r=[stt.spec(cc_, cc_ + 1)], w=[stt.spec(cc_, cc_ + 1)], tag=f"w4_{sg2}")
                dve(lambda e: e.tensor_tensor(stt.ap[:, cd:cd + 1], hlt.ap[:, 0:1],
                                              stt.ap[:, cb_:cb_ + 1], op=ALU.add),
                    r=[hlt.spec(0, 1), stt.spec(cb_, cb_ + 1)],
                    w=[stt.spec(cd, cd + 1)], tag=f"w5_{sg2}")
                dve(lambda e, ss=sg2: e.tensor_tensor(wsum.ap[:, ss:ss + 1],
                                                      stt.ap[:, cd:cd + 1],
                                                      stt.ap[:, cc_:cc_ + 1], op=ALU.add),
                    r=[stt.spec(cc_, cd + 1)], w=[wsum.spec(sg2, sg2 + 1)], tag=f"w6_{sg2}")

    p1_order = (0, 4, 1, 5, 2, 6, 3, 7)
    comp_q = []
    for c_ in p1_order:
        p1_ln(c_, comp_q)
        if c_ < 4:
            comp_q.extend(p1_compute(c_))
    while comp_q:
        comp_q.pop(0)()

    ln1g.free(); ln1b.free()
    for bb in x0r + n0r + nrmr + gts + winr + lnt4 + zrs:
        bb.free()
    wgateS.free(); lntmp.free(); lntmp2.free()

    y1s = [B(f"y1_{i}", [128, 1024], F32) for i in range(4)]
    y2s = [B(f"y2_{i}", [128, 1024], F32) for i in range(4)]
    silys = [B(f"sily{i}", [128, 1024], BF16) for i in range(4)]
    uh = B("uh", [128, 8, 1024], BF16)
    woS = B("woS", [128, 8, D], BF16)
    ssms = [B(f"ssms{i}", [128, D], F32) for i in range(2)]
    dma("qpool", woS.ap, din["Wout_ssm"].rearrange("(g p) n -> p g n", p=128),
        w=[woS.spec()], tag="wout_ssm")

    zls = [B(f"zl{i}", [128, 1024], BF16) for i in range(4)]
    halt_q = list(range(NS))
    sm_list = []
    for th in range(2):
        tb = th * 1024
        def conv_stages(ccc):
            y1, y2, sily = y1s[ccc % 4], y2s[ccc % 4], silys[ccc % 4]
            zl = zls[ccc % 4]

            def st0():
                p1_halt(halt_q.pop(0))
                dma("qact", zl.ap, sgz_scr[ccc][:, tb:tb + 1024],
                    r=[("hb_z", ccc * 4 + th * 2, ccc * 4 + th * 2 + 2)],
                    w=[zl.spec()], tag=f"zl_{th}_{ccc}")
            stg = [st0]
            stg.append(lambda: act(AF.Sigmoid, zl.ap, zl.ap, r=[zl.spec()],
                                   w=[zl.spec()], tag=f"sgs_{th}_{ccc}"))
            stg.append(lambda: dve(lambda e, cc2=ccc, t=tb, y=y1: e.tensor_scalar(
                y.ap, XS.ap[:, cc2, 1 + t:1 + t + 1024], convw.ap[:, cc2, 0:1], None,
                op0=ALU.mult),
                r=[XS.spec3(ccc, 1 + tb, 1 + tb + 1024), convw.spec()],
                w=[y1.spec()], tag=f"cv0_{th}_{ccc}"))
            pairs = [(y1, y2), (y2, y1), (y1, y2)]
            for j in range(1, 4):
                si, di_ = pairs[j - 1]
                stg.append(lambda j=j, si=si, di_=di_: dve(
                    lambda e, cc2=ccc, t=tb, jj=j, s=si, d=di_: e.scalar_tensor_tensor(
                        d.ap, XS.ap[:, cc2, 1 + t + jj:1 + t + jj + 1024],
                        convw.ap[:, cc2, jj:jj + 1], s.ap, op0=ALU.mult, op1=ALU.add),
                    r=[XS.spec3(ccc, 1 + tb + j, 1 + tb + j + 1024), convw.spec(),
                       si.spec()], w=[di_.spec()], tag=f"cv{j}_{th}_{ccc}"))
            stg.append(lambda: act(AF.Silu, sily.ap, y2.ap,
                                   r=[y2.spec(), convb.spec()], w=[sily.spec()],
                                   bias=convb.ap[:, ccc:ccc + 1], tag=f"si_{th}_{ccc}"))
            stg.append(lambda: P.op("pool", lambda e, cc2=ccc, sl_=sily, z=zl: e.tensor_tensor(
                uh.ap[:, cc2, :], sl_.ap, z.ap, op=ALU.mult),
                r=[sily.spec(), zl.spec()],
                w=[uh.spec3(ccc, 0, 1024)], tag=f"u_{th}_{ccc}"))
            return stg

        for c4_ in range(2):
            quads = [conv_stages(4 * c4_ + i) for i in range(4)]
            for k in range(len(quads[0])):
                for q_ in quads:
                    q_[k]()
        for j8 in range(8):
            def sm_unit(th=th, j8=j8):
                sg = th * 8 + j8
                tl = j8 * 128
                pb0 = 0 if sg % 2 == 0 else 4
                for cc2 in range(8):
                    for nh in range(2):
                        mm(psf_bank(pb0 + nh), uh.ap[:, cc2, tl:tl + 128],
                           woS.ap[:, cc2, nh * 512:(nh + 1) * 512], cc2 == 0, cc2 == 7,
                           r=[uh.spec3(cc2, tl, tl + 128),
                              woS.spec3(cc2, nh * 512, (nh + 1) * 512)],
                           w=[ps_spec(pb0 + nh)], tag=f"sm_{sg}_{nh}")
                sb_ = ssms[sg % 2]
                dve(lambda e, b=sb_, p=pb0: e.tensor_copy(b.ap, PSF.ap()[:, p * 512:(p + 2) * 512]),
                    r=[ps_spec(pb0, pb0 + 2)], w=[sb_.spec()], tag=f"sme_{sg}")
                row = sg * 128
                dma("qact", sp_ssm[row:row + 128, :], sb_.ap, r=[sb_.spec()],
                    w=[("hb_ssm", row, row + 128)], tag=f"ssp_{sg}")
            sm_list.append(sm_unit)

    XS.free()
    for bb in y1s + y2s + silys + zls:
        bb.free()

    kta = [B("kta0", [128, KV], BF16)]
    ktb = [B("ktb0", [128, KV], BF16)]
    qta = [B("qta0", [128, TOK], BF16)]
    qtb = [B("qtb0", [128, TOK], BF16)]
    v2 = [B("v2_0", [128, 32, 130], BF16)]
    vts = [B("vts_0", [128, KV], BF16)]
    wq2 = [B("wq2_0", [128, 8, 128], BF16)]
    wk2 = [B("wk2_0", [128, 8, 128], BF16)]
    wv2 = [B("wv2_0", [128, 8, 128], BF16)]
    nsq = B("nsq", [128, 1024], BF16)
    qstg = B("qstg", [128, 512], F32)
    nsc = B("nsc", [128, 8], F32)

    def init_kqv(par_):
        P.op("dve", lambda e, k=kta[par_]: e.memset(k.ap[64:128, :], 0.0),
             w=[kta[par_].spec()], tag=f"kz_{par_}")
        P.op("dve", lambda e, k=kta[par_]: e.memset(k.ap[64:65, :], -1.0),
             r=[kta[par_].spec()], w=[kta[par_].spec()], tag=f"k1_{par_}")
        P.op("dve", lambda e, k=qta[par_]: e.memset(k.ap[64:128, :], 0.0),
             w=[qta[par_].spec()], tag=f"qz_{par_}")
        P.op("dve", lambda e, v=v2[par_]: e.memset(v.ap, 0.0), w=[v2[par_].spec()],
             tag=f"vz_{par_}")
        P.op("dve", lambda e, v=v2[par_]: e.memset(v.ap[:, :, 64:65], 1.0),
             r=[v2[par_].spec()], w=[v2[par_].spec()], tag=f"vo1_{par_}")
        P.op("dve", lambda e, v=v2[par_]: e.memset(v.ap[:, :, 129:130], 1.0),
             r=[v2[par_].spec()], w=[v2[par_].spec()], tag=f"vo2_{par_}")

    init_kqv(0)

    def headb_move_step(hp, kind):
        par = hp % 2
        db_ = (qtb if kind == "q" else ktb)[par]

        def stp(db_=db_, kind=kind, hp=hp):
            dma("qsp", db_.ap[0:64, :], db_.ap[64:128, :], r=[db_.spec()],
                w=[db_.spec()], tag=f"{kind}mv_{hp}")
            P.op("pool", lambda e, k=db_: e.memset(k.ap[64:128, :], 0.0), r=[db_.spec()],
                 w=[db_.spec()], tag=f"{kind}zz_{hp}")
            if kind == "k":
                P.op("pool", lambda e, k=db_: e.memset(k.ap[64:65, :], -1.0), r=[db_.spec()],
                     w=[db_.spec()], tag=f"{kind}one_{hp}")
        return stp

    def proj_steps(hp):
        par = hp % 2
        wq, wk, wv = wq2[0], wk2[0], wv2[0]
        c0 = hp * 128
        dma("qpool", wq.ap, din["Wqkv"][:, c0:c0 + 128]
            .rearrange("(g p) n -> p g n", p=128), w=[wq.spec()], tag=f"wq_{hp}")
        dma("qpool", wk.ap, din["Wqkv"][:, D + c0:D + c0 + 128]
            .rearrange("(g p) n -> p g n", p=128), w=[wk.spec()], tag=f"wk_{hp}")
        dma("qpool", wv.ap, din["Wqkv"][:, 2 * D + c0:2 * D + c0 + 128]
            .rearrange("(g p) n -> p g n", p=128), w=[wv.spec()], tag=f"wv_{hp}")
        steps = []
        gctr = [0]

        def grp(kind, wbuf, da, db_, t0):
            def g(kind=kind, wbuf=wbuf, da=da, db_=db_, t0=t0):
                bank = 3 if gctr[0] % 2 == 0 else 7
                gctr[0] += 1
                for i in range(IC):
                    mm(psf_bank(bank), wbuf.ap[:, i, :], NT.ap[:, i, t0:t0 + 512],
                       i == 0, i == IC - 1,
                       r=[wbuf.spec3(i, 0, 128), NT.spec3(i, t0, t0 + 512)],
                       w=[ps_spec(bank)], tag=f"{kind}p_{hp}_{t0}")
                if kind == "v":
                    dve(lambda e, t=t0, b=bank: e.tensor_copy(
                        vts[0].ap[:, t:t + 512], psf_bank(b)),
                        r=[ps_spec(bank)], w=[vts[0].spec(t0, t0 + 512)],
                        tag=f"ve_{hp}_{t0}")
                else:
                    dve(lambda e, d=da, t=t0, b=bank: e.tensor_copy(
                        d.ap[0:64, t:t + 512], psf_bank(b, 0, 64)),
                        r=[ps_spec(bank)], w=[da.spec(t0, t0 + 512)],
                        tag=f"{kind}a_{hp}_{t0}")
                    dve(lambda e, d=db_, t=t0, b=bank: e.tensor_copy(
                        d.ap[64:128, t:t + 512], psf_bank(b, 64, 128)),
                        r=[ps_spec(bank)], w=[db_.spec(t0, t0 + 512)],
                        tag=f"{kind}b_{hp}_{t0}")
            return g

        for t0 in range(0, TOK, 512):
            steps.append(grp("q", wq, qta[par], qtb[par], t0))
        steps.append(headb_move_step(hp, "q"))
        for t0 in range(0, KV, 512):
            steps.append(grp("k", wk, kta[par], ktb[par], t0))
        steps.append(headb_move_step(hp, "k"))
        for t0 in range(0, KV, 512):
            steps.append(grp("v", wv, None, None, t0))
        return steps

    def v_steps(hp):
        par = hp % 2
        vv, vt = v2[par], vts[0]
        steps = []
        for ts_ in range(32):
            def step(ts2=ts_, vv2=vv, vt2=vt, hp2=hp):
                t0 = ts2 * 128
                transp(psbb(7, 0, 128, 0, 128), vt2.ap[:, t0:t0 + 128],
                       r=[vt2.spec(t0, t0 + 128)], w=[ps_spec(7)], tag=f"vt_{hp2}_{ts2}")
                dst = vv2.ap[:, ts2, :].rearrange("p (a b) -> p a b", a=2, b=65)[:, :, 0:64]
                srcp = psbb(7, 0, 128, 0, 128).rearrange("p (a b) -> p a b", a=2, b=64)
                dve(lambda e, d=dst, s2=srcp: e.tensor_copy(d, s2),
                    r=[ps_spec(7)], w=[vv2.spec3(ts2, 0, 130)], tag=f"va_{hp2}_{ts2}")
            steps.append(step)
        return steps

    def norm_steps(hp):
        par = hp % 2
        steps = []
        for qbuf, kbuf in ((qta[par], kta[par]), (qtb[par], ktb[par])):
            for h in range(4):
                def sq_k(kb2=kbuf, hp2=hp, h2=h):
                    dve(lambda e: e.tensor_tensor(
                        nsq.ap, kb2.ap[:, h2 * 1024:(h2 + 1) * 1024],
                        kb2.ap[:, h2 * 1024:(h2 + 1) * 1024], op=ALU.mult),
                        r=[kb2.spec()], w=[nsq.spec()], tag=f"ksq_{hp2}_{h2}")
                steps.append(sq_k)
                for j in range(2):
                    def kn_j(j2=j, hp2=hp, h2=h):
                        mm(PSF.ap()[64:65, 3 * 512:3 * 512 + 512], onesb.ap,
                           nsq.ap[:, j2 * 512:(j2 + 1) * 512], True, True,
                           r=[onesb.spec(), nsq.spec()], w=[ps_spec(3)],
                           tag=f"kn_{hp2}_{h2}_{j2}")
                        dve(lambda e: e.reduce_max(nsc.ap[64:65, 1:2],
                                                   PSF.ap()[64:65, 3 * 512:3 * 512 + 512],
                                                   axis=AX.X),
                            r=[ps_spec(3)], w=[nsc.spec(1, 2)], tag=f"knr_{hp2}_{h2}_{j2}")
                        if h2 == 0 and j2 == 0:
                            dve(lambda e: e.tensor_copy(nsc.ap[64:65, 0:1],
                                                        nsc.ap[64:65, 1:2]),
                                r=[nsc.spec(1, 2)], w=[nsc.spec(0, 1)], tag=f"kni_{hp2}")
                        else:
                            dve(lambda e: e.tensor_tensor(nsc.ap[64:65, 0:1],
                                                          nsc.ap[64:65, 0:1],
                                                          nsc.ap[64:65, 1:2], op=ALU.max),
                                r=[nsc.spec(0, 2)], w=[nsc.spec(0, 1)],
                                tag=f"knm_{hp2}_{h2}_{j2}")
                    steps.append(kn_j)

            def km(hp2=hp):
                act(AF.Ln, nsc.ap[64:65, 2:3], nsc.ap[64:65, 0:1],
                    r=[nsc.spec(0, 1)], w=[nsc.spec(2, 3)], tag=f"km_{hp2}")
            steps.append(km)

            for h in range(2):
                def sq_q(qb2=qbuf, hp2=hp, h2=h):
                    dve(lambda e: e.tensor_tensor(
                        nsq.ap[0:64, :], qb2.ap[0:64, h2 * 1024:(h2 + 1) * 1024],
                        qb2.ap[0:64, h2 * 1024:(h2 + 1) * 1024], op=ALU.mult),
                        r=[qb2.spec()], w=[nsq.spec()], tag=f"qsq_{hp2}_{h2}")
                steps.append(sq_q)
                for j in range(2):
                    def qn_j(j2=j, qb2=qbuf, hp2=hp, h2=h):
                        mm(PSF.ap()[64:65, 3 * 512:3 * 512 + 512], onesb.ap,
                           nsq.ap[:, j2 * 512:(j2 + 1) * 512], True, True,
                           r=[onesb.spec(), nsq.spec()], w=[ps_spec(3)],
                           tag=f"qn_{hp2}_{h2}_{j2}")
                        act(AF.Ln, qstg.ap[64:65, :],
                            PSF.ap()[64:65, 3 * 512:3 * 512 + 512],
                            r=[ps_spec(3)], w=[qstg.spec()], tag=f"ql_{hp2}_{h2}_{j2}")
                        dve(lambda e: e.tensor_scalar(qstg.ap[64:65, :],
                                                      qstg.ap[64:65, :],
                                                      nsc.ap[64:65, 2:3], None,
                                                      op0=ALU.add),
                            r=[qstg.spec(), nsc.spec(2, 3)], w=[qstg.spec()],
                            tag=f"qd_{hp2}_{h2}_{j2}")
                        q0_ = (h2 * 2 + j2) * 512
                        act(AF.Exp, qb2.ap[64:65, q0_:q0_ + 512],
                            qstg.ap[64:65, :], r=[qstg.spec()], w=[qb2.spec()],
                            scale=0.5, tag=f"qw_{hp2}_{h2}_{j2}")
                    steps.append(qn_j)
        return steps

    def st_pv_qb2(hp, qb, steps, pending, len0):
        par = hp % 2
        q0 = qb * 512
        vv = v2[par]
        cfg = []
        for hoff in range(2):
            cfg.append(((qta if hoff == 0 else qtb)[par],
                        (kta if hoff == 0 else ktb)[par],
                        0 if hoff == 0 else 4, 2 if hoff == 0 else 6,
                        ptbs[hoff], 65 * hoff))

        def S(hoff, kc):
            qsrc, ksrc, sb0, ob, ptbx, voff = cfg[hoff]
            bank = sb0 + kc % 2
            mm(psf_bank(bank), ksrc.ap[:, kc * 128:(kc + 1) * 128],
               qsrc.ap[:, q0:q0 + 512], True, True,
               r=[ksrc.spec(kc * 128, (kc + 1) * 128), qsrc.spec(q0, q0 + 512)],
               w=[ps_spec(bank)], tag=f"st_{hp}_{hoff}_{qb}_{kc}")

        def E(hoff, pp):
            qsrc, ksrc, sb0, ob, ptbx, voff = cfg[hoff]
            slot = (2 * pp) % 4
            act(AF.Exp, ptbx.ap[:, slot:slot + 2, :],
                PSF.ap()[:, sb0 * 512:(sb0 + 2) * 512],
                r=[ps_spec(sb0, sb0 + 2)],
                w=[ptbx.spec3(slot, 0, 512), ptbx.spec3(slot + 1, 0, 512)],
                scale=0.125, tag=f"ex_{hp}_{hoff}_{qb}_{pp}")

        def PV(hoff, kc):
            qsrc, ksrc, sb0, ob, ptbx, voff = cfg[hoff]
            slot = kc % 4
            mm(psf_bank(ob, 0, 65), vv.ap[:, kc, voff:voff + 65], ptbx.ap[:, slot, :],
               kc == 0, kc == 31,
               r=[vv.spec3(kc, voff, voff + 65), ptbx.spec3(slot, 0, 512)],
               w=[ps_spec(ob)], tag=f"pv_{hp}_{hoff}_{qb}_{kc}")

        idx0 = qb * 16
        for pp in range(16):
            for hoff in range(2):
                S(hoff, 2 * pp)
                S(hoff, 2 * pp + 1)
            E(0, pp)
            E(1, pp)
            if pp >= 1:
                for hoff in range(2):
                    PV(hoff, 2 * pp - 2)
                    PV(hoff, 2 * pp - 1)
            if pending and pp in (1, 2):
                pending.pop(0)()
            idx = idx0 + pp
            tgt = len0 if idx < 6 else len0 * (63 - idx) // 58
            while steps and len(steps) > tgt:
                steps.pop(0)()
        for hoff in range(2):
            PV(hoff, 30)
            PV(hoff, 31)
        for hoff in range(2):
            qsrc, ksrc, sb0, ob, ptbx, voff = cfg[hoff]
            tmpx, rzx = tmpas[hoff], rz0s[hoff]
            dve(lambda e, b=ob, t=tmpx: e.tensor_copy(t.ap, psf_bank(b, 0, 65)),
                r=[ps_spec(ob)], w=[tmpx.spec()], tag=f"fe_{hp}_{hoff}_{qb}")
            act(AF.Ln, rzx.ap[64:65, :], tmpx.ap[64:65, :],
                r=[tmpx.spec()], w=[rzx.spec()], tag=f"rl_{hp}_{hoff}_{qb}")
            act(AF.Exp, rzx.ap[64:65, :], rzx.ap[64:65, :],
                r=[rzx.spec()], w=[rzx.spec()], scale=-1.0,
                tag=f"rz_{hp}_{hoff}_{qb}")

            def part_b(hp2=hp, hoff2=hoff, qb2=qb, q02=q0, tmpx=tmpx, rzx=rzx):
                cb = 3 if hoff2 == 0 else 7
                mm(psf_bank(cb, 0, 64), ones_f.ap[64:65, 0:64], rzx.ap[64:65, :],
                   True, True, r=[ones_f.spec(), rzx.spec()], w=[ps_spec(cb)],
                   tag=f"bc_{hp2}_{hoff2}_{qb2}")
                if hoff2 == 0:
                    dve(lambda e: e.tensor_tensor(ATT.ap[0:64, hp2, q02:q02 + 512],
                                                  tmpx.ap[0:64, :], psf_bank(cb, 0, 64),
                                                  op=ALU.mult),
                        r=[tmpx.spec(), ps_spec(cb)], w=[ATT.spec3(hp2, q02, q02 + 512)],
                        tag=f"fn_{hp2}_{hoff2}_{qb2}")
                else:
                    dve(lambda e: e.tensor_tensor(nrmb.ap, tmpx.ap[0:64, :],
                                                  psf_bank(cb, 0, 64), op=ALU.mult),
                        r=[tmpx.spec(), ps_spec(cb)], w=[nrmb.spec()],
                        tag=f"fn_{hp2}_{hoff2}_{qb2}")
                    dma("qsp", ATT.ap[64:128, hp2, q02:q02 + 512], nrmb.ap, r=[nrmb.spec()],
                        w=[ATT.spec3(hp2, q02, q02 + 512)], tag=f"fv_{hp2}_{hoff2}_{qb2}")
            pending.append(part_b)

    def make_p0_steps():
        p0 = {}
        for nm, sh in [("g1r", [32, 128]), ("g1i", [32, 128]),
                       ("g2r", [16, 4, 16]), ("g2i", [16, 4, 16]),
                       ("qselb", [32, 512]), ("pseltb", [32, 4, 128]),
                       ("rseltb", [16, 128]), ("sselb", [16, 512]),
                       ("t1s", [32, 512]), ("t2s", [16, 512]),
                       ("e1s", [128, 512]), ("tmpw", [128, 512]),
                       ("wtRf", [128, 4, 512]), ("wtIf", [128, 4, 512])]:
            p0[nm] = B(nm, sh, F32)
        dma("qsp", p0["g1r"].ap, din["G1R"].rearrange("o b c d -> (o b) (c d)"), w=[p0["g1r"].spec()])
        dma("qsp", p0["g1i"].ap, din["G1I"].rearrange("o b c d -> (o b) (c d)"), w=[p0["g1i"].spec()])
        for c in range(4):
            dma("qsp", p0["g2r"].ap[:, c, :], din["G2R"][c].rearrange("e o g -> e (o g)"),
                w=[p0["g2r"].spec3(c, 0, 16)])
            dma("qsp", p0["g2i"].ap[:, c, :], din["G2I"][c].rearrange("e o g -> e (o g)"),
                w=[p0["g2i"].spec3(c, 0, 16)])
        dma("qsp", p0["qselb"].ap, din["qsel"], w=[p0["qselb"].spec()])
        dma("qsp", p0["pseltb"].ap, din["pselt"].rearrange("a (t n) -> a t n", t=4),
            w=[p0["pseltb"].spec()])
        dma("qsp", p0["rseltb"].ap, din["rselt"], w=[p0["rseltb"].spec()])
        dma("qsp", p0["sselb"].ap, din["ssel"], w=[p0["sselb"].spec()])

        steps = []
        for g1, g2, wtf in ((p0["g1r"], p0["g2r"], p0["wtRf"]),
                            (p0["g1i"], p0["g2i"], p0["wtIf"])):
            for c in range(4):
                def stepc(g1=g1, g2=g2, c=c):
                    mm(psf_bank(3, 0, 32), g1.ap[:, c * 32:(c + 1) * 32], p0["qselb"].ap,
                       True, True, r=[g1.spec(c * 32, (c + 1) * 32), p0["qselb"].spec()],
                       w=[ps_spec(3)], tag="t1")
                    dve(lambda e: e.tensor_copy(p0["t1s"].ap, psf_bank(3, 0, 32)),
                        r=[ps_spec(3)], w=[p0["t1s"].spec()], tag="t1e")
                    mm(psf_bank(7, 0, 16), g2.ap[:, c, :], p0["sselb"].ap,
                       True, True, r=[g2.spec3(c, 0, 16), p0["sselb"].spec()],
                       w=[ps_spec(7)], tag="t2")
                    dve(lambda e: e.tensor_copy(p0["t2s"].ap, psf_bank(7, 0, 16)),
                        r=[ps_spec(7)], w=[p0["t2s"].spec()], tag="t2e")
                steps.append(stepc)
                for t in range(4):
                    def stept(wtf=wtf, c=c, t=t):
                        mm(psf_bank(3), p0["pseltb"].ap[:, t, :], p0["t1s"].ap, True, True,
                           r=[p0["pseltb"].spec3(t, 0, 128), p0["t1s"].spec()],
                           w=[ps_spec(3)], tag="e1")
                        mm(psf_bank(7), p0["rseltb"].ap, p0["t2s"].ap, True, True,
                           r=[p0["rseltb"].spec(), p0["t2s"].spec()], w=[ps_spec(7)], tag="e2")
                        dve(lambda e: e.tensor_copy(p0["e1s"].ap, psf_bank(3)),
                            r=[ps_spec(3)], w=[p0["e1s"].spec()], tag="e1e")
                        if c == 0:
                            dve(lambda e, tt=t, wf=wtf: e.tensor_tensor(
                                wf.ap[:, tt, :], p0["e1s"].ap, psf_bank(7), op=ALU.mult),
                                r=[p0["e1s"].spec(), ps_spec(7)], w=[wtf.spec3(t, 0, 512)], tag="wt0")
                        else:
                            dve(lambda e: e.tensor_tensor(p0["tmpw"].ap, p0["e1s"].ap,
                                                          psf_bank(7), op=ALU.mult),
                                r=[p0["e1s"].spec(), ps_spec(7)], w=[p0["tmpw"].spec()], tag="wtm")
                            dve(lambda e, tt=t, wf=wtf: e.tensor_tensor(
                                wf.ap[:, tt, :], wf.ap[:, tt, :], p0["tmpw"].ap, op=ALU.add),
                                r=[p0["tmpw"].spec(), wtf.spec3(t, 0, 512)],
                                w=[wtf.spec3(t, 0, 512)], tag="wta")
                    steps.append(stept)

        def stepfin():
            dma("qpool", wt_scr[0], p0["wtRf"].ap, r=[p0["wtRf"].spec()],
                w=[("hb_wt", 0, 1)], tag="wtRs")
            dma("qpool", wt_scr[1], p0["wtIf"].ap, r=[p0["wtIf"].spec()],
                w=[("hb_wt", 1, 2)], tag="wtIs")
            for v in p0.values():
                v.free()
        steps.append(stepfin)
        return steps

    prep0 = proj_steps(0) + v_steps(0) + norm_steps(0)
    ksm = 0
    for i_, stp in enumerate(prep0):
        stp()
        if i_ % 6 == 5 and ksm < len(sm_list):
            sm_list[ksm]()
            ksm += 1
    while ksm < len(sm_list):
        sm_list[ksm]()
        ksm += 1
    uh.free(); woS.free()
    for bb in ssms:
        bb.free()
    ATT = B("ATT", [128, 8, TOK], BF16)
    kta.append(B("kta1", [128, KV], BF16))
    ktb.append(B("ktb1", [128, KV], BF16))
    qta.append(B("qta1", [128, TOK], BF16))
    qtb.append(B("qtb1", [128, TOK], BF16))
    v2.append(B("v2_1", [128, 32, 130], BF16))
    ptbs = [B(f"ptb{i}", [128, 4, 512], BF16) for i in range(2)]
    tmpas = [B(f"tmpa{i}", [65, 512], F32) for i in range(2)]
    rz0s = [B(f"rz0{i}", [128, 512], F32) for i in range(2)]
    nrmb = B("nrmb", [64, 512], BF16)
    init_kqv(1)
    pending = []
    for hp in range(8):
        if hp < 7:
            steps = proj_steps(hp + 1) + v_steps(hp + 1) + norm_steps(hp + 1)
        else:
            NT.free()
            steps = make_p0_steps()
        len0 = len(steps)
        for qb in range(4):
            st_pv_qb2(hp, qb, steps, pending, len0)
        for stp in steps:
            stp()
    for pb in pending:
        pb()

    for bb in kta + ktb + qta + qtb + v2 + vts + wq2 + wk2 + wv2 + ptbs + tmpas + rz0s:
        bb.free()
    nrmb.free()
    qstg.free(); nsc.free(); nsq.free()

    ACC2 = [B(f"ACC{i}", [128, NS // 2, D], F32) for i in range(2)]

    def ACCa(s):
        return ACC2[s // 8].ap[:, s % 8, :]

    def ACCs(s):
        return ACC2[s // 8].spec3(s % 8, 0, D)
    MIT = B("MIT", [128, IC, TOK], BF16)
    woutS = B("woutS", [128, 8, D], BF16)
    ln2g = B("ln2g", [128, D], F32)
    ln2b = B("ln2b", [128, D], F32)
    ls1t = B("ls1t", [128, D], F32)
    ls2t = B("ls2t", [128, D], F32)
    dma("qpool", woutS.ap, din["Wout"].rearrange("(g p) n -> p g n", p=128),
        w=[woutS.spec()], tag="wout")
    bcast_load("qsp", ln2g, din["ln2_g"])
    bcast_load("qsp", ln2b, din["ln2_b"])
    bcast_load("qsp", ls1t, din["ls1"])
    bcast_load("qsp", ls2t, din["ls2"])
    gr = [B(f"gr{i}", [128, D], F32) for i in range(2)]
    sr = [B(f"sr{i}", [128, D], F32) for i in range(2)]
    nr = [B(f"nr{i}", [128, D], F32) for i in range(2)]
    agbs = [B(f"agb{i}", [128, D], F32) for i in range(2)]
    mints = [B(f"mint{i}", [128, D], BF16) for i in range(2)]
    prbs = [B(f"prb{i}", [128, NE], F32) for i in range(2)]
    t8bs = [B(f"t8b{i}", [128, NE], F32) for i in range(2)]
    mskbs = [B(f"mskb{i}", [128, NE], F32) for i in range(2)]

    def p3_tile_stages(s):
        tl = s * 128
        par = s % 2
        gb, sb_, nb = gr[par], sr[par], nr[par]
        agb = agbs[par]
        mint = mints[par]
        prb, t8b, mskb = prbs[par], t8bs[par], mskbs[par]
        apb = par * 2
        g0 = 40 + 6 * par

        def gp(a, b2):
            return stt.ap[:, g0 + a:g0 + b2], stt.spec(g0 + a, g0 + b2)

        def st_loads():
            dma("qact", gb.ap, sp_gate[tl:tl + 128, :], r=[("hb_gate", tl, tl + 128)],
                w=[gb.spec()], tag=f"grl_{s}")
            dma("qsp", sb_.ap, sp_ssm[tl:tl + 128, :], r=[("hb_ssm", tl, tl + 128)],
                w=[sb_.spec()], tag=f"srl_{s}")
            dma("qpool", nb.ap, sp_n0[tl:tl + 128, :], r=[("hb_n0", tl, tl + 128)],
                w=[nb.spec()], tag=f"nrl_{s}")

        def st_ap():
            for nh in range(2):
                bank = apb + nh
                for cc in range(8):
                    mm(psf_bank(bank), ATT.ap[:, cc, tl:tl + 128],
                       woutS.ap[:, cc, nh * 512:(nh + 1) * 512], cc == 0, cc == 7,
                       r=[ATT.spec3(cc, tl, tl + 128),
                          woutS.spec3(cc, nh * 512, (nh + 1) * 512)],
                       w=[ps_spec(bank)], tag=f"ap_{s}_{nh}")

        stages = [st_loads, st_ap]
        stages.append(lambda: dve(lambda e, g=gb, p=apb, a=agb: e.tensor_tensor(
            a.ap, PSF.ap()[:, p * 512:(p + 2) * 512], g.ap, op=ALU.mult),
            r=[ps_spec(apb, apb + 2), gb.spec()], w=[agb.spec()], tag=f"ag_{s}"))
        stages.append(lambda: dve(lambda e, b=sb_, a=agb: e.tensor_tensor(
            a.ap, a.ap, b.ap, op=ALU.subtract),
            r=[agb.spec(), sb_.spec()], w=[agb.spec()], tag=f"d_{s}"))
        stages.append(lambda: dve(lambda e, b=sb_, ss=s, a=agb: e.scalar_tensor_tensor(
            a.ap, a.ap, alpha.ap[:, ss:ss + 1], b.ap, op0=ALU.mult, op1=ALU.add),
            r=[agb.spec(), alpha.spec(s, s + 1), sb_.spec()], w=[agb.spec()],
            tag=f"bl_{s}"))
        stages.append(lambda: dve(lambda e, a=agb: e.tensor_tensor(
            a.ap, a.ap, ls1t.ap, op=ALU.mult),
            r=[agb.spec(), ls1t.spec()], w=[agb.spec()], tag=f"ls_{s}"))
        stages.append(lambda: dve(lambda e, n=nb, ss=s, a=agb: e.tensor_tensor(
            ACCa(ss), a.ap, n.ap, op=ALU.add),
            r=[agb.spec(), nb.spec()], w=[ACCs(s)], tag=f"acc_{s}"))

        c = 8 * par
        s6 = 16 * par

        def sc(a, b2):
            return stt.ap[:, c + a:c + b2], stt.spec(c + a, c + b2)
        stages.append(lambda: dve(lambda e: e.bn_stats(st6.ap[:, s6:s6 + 6],
                                                       ACCa(s)[:, 0:512]),
                                  r=[ACCs(s)], w=[st6.spec(s6, s6 + 6)], tag=f"mi_{s}b0"))
        stages.append(lambda: dve(lambda e: e.bn_stats(st6.ap[:, s6 + 6:s6 + 12],
                                                       ACCa(s)[:, 512:1024]),
                                  r=[ACCs(s)], w=[st6.spec(s6 + 6, s6 + 12)], tag=f"mi_{s}b1"))
        stages.append(lambda: dve(lambda e: e.bn_aggr(sc(4, 6)[0], st6.ap[:, s6:s6 + 12]),
                                  r=[st6.spec(s6, s6 + 12)], w=[sc(4, 6)[1]], tag=f"mi_{s}ag"))
        stages.append(lambda: act(AF.Sqrt, sc(2, 3)[0], sc(5, 6)[0], r=[sc(5, 6)[1]],
                                  w=[sc(2, 3)[1]], bias=float(EPS), tag=f"mi_{s}sd"))
        stages.append(lambda: dve(lambda e: e.reciprocal(sc(1, 2)[0], sc(2, 3)[0]),
                                  r=[sc(2, 3)[1]], w=[sc(1, 2)[1]], tag=f"mi_{s}rs"))
        stages.append(lambda: dve(lambda e: e.tensor_scalar(
            agb.ap, ACCa(s), sc(4, 5)[0], sc(1, 2)[0],
            op0=ALU.subtract, op1=ALU.mult),
            r=[ACCs(s), sc(4, 5)[1], sc(1, 2)[1]], w=[agb.spec()], tag=f"mi_{s}ap"))
        stages.append(lambda: dve(lambda e: e.tensor_tensor(agb.ap, agb.ap, ln2g.ap,
                                                            op=ALU.mult),
                                  r=[agb.spec(), ln2g.spec()], w=[agb.spec()], tag=f"mi_{s}gm"))
        stages.append(lambda: dve(lambda e, m=mint: e.tensor_tensor(m.ap, agb.ap, ln2b.ap,
                                                                    op=ALU.add),
                                  r=[agb.spec(), ln2b.spec()], w=[mint.spec()], tag=f"mi_{s}ba"))

        def st_tr():
            for ic in range(IC):
                tb_ = 6 + ic % 2
                transp(psbb(tb_, 0, 128, 0, 128), mint.ap[:, ic * 128:(ic + 1) * 128],
                       r=[mint.spec(ic * 128, (ic + 1) * 128)], w=[ps_spec(tb_)],
                       tag=f"mt_{s}_{ic}")
                dve(lambda e, i=ic, t=tl, b=tb_: e.tensor_copy(
                    MIT.ap[:, i, t:t + 128], psbb(b, 0, 128, 0, 128)),
                    r=[ps_spec(tb_)], w=[MIT.spec3(ic, tl, tl + 128)], tag=f"me_{s}_{ic}")
        stages.append(st_tr)

        wgb = 4 + par

        def st_wg():
            for i in range(IC):
                mm(psf_bank(wgb, 0, 128, 0, NE), MIT.ap[:, i, tl:tl + 128],
                   wg_s.ap[:, i, :], i == 0, i == IC - 1,
                   r=[MIT.spec3(i, tl, tl + 128), wg_s.spec3(i, 0, NE)],
                   w=[ps_spec(wgb)], tag=f"wg_{s}")
        stages.append(st_wg)
        stages.append(lambda: dve(lambda e: e.reduce_max(
            gp(0, 1)[0], psf_bank(wgb, 0, 128, 0, NE), axis=AX.X, negate=True),
            r=[ps_spec(wgb)], w=[gp(0, 1)[1]], tag=f"gm_{s}"))
        stages.append(lambda: P.op("act", lambda e: e.activation(
            prb.ap, psf_bank(wgb, 0, 128, 0, NE), AF.Exp, bias=gp(0, 1)[0],
            accum_out=gp(1, 2)[0]),
            r=[ps_spec(wgb), gp(0, 1)[1]], w=[prb.spec(), gp(1, 2)[1]], tag=f"ge_{s}"))
        stages.append(lambda: dve(lambda e: e.reciprocal(gp(2, 3)[0], gp(1, 2)[0]),
                                  r=[gp(1, 2)[1]], w=[gp(2, 3)[1]], tag=f"gr_{s}"))
        stages.append(lambda: dve(lambda e: e.tensor_scalar(
            prb.ap, prb.ap, gp(2, 3)[0], None, op0=ALU.mult),
            r=[prb.spec(), gp(2, 3)[1]], w=[prb.spec()], tag=f"gn_{s}"))
        stages.append(lambda: dve(lambda e: e.max(t8b.ap, prb.ap),
                                  r=[prb.spec()], w=[t8b.spec()], tag=f"g8_{s}"))
        stages.append(lambda: dve(lambda e: e.tensor_tensor(
            gp(3, 4)[0], t8b.ap[:, 0:1], t8b.ap[:, 1:2], op=ALU.add),
            r=[t8b.spec()], w=[gp(3, 4)[1]], tag=f"gd_{s}"))
        stages.append(lambda: dve(lambda e: e.tensor_scalar(
            gp(3, 4)[0], gp(3, 4)[0], 1e-8, None, op0=ALU.add),
            r=[gp(3, 4)[1]], w=[gp(3, 4)[1]], tag=f"gd2_{s}"))
        stages.append(lambda: dve(lambda e: e.reciprocal(gp(4, 5)[0], gp(3, 4)[0]),
                                  r=[gp(3, 4)[1]], w=[gp(4, 5)[1]], tag=f"gd3_{s}"))
        stages.append(lambda: dve(lambda e: e.tensor_scalar(
            mskb.ap, prb.ap, t8b.ap[:, 1:2], None, op0=ALU.is_ge),
            r=[prb.spec(), t8b.spec()], w=[mskb.spec()], tag=f"gk_{s}"))
        stages.append(lambda: dve(lambda e, ss=s: e.scalar_tensor_tensor(
            cwb.ap[:, ss, :], prb.ap, gp(4, 5)[0], mskb.ap,
            op0=ALU.mult, op1=ALU.mult),
            r=[prb.spec(), gp(4, 5)[1], mskb.spec()],
            w=[cwb.spec3(s, 0, NE)], tag=f"gc_{s}"))
        return stages

    for t_ in range(8):
        pair = [p3_tile_stages(2 * t_), p3_tile_stages(2 * t_ + 1)]
        for k in range(len(pair[0])):
            pair[0][k]()
            pair[1][k]()

    ATT.free(); woutS.free()
    for bb in gr + sr + agbs + mints + prbs + t8bs + mskbs:
        bb.free()
    ln2g.free(); ln2b.free(); ls1t.free()
    for bb in nr:
        bb.free()

    HT = B("HT", [128, 16, 1024], BF16)
    w2q = [B(f"w2q{i}", [128, 4, D], BF16) for i in range(4)]
    w1r = [B(f"w1r{i}", [128, 8, 128], BF16) for i in range(4)]
    w3r = [B(f"w3r{i}", [128, 8, 128], BF16) for i in range(4)]
    s1bs = [B(f"s1b{i}", [128, 512], F32) for i in range(2)]
    mtmp = B("mtmp", [128, D], F32)

    def moe_fc(e_, tb):
        def w2_load(ee=e_):
            for qq in range(4):
                dma("qpool", w2q[qq].ap,
                    din["W2"][ee][qq * 512:(qq + 1) * 512]
                    .rearrange("(g p) n -> p g n", p=128),
                    w=[w2q[qq].spec()], tag=f"w2_{ee}_{qq}")
        if True:
            tq0 = tb * 1024
            for fc in range(16):
                f0 = fc * 128
                gidx = (tb * 16 + fc) % 4
                wb1, wb3 = w1r[gidx], w3r[gidx]
                dma("qpool", wb1.ap, din["W1"][e_][:, f0:f0 + 128]
                    .rearrange("(g p) n -> p g n", p=128), w=[wb1.spec()],
                    tag=f"w1_{e_}_{tb}_{fc}")
                dma("qpool", wb3.ap, din["W3"][e_][:, f0:f0 + 128]
                    .rearrange("(g p) n -> p g n", p=128), w=[wb3.spec()],
                    tag=f"w3_{e_}_{tb}_{fc}")
                if tb == 0 and fc == 2:
                    w2_load()
                for tq in range(2):
                    t0 = tq0 + tq * 512
                    ba = tq % 2
                    for i in range(IC):
                        mm(psf_bank(ba * 2), wb1.ap[:, i, :],
                           MIT.ap[:, i, t0:t0 + 512], i == 0, i == IC - 1,
                           r=[wb1.spec3(i, 0, 128),
                              MIT.spec3(i, t0, t0 + 512)],
                           w=[ps_spec(ba * 2)], tag=f"a1_{e_}_{tb}_{fc}_{tq}")
                    for i in range(IC):
                        mm(psf_bank(ba * 2 + 1), wb3.ap[:, i, :],
                           MIT.ap[:, i, t0:t0 + 512], i == 0, i == IC - 1,
                           r=[wb3.spec3(i, 0, 128),
                              MIT.spec3(i, t0, t0 + 512)],
                           w=[ps_spec(ba * 2 + 1)], tag=f"a3_{e_}_{tb}_{fc}_{tq}")
                    s1x = s1bs[tq]
                    act(AF.Silu, s1x.ap, psf_bank(ba * 2),
                        r=[ps_spec(ba * 2)], w=[s1x.spec()],
                        tag=f"hs_{e_}_{tb}_{fc}_{tq}")
                    dve(lambda e, f=fc, t=tq * 512, b=ba, sx=s1x: e.tensor_tensor(
                        HT.ap[:, f, t:t + 512], sx.ap, psf_bank(b * 2 + 1),
                        op=ALU.mult),
                        r=[s1x.spec(), ps_spec(ba * 2 + 1)],
                        w=[HT.spec3(fc, tq * 512, tq * 512 + 512)],
                        tag=f"hm_{e_}_{tb}_{fc}_{tq}")
    def moe_out(e_, tb, pulls=None):
        if True:
            for j8 in range(8):
                s = tb * 8 + j8
                tl = j8 * 128
                ob_ = 4 if (e_ == NE - 1 and tb == 1) else 4 + (s % 2) * 2
                for nh in range(2):
                    for fc in range(16):
                        w2x = w2q[fc // 4]
                        mm(psf_bank(ob_ + nh), HT.ap[:, fc, tl:tl + 128],
                           w2x.ap[:, fc % 4, nh * 512:(nh + 1) * 512], fc == 0, fc == 15,
                           r=[HT.spec3(fc, tl, tl + 128),
                              w2x.spec3(fc % 4, nh * 512, (nh + 1) * 512)],
                           w=[ps_spec(ob_ + nh)], tag=f"o2_{e_}_{s}_{nh}")
                dve(lambda e, ss=s, ee=e_, p=ob_: e.scalar_tensor_tensor(
                    mtmp.ap, PSF.ap()[:, p * 512:(p + 2) * 512], cwb.ap[:, ss, ee:ee + 1],
                    ls2t.ap, op0=ALU.mult, op1=ALU.mult),
                    r=[ps_spec(ob_, ob_ + 2), cwb.spec3(s, e_, e_ + 1), ls2t.spec()],
                    w=[mtmp.spec()], tag=f"oc_{e_}_{s}")
                dve(lambda e, ss=s: e.tensor_tensor(ACCa(ss), ACCa(ss),
                                                    mtmp.ap, op=ALU.add),
                    r=[mtmp.spec(), ACCs(s)], w=[ACCs(s)],
                    tag=f"oa_{e_}_{s}")
                if pulls:
                    pulls.pop(0)()

    for e_ in range(NE - 1):
        moe_fc(e_, 0)
        moe_out(e_, 0)
        moe_fc(e_, 1)
        moe_out(e_, 1)
    moe_fc(NE - 1, 0)
    moe_out(NE - 1, 0)
    moe_fc(NE - 1, 1)
    MIT.free()
    for bb in w1r + w3r:
        bb.free()

    N1T = B("N1T", [128, IC, TOK], BF16)
    n0r2 = [B(f"n0q{i}", [128, D], F32) for i in range(2)]
    x0r2 = [B(f"x0q{i}", [128, D], F32) for i in range(2)]
    etmps = [B(f"etmp_{i}", [128, D], F32) for i in range(2)]
    n1bs = [B(f"n1b_{i}", [128, D], BF16) for i in range(2)]

    def lns(x_ap, x_spec, out_ap, out_spec, slot, tag):
        c = 8 * slot
        s6 = 16 * slot

        def sc(a, b2):
            return stt.ap[:, c + a:c + b2], stt.spec(c + a, c + b2)
        return [
            lambda: dve(lambda e: e.bn_stats(st6.ap[:, s6:s6 + 6], x_ap[:, 0:512]),
                        r=[x_spec], w=[st6.spec(s6, s6 + 6)], tag=tag + "b0"),
            lambda: dve(lambda e: e.bn_stats(st6.ap[:, s6 + 6:s6 + 12], x_ap[:, 512:1024]),
                        r=[x_spec], w=[st6.spec(s6 + 6, s6 + 12)], tag=tag + "b1"),
            lambda: dve(lambda e: e.bn_aggr(sc(4, 6)[0], st6.ap[:, s6:s6 + 12]),
                        r=[st6.spec(s6, s6 + 12)], w=[sc(4, 6)[1]], tag=tag + "ag"),
            lambda: act(AF.Sqrt, sc(2, 3)[0], sc(5, 6)[0], r=[sc(5, 6)[1]],
                        w=[sc(2, 3)[1]], bias=float(EPS), tag=tag + "sd"),
            lambda: dve(lambda e: e.reciprocal(sc(1, 2)[0], sc(2, 3)[0]),
                        r=[sc(2, 3)[1]], w=[sc(1, 2)[1]], tag=tag + "rs"),
            lambda: dve(lambda e: e.tensor_scalar(out_ap, x_ap, sc(4, 5)[0], sc(1, 2)[0],
                                                  op0=ALU.subtract, op1=ALU.mult),
                        r=[x_spec, sc(4, 5)[1], sc(1, 2)[1]], w=[out_spec], tag=tag + "ap"),
        ]

    def p6_combine_stages(s):
        tl = s * 128
        par = (s + s // 8) % 2
        nb, xb = n0r2[par], x0r2[par]
        etmp, n1b = etmps[par], n1bs[par]

        def st_loads():
            dma("qpool", nb.ap, sp_n0[tl:tl + 128, :], r=[("hb_n0", tl, tl + 128)],
                w=[nb.spec()], tag=f"n0q_{s}")
            dma("qsp", xb.ap, din["x0"][tl:tl + 128, :], w=[xb.spec()], tag=f"x0q_{s}")
        stages = [st_loads]
        stages.append(lambda: dve(lambda e, ss=s, n=nb, t_=etmp: e.scalar_tensor_tensor(
            t_.ap, ACCa(ss), wsum.ap[:, ss:ss + 1], n.ap,
            op0=ALU.mult, op1=ALU.add),
            r=[ACCs(s), wsum.spec(s, s + 1), nb.spec()],
            w=[etmp.spec()], tag=f"bo_{s}"))
        stages += lns(etmp.ap, etmp.spec(), nb.ap, nb.spec(), par, f"l1_{s}")
        stages.append(lambda: dve(lambda e, ss=s, x=xb, t_=nb: e.tensor_tensor(
            ACCa(ss), t_.ap, x.ap, op=ALU.add),
            r=[nb.spec(), xb.spec()], w=[ACCs(s)], tag=f"x1_{s}"))
        stages += lns(ACCa(s), ACCs(s), n1b.ap, n1b.spec(), 2 + par, f"l2_{s}")

        def st_tr():
            for ic in range(IC):
                tb_ = 6 + ic % 2
                transp(psbb(tb_, 0, 128, 0, 128), n1b.ap[:, ic * 128:(ic + 1) * 128],
                       r=[n1b.spec(ic * 128, (ic + 1) * 128)], w=[ps_spec(tb_)],
                       tag=f"n1t_{s}_{ic}")
                dve(lambda e, i=ic, t=tl, b=tb_: e.tensor_copy(
                    N1T.ap[:, i, t:t + 128], psbb(b, 0, 128, 0, 128)),
                    r=[ps_spec(tb_)], w=[N1T.spec3(ic, tl, tl + 128)], tag=f"n1e_{s}_{ic}")
        stages.append(st_tr)
        return stages

    def p6_combine_pair(s0, s1):
        ca_, cb2 = p6_combine_stages(s0), p6_combine_stages(s1)
        for k in range(len(ca_)):
            ca_[k]()
            cb2[k]()

    moe_out(NE - 1, 1, pulls=[(lambda j8=j8: p6_combine_pair(j8, 8 + j8))
                              for j8 in range(8)])

    HT.free(); mtmp.free()
    for bb in w2q:
        bb.free()
    for bb in s1bs:
        bb.free()

    wtRb = B("wtRb", [128, 4, 512], BF16)
    wtIb = B("wtIb", [128, 4, 512], BF16)
    wtInb = B("wtInb", [128, 4, 512], BF16)
    biasRt = B("biasRt", [128, 512], F32)
    biasIt = B("biasIt", [128, 512], F32)
    bcast_load("qsp", biasRt, din["bias_r"])
    bcast_load("qsp", biasIt, din["bias_i"])
    dma("qsp", wtRb.ap, wt_scr[0], r=[("hb_wt", 0, 1)], w=[wtRb.spec()], tag="wtRl")
    dma("qsp", wtIb.ap, wt_scr[1], r=[("hb_wt", 1, 2)], w=[wtIb.spec()], tag="wtIl")
    dve(lambda e: e.tensor_scalar(wtInb.ap, wtIb.ap, -1.0, None, op0=ALU.mult),
        r=[wtIb.spec()], w=[wtInb.spec()], tag="wtInb")

    ffss = [B(f"ffs_{i}", [128, D], F32) for i in range(3)]
    fts = [B(f"fts_{i}", [128, D], F32) for i in range(3)]
    outs = [B(f"outs{i}", [128, D], F32) for i in range(3)]

    def p6_ffn_stages(s, wide=2):
        tl = s * 128
        etmp, ffs = fts[s % wide], ffss[s % wide]
        fb0 = (s % wide) * 2

        def st_fr():
            for k, wt in ((0, wtRb), (4, wtInb)):
                for i4 in range(4):
                    mm(psf_bank(fb0), N1T.ap[:, k // 4 * 4 + i4, tl:tl + 128],
                       wt.ap[:, i4, :], (k + i4) == 0, (k + i4) == 7,
                       r=[N1T.spec3(k // 4 * 4 + i4, tl, tl + 128), wt.spec3(i4, 0, 512)],
                       w=[ps_spec(fb0)], tag=f"fr_{s}_{k}_{i4}")

        def st_fi():
            for k, wt in ((0, wtIb), (4, wtRb)):
                for i4 in range(4):
                    mm(psf_bank(fb0 + 1), N1T.ap[:, k // 4 * 4 + i4, tl:tl + 128],
                       wt.ap[:, i4, :], (k + i4) == 0, (k + i4) == 7,
                       r=[N1T.spec3(k // 4 * 4 + i4, tl, tl + 128), wt.spec3(i4, 0, 512)],
                       w=[ps_spec(fb0 + 1)], tag=f"fi_{s}_{k}_{i4}")
        stages = [st_fr, st_fi]
        stages.append(lambda: dve(lambda e, p=fb0, f_=ffs: e.tensor_tensor(
            f_.ap[:, 0:512], psf_bank(p), biasRt.ap, op=ALU.add),
            r=[ps_spec(fb0), biasRt.spec()], w=[ffs.spec(0, 512)], tag=f"fb_{s}"))
        stages.append(lambda: dve(lambda e, p=fb0, f_=ffs: e.tensor_tensor(
            f_.ap[:, 512:1024], psf_bank(p + 1), biasIt.ap, op=ALU.add),
            r=[ps_spec(fb0 + 1), biasIt.spec()], w=[ffs.spec(512, 1024)], tag=f"fb2_{s}"))
        stages += lns(ffs.ap, ffs.spec(), etmp.ap, etmp.spec(),
                      (4 + s % 2) if wide == 2 else (3 + s % 3), f"l3_{s}")

        def st_out():
            ob = outs[s % 3]
            dve(lambda e, ss=s, o=ob, t_=etmp: e.tensor_tensor(
                o.ap, t_.ap, ACCa(ss), op=ALU.add),
                r=[etmp.spec(), ACCs(s)], w=[ob.spec()], tag=f"o_{s}")
            dma("qact", out_d[tl:tl + 128, :], ob.ap, r=[ob.spec()],
                w=[("hb_out", tl, tl + 128)], tag=f"od_{s}")
        stages.append(st_out)
        return stages

    def p6_ffn_pair(s0, s1):
        fa, fb_ = p6_ffn_stages(s0), p6_ffn_stages(s1)
        for k in range(len(fa)):
            fa[k]()
            fb_[k]()

    def p6_ffn_multi(ss_list):
        chains = [p6_ffn_stages(s_, 3) for s_ in ss_list]
        for k in range(len(chains[0])):
            for ch in chains:
                ch[k]()

    for g0 in range(0, 16, 3):
        p6_ffn_multi(list(range(g0, min(g0 + 3, 16))))

    P.finalize()
    with nc.Block("init") as blk:
        @blk.gpsimd
        def _(e):
            for s in sems.values():
                e.sem_clear(s)
    with nc.Block("main") as blk:
        @blk.tensor
        def _(e):
            P.emit_engine("pe", e, sems)

        @blk.scalar
        def _(e):
            P.emit_engine("act", e, sems)
            for s, v in P.slot_total.items():
                if s.startswith("qact"):
                    e.wait_ge(sems[s], v)

        @blk.vector
        def _(e):
            P.emit_engine("dve", e, sems)

        @blk.gpsimd
        def _(e):
            P.emit_engine("pool", e, sems)
            for s, v in P.slot_total.items():
                if s.startswith("qpool"):
                    e.wait_ge(sems[s], v)

        @blk.sync
        def _(e):
            P.emit_engine("sp", e, sems)
            for s, v in P.slot_total.items():
                e.wait_ge(sems[s], v)
    st.close()
    return nc


def host_constants():
    ident = np.eye(128, dtype=BFDT)
    qsel = np.kron(np.eye(32, dtype=np.float32), np.ones((1, 16), np.float32))
    pselt = np.zeros((32, 512), np.float32)
    for t in range(4):
        blk = np.kron(np.eye(32, dtype=np.float32)[:, 8 * t:8 * t + 8],
                      np.ones((1, 16), np.float32))
        pselt[:, t * 128:(t + 1) * 128] = blk
    rselt = np.concatenate([np.eye(16, dtype=np.float32)] * 8, axis=1)
    ssel = np.tile(np.eye(16, dtype=np.float32), (1, 32)).astype(np.float32)
    return ident, qsel, pselt, rselt, ssel


_NC_CACHE = {}


def kernel(**inputs):
    x = np.ascontiguousarray(np.asarray(inputs["L0_attn_x"], dtype=np.float32))
    ident, qsel, pselt, rselt, ssel = host_constants()
    wmap = {
        "Wr": "L0_attn_Wr", "br": "L0_attn_br", "step_bias": "L0_attn_step_bias",
        "ln1_g": "L0_attn_ln1_g", "ln1_b": "L0_attn_ln1_b",
        "ln2_g": "L0_attn_ln2_g", "ln2_b": "L0_attn_ln2_b",
        "ls1": "L0_attn_ls1", "ls2": "L0_attn_ls2",
        "Wqkv": "L0_attn_Wqkv", "Wout": "L0_attn_Wout", "Wgate": "L0_attn_Wgate",
        "Win": "L0_attn_Win_ssm", "conv_w": "L0_attn_conv_w", "conv_b": "L0_attn_conv_b",
        "Wout_ssm": "L0_attn_Wout_ssm", "Wg": "L0_attn_Wg",
        "W1": "L0_attn_W1", "W2": "L0_attn_W2", "W3": "L0_attn_W3",
        "G1R": "L0_ffn_G1_R", "G2R": "L0_ffn_G2_R",
        "G1I": "L0_ffn_G1_I", "G2I": "L0_ffn_G2_I",
        "bias_r": "L0_ffn_bias_real", "bias_i": "L0_ffn_bias_imag",
    }
    weights = {k: np.ascontiguousarray(np.asarray(inputs[v], dtype=np.float32))
               for k, v in wmap.items()}
    in_maps = []
    for c in range(8):
        b, h = c // 2, c % 2
        x0 = np.ascontiguousarray(x[b, h * TOK:(h + 1) * TOK])
        xsib = np.ascontiguousarray(x[b, (1 - h) * TOK:(2 - h) * TOK])
        if h == 1:
            xp3 = np.ascontiguousarray(x[b, TOK - 3:TOK])
            cf = np.ones((1, 1), np.float32)
        else:
            xp3 = np.zeros((3, D), np.float32)
            cf = np.zeros((1, 1), np.float32)
        m = {"x0": x0, "xsib": xsib, "xp3": xp3, "cflag": cf,
             "ident": ident, "qsel": qsel, "pselt": pselt, "rselt": rselt,
             "ssel": ssel}
        m.update(weights)
        in_maps.append(m)
    if "nc" not in _NC_CACHE:
        _NC_CACHE["nc"] = build_nc()
    nc = _NC_CACHE["nc"]
    res = run_bass_kernel_spmd(nc, in_maps, core_ids=list(range(8)))
    out = np.empty((4, KV, D), np.float32)
    for c in range(8):
        b, h = c // 2, c % 2
        out[b, h * TOK:(h + 1) * TOK] = np.asarray(res.results[c]["out"],
                                                   dtype=np.float32)
    return out


# version 41 (final)
# speedup vs baseline: 1.0536x; 1.0536x over previous
import contextlib
import numpy as np
import ml_dtypes
import concourse.bass as bass
import concourse.mybir as mybir
from concourse.bass_utils import run_bass_kernel_spmd

F32 = mybir.dt.float32
BF16 = mybir.dt.bfloat16
U8 = mybir.dt.uint8
AF = mybir.ActivationFunctionType
ALU = mybir.AluOpType
AX = mybir.AxisListType
BFDT = ml_dtypes.bfloat16

D = 1024
TOK = 2048
KV = 4096
NS = 16
IC = 8
H = 16
NE = 8
FF = 2048
EPS = 1e-5
SB_LIMIT = 207 * 1024

ENGINES = ("pe", "act", "dve", "pool", "sp")
QUEUES = {"qsp": "sp", "qact": "act", "qpool": "pool"}
NQSLOT = 8
QSLOT_NAMES = [f"{q}{i}" for q in QUEUES for i in range(NQSLOT)]
ISZ = {F32: 4, BF16: 2}


class Prog:

    def __init__(self):
        self.ops = []
        self.segs = {}

    def _access(self, space, lo, hi, opid, is_write):
        segs = self.segs.setdefault(space, [])
        deps = set()
        rest = []
        for seg in segs:
            slo, shi, w, rd = seg
            if shi <= lo or slo >= hi:
                rest.append(seg)
                continue
            if w >= 0:
                deps.add(w)
            if is_write:
                deps.update(rd)
            if slo < lo:
                rest.append([slo, lo, w, list(rd)])
            if shi > hi:
                rest.append([hi, shi, w, list(rd)])
            if not is_write:
                plo, phi = max(slo, lo), min(shi, hi)
                rest.append([plo, phi, w, rd + [opid]])
        if is_write:
            rest.append([lo, hi, opid, []])
        self.segs[space] = rest
        return deps

    def op(self, engine, emit, r=(), w=(), q=None, tag=""):
        i = len(self.ops)
        deps = set()
        for (sp, lo, hi) in r:
            deps |= self._access(sp, lo, hi, i, False)
        for (sp, lo, hi) in w:
            deps |= self._access(sp, lo, hi, i, True)
        deps.discard(i)
        self.ops.append({"eng": engine, "emit": emit, "deps": sorted(deps),
                         "q": q, "tag": tag, "event": None, "waits": None,
                         "prewait": None})
        return i

    def finalize(self):
        n = len(self.ops)
        needed = [False] * n
        for o in self.ops:
            for d in o["deps"]:
                if o["eng"] == "pe" and self.ops[d]["eng"] == "pe" \
                        and self.ops[d]["q"] is None and o["q"] is None:
                    continue
                needed[d] = True
        inc_count = {e: 0 for e in ENGINES}
        q_idx = {q: 0 for q in QUEUES}
        slot_last = {}
        for i, o in enumerate(self.ops):
            if o["q"] is not None:
                k = q_idx[o["q"]]
                q_idx[o["q"]] += 1
                slot, cyc = k % NQSLOT, k // NQSLOT
                sname = f"{o['q']}{slot}"
                o["event"] = (sname, 16 * (cyc + 1))
                o["prewait"] = (sname, 16 * cyc, slot_last.get(sname)) if cyc else None
                slot_last[sname] = i
            elif needed[i]:
                e = o["eng"]
                inc_count[e] += 1
                o["event"] = (e, inc_count[e])
        self.slot_total = {s: (self.ops[i]["event"][1]) for s, i in slot_last.items()}
        clock = {e: {} for e in ENGINES}
        opclock = [None] * n
        for i, o in enumerate(self.ops):
            E = o["eng"]
            ck = clock[E]
            waits = {}

            def need(s, v, src_op):
                if ck.get(s, 0) < v:
                    waits[s] = max(waits.get(s, 0), v)
                    ck[s] = v
                    dck = opclock[src_op] if src_op is not None else None
                    if dck:
                        for s2, v2 in dck.items():
                            if ck.get(s2, 0) < v2:
                                ck[s2] = v2

            pw = o.get("prewait")
            if pw:
                need(pw[0], pw[1], pw[2])
            for d in o["deps"]:
                dop = self.ops[d]
                if E == "pe" and dop["eng"] == "pe" and dop["q"] is None \
                        and o["q"] is None:
                    dck = opclock[d]
                    if dck:
                        for s2, v2 in dck.items():
                            if ck.get(s2, 0) < v2:
                                ck[s2] = v2
                    continue
                s, v = dop["event"]
                need(s, v, d)
            o["waits"] = sorted(waits.items())
            myck = dict(ck)
            if o["event"] is not None:
                s, v = o["event"]
                myck[s] = max(myck.get(s, 0), v)
            opclock[i] = myck

    def emit_engine(self, eng_name, e, sems):
        for o in self.ops:
            if o["eng"] != eng_name:
                continue
            for (s, v) in o["waits"]:
                e.wait_ge(sems[s], v)
            inst = o["emit"](e)
            if o["event"] is not None:
                s, _ = o["event"]
                inst.then_inc(sems[s], 16 if o["q"] is not None else 1)


class Alloc:
    def __init__(self, limit):
        self.free_segs = [(0, limit)]
        self.peak = 0

    def alloc(self, nbytes, align=64):
        nbytes = (nbytes + align - 1) // align * align
        for idx, (lo, hi) in enumerate(self.free_segs):
            lo_a = (lo + align - 1) // align * align
            if hi - lo_a >= nbytes:
                new = []
                if lo < lo_a:
                    new.append((lo, lo_a))
                if lo_a + nbytes < hi:
                    new.append((lo_a + nbytes, hi))
                self.free_segs[idx:idx + 1] = new
                self.peak = max(self.peak, lo_a + nbytes)
                return lo_a
        raise MemoryError(f"sbuf arena exhausted (want {nbytes})")

    def dealloc(self, off, nbytes, align=64):
        nbytes = (nbytes + align - 1) // align * align
        self.free_segs.append((off, off + nbytes))
        self.free_segs.sort()
        merged = []
        for seg in self.free_segs:
            if merged and merged[-1][1] >= seg[0]:
                merged[-1] = (merged[-1][0], max(merged[-1][1], seg[1]))
            else:
                merged.append(list(seg))
        self.free_segs = [tuple(s) for s in merged]


class Buf:
    def __init__(self, al, af32, abf, name, shape, dtype):
        self.name, self.shape, self.dtype = name, list(shape), dtype
        isz = ISZ[dtype]
        self.isz = isz
        per_part = int(np.prod(shape[1:])) * isz
        self.nbytes = per_part
        self.off = al.alloc(per_part)
        self.al = al
        base = af32 if dtype == F32 else abf
        flat = base.ap()[0:shape[0], self.off // isz: self.off // isz + int(np.prod(shape[1:]))]
        if len(shape) == 2:
            self.ap = flat
        elif len(shape) == 3:
            self.ap = flat.rearrange("p (a b) -> p a b", a=shape[1], b=shape[2])
        else:
            raise ValueError(shape)

    def spec(self, lo_elem=0, hi_elem=None):
        hi_elem = int(np.prod(self.shape[1:])) if hi_elem is None else hi_elem
        return ("sb", self.off + lo_elem * self.isz, self.off + hi_elem * self.isz)

    def spec3(self, a, e0, e1):
        inner = self.shape[2]
        return self.spec(a * inner + e0, a * inner + e1)

    def free(self):
        self.al.dealloc(self.off, self.nbytes)


def ps_spec(b0, b1=None):
    return ("ps", b0, (b0 + 1) if b1 is None else b1)


def build_nc():
    nc = bass.Bass("TRN2", target_bir_lowering=False)

    din = {}

    def dram_in(name, shape, dt=F32):
        din[name] = nc.dram_tensor(name, list(shape), dt, kind="ExternalInput").ap()

    dram_in("x0", [TOK, D]); dram_in("xsib", [TOK, D]); dram_in("xp3", [3, D])
    dram_in("cflag", [1, 1])
    dram_in("Wr", [D, 2]); dram_in("br", [2]); dram_in("step_bias", [3])
    dram_in("ln1_g", [D]); dram_in("ln1_b", [D]); dram_in("ln2_g", [D]); dram_in("ln2_b", [D])
    dram_in("ls1", [D]); dram_in("ls2", [D])
    dram_in("Wqkv", [D, 3 * D]); dram_in("Wout", [D, D]); dram_in("Wgate", [D, D])
    dram_in("Win", [D, 2 * D]); dram_in("conv_w", [D, 1, 4]); dram_in("conv_b", [D])
    dram_in("Wout_ssm", [D, D]); dram_in("Wg", [D, NE])
    dram_in("W1", [NE, D, FF]); dram_in("W2", [NE, FF, D]); dram_in("W3", [NE, D, FF])
    dram_in("G1R", [1, 32, 4, 32]); dram_in("G2R", [4, 16, 1, 16])
    dram_in("G1I", [1, 32, 4, 32]); dram_in("G2I", [4, 16, 1, 16])
    dram_in("bias_r", [512]); dram_in("bias_i", [512])
    dram_in("ident", [128, 128], BF16)
    dram_in("qsel", [32, 512]); dram_in("pselt", [32, 512])
    dram_in("rselt", [16, 128]); dram_in("ssel", [16, 512])

    out_d = nc.dram_tensor("out", [TOK, D], F32, kind="ExternalOutput").ap()
    sp_n0 = nc.dram_tensor("sp_n0", [TOK, D], F32, kind="Internal").ap()
    sp_gate = nc.dram_tensor("sp_gate", [TOK, D], F32, kind="Internal").ap()
    sp_ssm = nc.dram_tensor("sp_ssm", [TOK, D], F32, kind="Internal").ap()
    nm_scr = nc.dram_tensor("nm_scr", [H, 16, 128], BF16, kind="Internal").ap()
    wt_scr = nc.dram_tensor("wt_scr", [2, 128, 4, 512], BF16, kind="Internal").ap()
    sgz_scr = nc.dram_tensor("sgz_scr", [8, 128, TOK], BF16, kind="Internal").ap()

    st = contextlib.ExitStack()
    sems = {}
    for s in list(ENGINES) + QSLOT_NAMES:
        sems[s] = st.enter_context(nc.semaphore("sem_" + s))
    ARENA = st.enter_context(nc.sbuf_tensor("arena", [128, SB_LIMIT], U8))
    PSF = st.enter_context(nc.psum_tensor("psf", [128, 4096], F32))
    PSBV = PSF.bitcast(BF16)
    AF32 = ARENA.bitcast(F32)
    ABF = ARENA.bitcast(BF16)
    al = Alloc(SB_LIMIT)
    P = Prog()

    def B(name, shape, dtype):
        return Buf(al, AF32, ABF, name, shape, dtype)

    def psf_bank(b, p0=0, p1=128, c0=0, c1=512):
        return PSF.ap()[p0:p1, b * 512 + c0: b * 512 + c1]

    def psbb(b, p0=0, p1=128, c0=0, c1=1024):
        return PSBV.ap()[p0:p1, b * 1024 + c0: b * 1024 + c1]

    def dma(q, dst, src, r=(), w=(), tag="", slow=False):
        eng = QUEUES[q]
        if slow:
            def em(e, d=dst, s=src):
                return e.dma_start(d, s, allow_slow_non_contiguous=True)
        else:
            def em(e, d=dst, s=src):
                return e.dma_start(d, s)
        return P.op(eng, em, r=r, w=w, q=q, tag=tag)

    def bcast_load(q, dstbuf, src1d, tag="bc"):
        src = src1d.rearrange("(o n) -> o n", o=1).partition_broadcast(128)
        dma(q, dstbuf.ap, src, w=[dstbuf.spec()], tag=tag)

    def mm(out_ap, lhsT, rhs, start, stop, r, w, tag=""):
        return P.op("pe", lambda e, o=out_ap, l=lhsT, x=rhs, s=start, t=stop:
                    e.matmul(o, l, x, start=s, stop=t), r=r, w=w, tag=tag)

    const_pool = {}

    def get_const(v):
        if v not in const_pool:
            cb = Buf(al, AF32, ABF, f"const_{len(const_pool)}", [128, 1], F32)
            P.op("dve", lambda e, c=cb, vv=v: e.memset(c.ap, vv), r=[], w=[cb.spec()],
                 tag=f"cst_{v}")
            nc.const_aps.register(F32, float(v), cb.ap)
            const_pool[v] = cb
        return const_pool[v]

    def act(fn, out_ap, in_ap, r, w, bias=0.0, scale=1.0, accum=None, tag=""):
        rr = list(r)
        if isinstance(bias, float) and fn != AF.Copy:
            rr.append(get_const(bias).spec())
        ww = list(w) + ([] if accum is None else [])
        return P.op("act", lambda e, o=out_ap, i=in_ap, f=fn, b=bias, s=scale, a=accum:
                    e.activation(o, i, f, bias=b, scale=s, accum_out=a), r=rr, w=ww, tag=tag)

    def dve(fn, r, w, tag=""):
        return P.op("dve", fn, r=r, w=w, tag=tag)

    identb = B("identb", [128, 128], BF16)
    ones_f = B("ones_f", [128, 64], F32)
    convw = B("convw", [128, 8, 4], F32)
    convb = B("convb", [128, 8], F32)
    wr_s = B("wr_s", [128, 8, 2], BF16)
    wg_s = B("wg_s", [128, 8, 8], BF16)
    brt = B("brt", [128, 2], F32)
    sbt = B("sbt", [128, 3], F32)
    b1st = B("b1st", [128, 3], F32)
    cft = B("cft", [128, 1], F32)
    alpha = B("alpha", [128, 16], F32)
    wsum = B("wsum", [128, 16], F32)
    cwb = B("cwb", [128, 16, 8], F32)
    stt = B("stt", [128, 64], F32)
    st6 = B("st6", [128, 96], F32)
    hlt = B("hlt", [128, 3], F32)
    np3 = B("np3", [128, 8, 3], BF16)
    onesb = B("onesb", [128, 1], BF16)
    P.op("dve", lambda e: e.memset(onesb.ap, 1.0), r=[], w=[onesb.spec()], tag="onesb")
    for v_ in (0.0, float(EPS), float(EPS * EPS)):
        get_const(v_)

    def transp(out_ap, in_ap, r, w, tag="tr"):
        kdim = in_ap.shape[0]
        return P.op("pe", lambda e, o=out_ap, i=in_ap, d=identb.ap[0:kdim, 0:kdim]:
                    e.transpose(o, i, d), r=list(r) + [identb.spec()], w=w, tag=tag)

    dma("qsp", identb.ap, din["ident"], w=[identb.spec()], tag="ident")
    P.op("dve", lambda e: e.memset(ones_f.ap, 1.0), w=[ones_f.spec()], tag="ones")
    dma("qsp", convw.ap, din["conv_w"].rearrange("(g p) o k -> p g (o k)", p=128),
        w=[convw.spec()], tag="convw")
    dma("qsp", convb.ap, din["conv_b"].rearrange("(g p) -> p g", p=128),
        w=[convb.spec()], tag="convb", slow=True)
    dma("qpool", wr_s.ap, din["Wr"].rearrange("(g p) n -> p g n", p=128),
        w=[wr_s.spec()], tag="wr")
    dma("qpool", wg_s.ap, din["Wg"].rearrange("(g p) n -> p g n", p=128),
        w=[wg_s.spec()], tag="wgs")
    bcast_load("qsp", brt, din["br"])
    bcast_load("qsp", sbt, din["step_bias"])
    dma("qsp", cft.ap, din["cflag"].partition_broadcast(128), w=[cft.spec()], tag="cflag")
    dve(lambda e: e.tensor_scalar(b1st.ap, sbt.ap, brt.ap[:, 1:2], None, op0=ALU.add),
        r=[sbt.spec(), brt.spec()], w=[b1st.spec()], tag="b1st")

    def layer_norm(x_ap, x_spec, out_ap, out_spec, g=None, pdim=128, tag="ln", slot=0, aff_eng="dve"):
        sl = slice(0, pdim)
        c = 8 * slot
        s6 = 16 * slot

        def sc(a, b2):
            return stt.ap[sl, c + a:c + b2], stt.spec(c + a, c + b2)

        dve(lambda e: e.bn_stats(st6.ap[sl, s6:s6 + 6], x_ap[:, 0:512]),
            r=[x_spec], w=[st6.spec(s6, s6 + 6)], tag=tag + "b0")
        dve(lambda e: e.bn_stats(st6.ap[sl, s6 + 6:s6 + 12], x_ap[:, 512:1024]),
            r=[x_spec], w=[st6.spec(s6 + 6, s6 + 12)], tag=tag + "b1")
        dve(lambda e: e.bn_aggr(sc(4, 6)[0], st6.ap[sl, s6:s6 + 12]),
            r=[st6.spec(s6, s6 + 12)], w=[sc(4, 6)[1]], tag=tag + "ag")
        act(AF.Sqrt, sc(2, 3)[0], sc(5, 6)[0], r=[sc(5, 6)[1]],
            w=[sc(2, 3)[1]], bias=float(EPS), tag=tag + "sd")
        dve(lambda e: e.reciprocal(sc(1, 2)[0], sc(2, 3)[0]),
            r=[sc(2, 3)[1]], w=[sc(1, 2)[1]], tag=tag + "rs")
        if g is None:
            dve(lambda e: e.tensor_scalar(out_ap, x_ap, sc(4, 5)[0], sc(1, 2)[0],
                                          op0=ALU.subtract, op1=ALU.mult),
                r=[x_spec, sc(4, 5)[1], sc(1, 2)[1]], w=[out_spec], tag=tag + "ap")
        else:
            g_ap, g_spec, b_ap, b_spec, tmp_ap, tmp_spec = g
            dve(lambda e: e.tensor_scalar(tmp_ap, x_ap, sc(4, 5)[0], sc(1, 2)[0],
                                          op0=ALU.subtract, op1=ALU.mult),
                r=[x_spec, sc(4, 5)[1], sc(1, 2)[1]], w=[tmp_spec], tag=tag + "ap")
            P.op(aff_eng, lambda e: e.tensor_tensor(tmp_ap, tmp_ap, g_ap, op=ALU.mult),
                 r=[tmp_spec, g_spec], w=[tmp_spec], tag=tag + "gm")
            P.op(aff_eng, lambda e: e.tensor_tensor(out_ap, tmp_ap, b_ap, op=ALU.add),
                 r=[tmp_spec, b_spec], w=[out_spec], tag=tag + "ba")

    def layer_norm2_stages(x_ap, x_spec, n0_ap, n0_spec, nrm_ap, nrm_spec, g,
                           tag="ln2", slot=0):
        sl = slice(0, 128)
        c = 8 * slot
        s6 = 16 * slot

        def sc(a, b2):
            return stt.ap[sl, c + a:c + b2], stt.spec(c + a, c + b2)
        g_ap, g_spec, b_ap, b_spec, tmp_ap, tmp_spec = g
        return [
            lambda: dve(lambda e: e.bn_stats(st6.ap[sl, s6:s6 + 6], x_ap[:, 0:512]),
                        r=[x_spec], w=[st6.spec(s6, s6 + 6)], tag=tag + "b0"),
            lambda: dve(lambda e: e.bn_stats(st6.ap[sl, s6 + 6:s6 + 12], x_ap[:, 512:1024]),
                        r=[x_spec], w=[st6.spec(s6 + 6, s6 + 12)], tag=tag + "b1"),
            lambda: dve(lambda e: e.bn_aggr(sc(4, 6)[0], st6.ap[sl, s6:s6 + 12]),
                        r=[st6.spec(s6, s6 + 12)], w=[sc(4, 6)[1]], tag=tag + "ag"),
            lambda: act(AF.Sqrt, sc(2, 3)[0], sc(5, 6)[0], r=[sc(5, 6)[1]],
                        w=[sc(2, 3)[1]], bias=float(EPS), tag=tag + "sd"),
            lambda: act(AF.Sqrt, sc(3, 4)[0], sc(5, 6)[0], r=[sc(5, 6)[1]],
                        w=[sc(3, 4)[1]], bias=float(EPS * EPS), scale=float(1.0 + EPS),
                        tag=tag + "s2"),
            lambda: dve(lambda e: e.reciprocal(sc(1, 2)[0], sc(2, 3)[0]),
                        r=[sc(2, 3)[1]], w=[sc(1, 2)[1]], tag=tag + "rs"),
            lambda: dve(lambda e: e.reciprocal(sc(6, 7)[0], sc(3, 4)[0]),
                        r=[sc(3, 4)[1]], w=[sc(6, 7)[1]], tag=tag + "r2"),
            lambda: dve(lambda e: e.tensor_scalar(n0_ap, x_ap, sc(4, 5)[0], sc(1, 2)[0],
                                                  op0=ALU.subtract, op1=ALU.mult),
                        r=[x_spec, sc(4, 5)[1], sc(1, 2)[1]], w=[n0_spec], tag=tag + "ap"),
            lambda: dve(lambda e: e.tensor_scalar(tmp_ap, x_ap, sc(4, 5)[0], sc(6, 7)[0],
                                                  op0=ALU.subtract, op1=ALU.mult),
                        r=[x_spec, sc(4, 5)[1], sc(6, 7)[1]], w=[tmp_spec], tag=tag + "a2"),
            lambda: dve(lambda e: e.tensor_tensor(tmp_ap, tmp_ap, g_ap, op=ALU.mult),
                        r=[tmp_spec, g_spec], w=[tmp_spec], tag=tag + "gm"),
            lambda: dve(lambda e: e.tensor_tensor(nrm_ap, tmp_ap, b_ap, op=ALU.add),
                        r=[tmp_spec, b_spec], w=[nrm_spec], tag=tag + "ba"),
        ]

    def layer_norm2(x_ap, x_spec, n0_ap, n0_spec, nrm_ap, nrm_spec, g, pdim=128,
                    tag="ln2", slot=0, aff_eng="pool"):
        sl = slice(0, pdim)
        c = 8 * slot
        s6 = 16 * slot

        def sc(a, b2):
            return stt.ap[sl, c + a:c + b2], stt.spec(c + a, c + b2)

        dve(lambda e: e.bn_stats(st6.ap[sl, s6:s6 + 6], x_ap[:, 0:512]),
            r=[x_spec], w=[st6.spec(s6, s6 + 6)], tag=tag + "b0")
        dve(lambda e: e.bn_stats(st6.ap[sl, s6 + 6:s6 + 12], x_ap[:, 512:1024]),
            r=[x_spec], w=[st6.spec(s6 + 6, s6 + 12)], tag=tag + "b1")
        dve(lambda e: e.bn_aggr(sc(4, 6)[0], st6.ap[sl, s6:s6 + 12]),
            r=[st6.spec(s6, s6 + 12)], w=[sc(4, 6)[1]], tag=tag + "ag")
        act(AF.Sqrt, sc(2, 3)[0], sc(5, 6)[0], r=[sc(5, 6)[1]],
            w=[sc(2, 3)[1]], bias=float(EPS), tag=tag + "sd")
        act(AF.Sqrt, sc(3, 4)[0], sc(5, 6)[0], r=[sc(5, 6)[1]],
            w=[sc(3, 4)[1]], bias=float(EPS * EPS), scale=float(1.0 + EPS),
            tag=tag + "s2")
        dve(lambda e: e.reciprocal(sc(1, 2)[0], sc(2, 3)[0]),
            r=[sc(2, 3)[1]], w=[sc(1, 2)[1]], tag=tag + "rs")
        dve(lambda e: e.reciprocal(sc(6, 7)[0], sc(3, 4)[0]),
            r=[sc(3, 4)[1]], w=[sc(6, 7)[1]], tag=tag + "r2")
        dve(lambda e: e.tensor_scalar(n0_ap, x_ap, sc(4, 5)[0], sc(1, 2)[0],
                                      op0=ALU.subtract, op1=ALU.mult),
            r=[x_spec, sc(4, 5)[1], sc(1, 2)[1]], w=[n0_spec], tag=tag + "ap")
        g_ap, g_spec, b_ap, b_spec, tmp_ap, tmp_spec = g
        dve(lambda e: e.tensor_scalar(tmp_ap, x_ap, sc(4, 5)[0], sc(6, 7)[0],
                                      op0=ALU.subtract, op1=ALU.mult),
            r=[x_spec, sc(4, 5)[1], sc(6, 7)[1]], w=[tmp_spec], tag=tag + "a2")
        P.op(aff_eng, lambda e: e.tensor_tensor(tmp_ap, tmp_ap, g_ap, op=ALU.mult),
             r=[tmp_spec, g_spec], w=[tmp_spec], tag=tag + "gm")
        dve(lambda e: e.tensor_tensor(nrm_ap, tmp_ap, b_ap, op=ALU.add),
            r=[tmp_spec, b_spec], w=[nrm_spec], tag=tag + "ba")

    NT = B("NT", [128, IC, KV], BF16)
    XS = B("XS", [128, 8, 2052], BF16)
    zrs = [B(f"zr{i}", [128, 512], BF16) for i in range(2)]
    ln1g = B("ln1g", [128, D], F32)
    ln1b = B("ln1b", [128, D], F32)
    bcast_load("qsp", ln1g, din["ln1_g"])
    bcast_load("qsp", ln1b, din["ln1_b"])
    lntmp = B("lntmp", [128, D], F32)
    lntmp2 = B("lntmp2", [128, D], F32)
    x0p = B("x0p", [3, D], F32)
    nrmp = B("nrmp", [3, D], BF16)
    dma("qsp", x0p.ap, din["xp3"], w=[x0p.spec()], tag="xp3")
    layer_norm2(x0p.ap, x0p.spec(), lntmp.ap[0:3, :], lntmp.spec(),
                nrmp.ap, nrmp.spec(), pdim=3,
                g=(ln1g.ap[0:3, :], ln1g.spec(), ln1b.ap[0:3, :], ln1b.spec(),
                   lntmp2.ap[0:3, :], lntmp2.spec()), tag="p3")
    for ic in range(IC):
        tb_ = 6 + ic % 2
        transp(psbb(tb_, 0, 128, 0, 3), nrmp.ap[:, ic * 128:(ic + 1) * 128],
               r=[nrmp.spec(ic * 128, (ic + 1) * 128)], w=[ps_spec(tb_)], tag="p3t")
        dve(lambda e, i=ic, b=tb_: e.tensor_copy(np3.ap[:, i, :], psbb(b, 0, 128, 0, 3)),
            r=[ps_spec(tb_)], w=[np3.spec3(ic, 0, 3)], tag="p3e")

    x0p.free(); nrmp.free()
    x0r = [B(f"x0r{i}", [128, D], F32) for i in range(4)]
    n0r = [B(f"n0r{i}", [128, D], F32) for i in range(4)]
    nrmr = [B(f"nrm{i}", [128, D], BF16) for i in range(4)]
    lnt4 = [B(f"lnt4_{i}", [128, D], F32) for i in range(4)]
    gts = [B(f"gts{i}", [128, D], F32) for i in range(2)]
    rv = B("rv", [128, 16, 2], F32)
    winr = [B(f"winr{i}", [128, 8, 128], BF16) for i in range(2)]
    wgateS = B("wgateS", [128, 8, D], BF16)
    dma("qpool", wgateS.ap, din["Wgate"].rearrange("(g p) n -> p g n", p=128),
        w=[wgateS.spec()], tag="wgate")
    dve(lambda e: e.memset(XS.ap[:, :, 0:4], 0.0), r=[],
        w=[XS.spec3(c, 0, 4) for c in range(8)], tag="xspad")

    def p1_ln(c, comp_q):
        own = c < 4
        src = din["x0"] if own else din["xsib"]
        chains = []
        info = []
        for j in range(4):
            sg = c * 4 + j
            row0 = (c % 4) * 512 + j * 128
            xb = x0r[sg % 4]
            dma("qsp" if own else "qact", xb.ap, src[row0:row0 + 128, :],
                w=[xb.spec()], tag=f"x0_{sg}")
            n0b = n0r[sg % 4]
            nb = nrmr[sg % 4]
            ltb = lnt4[sg % 4]
            chains.append(layer_norm2_stages(
                xb.ap, xb.spec(), n0b.ap, n0b.spec(), nb.ap, nb.spec(),
                g=(ln1g.ap, ln1g.spec(), ln1b.ap, ln1b.spec(),
                   ltb.ap, ltb.spec()), tag=f"n_{sg}", slot=sg % 4))
            info.append((sg, row0, n0b, nb))
        for k in range(11):
            for ch in chains:
                ch[k]()
            if comp_q and k % 3 == 2:
                comp_q.pop(0)()
        for (sg, row0, n0b, nb) in info:
            if own:
                dma("qact", sp_n0[row0:row0 + 128, :], n0b.ap, r=[n0b.spec()],
                    w=[("hb_n0", row0, row0 + 128)], tag=f"spn0_{sg}")
            tcol = sg * 128 if own else 2048 + (sg - 16) * 128
            for ic in range(IC):
                tb_ = 6 + ic % 2
                transp(psbb(tb_, 0, 128, 0, 128), nb.ap[:, ic * 128:(ic + 1) * 128],
                       r=[nb.spec(ic * 128, (ic + 1) * 128)], w=[ps_spec(tb_)],
                       tag=f"tr_{sg}_{ic}")
                dve(lambda e, i=ic, t=tcol, b=tb_, nb2=nb: e.tensor_copy(
                    NT.ap[:, i, t:t + 128], psbb(b, 0, 128, 0, 128)),
                    r=[ps_spec(tb_)], w=[NT.spec3(ic, tcol, tcol + 128)], tag=f"te_{sg}_{ic}")
            for _ in range(2):
                if comp_q:
                    comp_q.pop(0)()

    def p1_compute(c):
        steps = []
        t0 = c * 512
        nt_r = [NT.spec3(i, t0, t0 + 512) for i in range(IC)]
        for cc in range(16):
            def step_cc(cc=cc):
                wb = winr[cc % 2]
                dma("qpool", wb.ap, din["Win"][:, cc * 128:(cc + 1) * 128]
                    .rearrange("(g p) n -> p g n", p=128), w=[wb.spec()],
                    tag=f"win_{c}_{cc}")
                bank = cc % 2
                for i in range(IC):
                    mm(psf_bank(bank), wb.ap[:, i, :], NT.ap[:, i, t0:t0 + 512],
                       i == 0, i == IC - 1, r=[wb.spec3(i, 0, 128), nt_r[i]],
                       w=[ps_spec(bank)], tag=f"xz_{c}_{cc}")
                if cc < 8:
                    dve(lambda e, ccc=cc, b=bank, tt=t0: e.tensor_copy(
                        XS.ap[:, ccc, 4 + tt: 4 + tt + 512], psf_bank(b)),
                        r=[ps_spec(bank)], w=[XS.spec3(cc, 4 + t0, 4 + t0 + 512)],
                        tag=f"xse_{c}_{cc}")
                else:
                    zr = zrs[cc % 2]
                    dve(lambda e, b=bank, z=zr: e.tensor_copy(z.ap, psf_bank(b)),
                        r=[ps_spec(bank)], w=[zr.spec()], tag=f"sgz_{c}_{cc}")
                    dma("qact", sgz_scr[cc - 8][:, t0:t0 + 512], zr.ap,
                        r=[zr.spec()], w=[("hb_z", (cc - 8) * 4 + t0 // 512,
                                          (cc - 8) * 4 + t0 // 512 + 1)],
                        tag=f"zs_{c}_{cc}")
                if c == 0 and cc < 8:
                    for i in range(IC):
                        mm(psf_bank(4, 0, 128, 0, 3), wb.ap[:, i, :], np3.ap[:, i, :],
                           i == 0, i == IC - 1,
                           r=[wb.spec3(i, 0, 128), np3.spec3(i, 0, 3)],
                           w=[ps_spec(4)], tag=f"ctx_{cc}")
                    dve(lambda e, ccc=cc: e.tensor_scalar(
                        XS.ap[:, ccc, 1:4], psf_bank(4, 0, 128, 0, 3), cft.ap, None,
                        op0=ALU.mult), r=[ps_spec(4), cft.spec()],
                        w=[XS.spec3(cc, 1, 4)], tag=f"ctxe_{cc}")
            steps.append(step_cc)
        for j in range(4):
            def step_j(j=j):
                sg2 = c * 4 + j
                tl = sg2 * 128
                for nh in range(2):
                    for i in range(IC):
                        mm(psf_bank(2 + nh), NT.ap[:, i, tl:tl + 128],
                           wgateS.ap[:, i, nh * 512:(nh + 1) * 512], i == 0, i == IC - 1,
                           r=[NT.spec3(i, tl, tl + 128),
                              wgateS.spec3(i, nh * 512, (nh + 1) * 512)],
                           w=[ps_spec(2 + nh)], tag=f"gate_{sg2}_{nh}")
                gb = gts[sg2 % 2]
                act(AF.Sigmoid, gb.ap, PSF.ap()[:, 2 * 512: 4 * 512],
                    r=[ps_spec(2), ps_spec(3)], w=[gb.spec()], tag=f"gsig_{sg2}")
                dma("qact", sp_gate[tl:tl + 128, :], gb.ap, r=[gb.spec()],
                    w=[("hb_gate", tl, tl + 128)], tag=f"gsp_{sg2}")
                for i in range(IC):
                    mm(psf_bank(4, 0, 128, 0, 2), NT.ap[:, i, tl:tl + 128],
                       wr_s.ap[:, i, :], i == 0, i == IC - 1,
                       r=[NT.spec3(i, tl, tl + 128), wr_s.spec3(i, 0, 2)],
                       w=[ps_spec(4)], tag=f"r_{sg2}")
                dve(lambda e, ss=sg2: e.tensor_copy(rv.ap[:, ss, :],
                                                    psf_bank(4, 0, 128, 0, 2)),
                    r=[ps_spec(4)], w=[rv.spec3(sg2, 0, 2)], tag=f"rv_{sg2}")
            steps.append(step_j)
        return steps

    def p1_halt(sg2):
        if True:
            if True:
                act(AF.Sigmoid, alpha.ap[:, sg2:sg2 + 1], rv.ap[:, sg2, 0:1],
                    r=[rv.spec3(sg2, 0, 2), brt.spec()], w=[alpha.spec(sg2, sg2 + 1)],
                    bias=brt.ap[:, 0:1], tag=f"al_{sg2}")
                for k in range(3):
                    act(AF.Sigmoid, hlt.ap[:, k:k + 1], rv.ap[:, sg2, 1:2],
                        r=[rv.spec3(sg2, 0, 2), b1st.spec()], w=[hlt.spec(k, k + 1)],
                        bias=b1st.ap[:, k:k + 1], tag=f"h_{sg2}_{k}")
                ca, cb_, cc_, cd = 32, 33, 34, 35
                dve(lambda e: e.tensor_scalar(stt.ap[:, ca:ca + 1], hlt.ap[:, 0:1],
                                              -1.0, 1.0, op0=ALU.mult, op1=ALU.add),
                    r=[hlt.spec(0, 1)], w=[stt.spec(ca, ca + 1)], tag=f"w0_{sg2}")
                dve(lambda e: e.tensor_scalar(stt.ap[:, ca:ca + 1], stt.ap[:, ca:ca + 1],
                                              0.0, None, op0=ALU.max),
                    r=[stt.spec(ca, ca + 1)], w=[stt.spec(ca, ca + 1)], tag=f"w1_{sg2}")
                dve(lambda e: e.tensor_tensor(stt.ap[:, cb_:cb_ + 1], hlt.ap[:, 1:2],
                                              stt.ap[:, ca:ca + 1], op=ALU.mult),
                    r=[hlt.spec(1, 2), stt.spec(ca, ca + 1)],
                    w=[stt.spec(cb_, cb_ + 1)], tag=f"w2_{sg2}")
                dve(lambda e: e.tensor_tensor(stt.ap[:, cc_:cc_ + 1], stt.ap[:, ca:ca + 1],
                                              stt.ap[:, cb_:cb_ + 1], op=ALU.subtract),
                    r=[stt.spec(ca, cb_ + 1)], w=[stt.spec(cc_, cc_ + 1)], tag=f"w3_{sg2}")
                dve(lambda e: e.tensor_scalar(stt.ap[:, cc_:cc_ + 1], stt.ap[:, cc_:cc_ + 1],
                                              0.0, None, op0=ALU.max),
                    r=[stt.spec(cc_, cc_ + 1)], w=[stt.spec(cc_, cc_ + 1)], tag=f"w4_{sg2}")
                dve(lambda e: e.tensor_tensor(stt.ap[:, cd:cd + 1], hlt.ap[:, 0:1],
                                              stt.ap[:, cb_:cb_ + 1], op=ALU.add),
                    r=[hlt.spec(0, 1), stt.spec(cb_, cb_ + 1)],
                    w=[stt.spec(cd, cd + 1)], tag=f"w5_{sg2}")
                dve(lambda e, ss=sg2: e.tensor_tensor(wsum.ap[:, ss:ss + 1],
                                                      stt.ap[:, cd:cd + 1],
                                                      stt.ap[:, cc_:cc_ + 1], op=ALU.add),
                    r=[stt.spec(cc_, cd + 1)], w=[wsum.spec(sg2, sg2 + 1)], tag=f"w6_{sg2}")

    p1_order = (0, 4, 1, 5, 2, 6, 3, 7)
    comp_q = []
    for c_ in p1_order:
        p1_ln(c_, comp_q)
        if c_ < 4:
            comp_q.extend(p1_compute(c_))
    while comp_q:
        comp_q.pop(0)()

    ln1g.free(); ln1b.free()
    for bb in x0r + n0r + nrmr + gts + winr + lnt4 + zrs:
        bb.free()
    wgateS.free(); lntmp.free(); lntmp2.free()

    y1s = [B(f"y1_{i}", [128, 1024], F32) for i in range(4)]
    y2s = [B(f"y2_{i}", [128, 1024], F32) for i in range(4)]
    silys = [B(f"sily{i}", [128, 1024], BF16) for i in range(4)]
    uh = B("uh", [128, 8, 1024], BF16)
    woS = B("woS", [128, 8, D], BF16)
    ssms = [B(f"ssms{i}", [128, D], F32) for i in range(2)]
    dma("qpool", woS.ap, din["Wout_ssm"].rearrange("(g p) n -> p g n", p=128),
        w=[woS.spec()], tag="wout_ssm")

    zls = [B(f"zl{i}", [128, 1024], BF16) for i in range(4)]
    halt_q = list(range(NS))
    sm_list = []
    for th in range(2):
        tb = th * 1024
        def conv_stages(ccc):
            y1, y2, sily = y1s[ccc % 4], y2s[ccc % 4], silys[ccc % 4]
            zl = zls[ccc % 4]

            def st0():
                p1_halt(halt_q.pop(0))
                dma("qact", zl.ap, sgz_scr[ccc][:, tb:tb + 1024],
                    r=[("hb_z", ccc * 4 + th * 2, ccc * 4 + th * 2 + 2)],
                    w=[zl.spec()], tag=f"zl_{th}_{ccc}")
            stg = [st0]
            stg.append(lambda: act(AF.Sigmoid, zl.ap, zl.ap, r=[zl.spec()],
                                   w=[zl.spec()], tag=f"sgs_{th}_{ccc}"))
            stg.append(lambda: dve(lambda e, cc2=ccc, t=tb, y=y1: e.tensor_scalar(
                y.ap, XS.ap[:, cc2, 1 + t:1 + t + 1024], convw.ap[:, cc2, 0:1], None,
                op0=ALU.mult),
                r=[XS.spec3(ccc, 1 + tb, 1 + tb + 1024), convw.spec()],
                w=[y1.spec()], tag=f"cv0_{th}_{ccc}"))
            pairs = [(y1, y2), (y2, y1), (y1, y2)]
            for j in range(1, 4):
                si, di_ = pairs[j - 1]
                stg.append(lambda j=j, si=si, di_=di_: dve(
                    lambda e, cc2=ccc, t=tb, jj=j, s=si, d=di_: e.scalar_tensor_tensor(
                        d.ap, XS.ap[:, cc2, 1 + t + jj:1 + t + jj + 1024],
                        convw.ap[:, cc2, jj:jj + 1], s.ap, op0=ALU.mult, op1=ALU.add),
                    r=[XS.spec3(ccc, 1 + tb + j, 1 + tb + j + 1024), convw.spec(),
                       si.spec()], w=[di_.spec()], tag=f"cv{j}_{th}_{ccc}"))
            stg.append(lambda: act(AF.Silu, sily.ap, y2.ap,
                                   r=[y2.spec(), convb.spec()], w=[sily.spec()],
                                   bias=convb.ap[:, ccc:ccc + 1], tag=f"si_{th}_{ccc}"))
            stg.append(lambda: P.op("pool", lambda e, cc2=ccc, sl_=sily, z=zl: e.tensor_tensor(
                uh.ap[:, cc2, :], sl_.ap, z.ap, op=ALU.mult),
                r=[sily.spec(), zl.spec()],
                w=[uh.spec3(ccc, 0, 1024)], tag=f"u_{th}_{ccc}"))
            return stg

        for c4_ in range(2):
            quads = [conv_stages(4 * c4_ + i) for i in range(4)]
            for k in range(len(quads[0])):
                for q_ in quads:
                    q_[k]()
        for j8 in range(8):
            def sm_unit(th=th, j8=j8):
                sg = th * 8 + j8
                tl = j8 * 128
                pb0 = 0 if sg % 2 == 0 else 4
                for cc2 in range(8):
                    for nh in range(2):
                        mm(psf_bank(pb0 + nh), uh.ap[:, cc2, tl:tl + 128],
                           woS.ap[:, cc2, nh * 512:(nh + 1) * 512], cc2 == 0, cc2 == 7,
                           r=[uh.spec3(cc2, tl, tl + 128),
                              woS.spec3(cc2, nh * 512, (nh + 1) * 512)],
                           w=[ps_spec(pb0 + nh)], tag=f"sm_{sg}_{nh}")
                sb_ = ssms[sg % 2]
                dve(lambda e, b=sb_, p=pb0: e.tensor_copy(b.ap, PSF.ap()[:, p * 512:(p + 2) * 512]),
                    r=[ps_spec(pb0, pb0 + 2)], w=[sb_.spec()], tag=f"sme_{sg}")
                row = sg * 128
                dma("qact", sp_ssm[row:row + 128, :], sb_.ap, r=[sb_.spec()],
                    w=[("hb_ssm", row, row + 128)], tag=f"ssp_{sg}")
            sm_list.append(sm_unit)

    XS.free()
    for bb in y1s + y2s + silys + zls:
        bb.free()

    kta = [B("kta0", [128, KV], BF16)]
    ktb = [B("ktb0", [128, KV], BF16)]
    qta = [B("qta0", [128, TOK], BF16)]
    qtb = [B("qtb0", [128, TOK], BF16)]
    v2 = [B("v2_0", [128, 32, 130], BF16)]
    vts = [B("vts_0", [128, KV], BF16)]
    wq2 = [B("wq2_0", [128, 8, 128], BF16)]
    wk2 = [B("wk2_0", [128, 8, 128], BF16)]
    wv2 = [B("wv2_0", [128, 8, 128], BF16)]
    nsq = B("nsq", [128, 1024], BF16)
    qstg = B("qstg", [128, 512], F32)
    nsc = B("nsc", [128, 8], F32)

    def init_kqv(par_):
        P.op("dve", lambda e, k=kta[par_]: e.memset(k.ap[64:128, :], 0.0),
             w=[kta[par_].spec()], tag=f"kz_{par_}")
        P.op("dve", lambda e, k=kta[par_]: e.memset(k.ap[64:65, :], -1.0),
             r=[kta[par_].spec()], w=[kta[par_].spec()], tag=f"k1_{par_}")
        P.op("dve", lambda e, k=qta[par_]: e.memset(k.ap[64:128, :], 0.0),
             w=[qta[par_].spec()], tag=f"qz_{par_}")
        P.op("dve", lambda e, v=v2[par_]: e.memset(v.ap, 0.0), w=[v2[par_].spec()],
             tag=f"vz_{par_}")
        P.op("dve", lambda e, v=v2[par_]: e.memset(v.ap[:, :, 64:65], 1.0),
             r=[v2[par_].spec()], w=[v2[par_].spec()], tag=f"vo1_{par_}")
        P.op("dve", lambda e, v=v2[par_]: e.memset(v.ap[:, :, 129:130], 1.0),
             r=[v2[par_].spec()], w=[v2[par_].spec()], tag=f"vo2_{par_}")

    init_kqv(0)

    def headb_move_step(hp, kind):
        par = hp % 2
        db_ = (qtb if kind == "q" else ktb)[par]

        def stp(db_=db_, kind=kind, hp=hp):
            dma("qsp", db_.ap[0:64, :], db_.ap[64:128, :], r=[db_.spec()],
                w=[db_.spec()], tag=f"{kind}mv_{hp}")
            P.op("pool", lambda e, k=db_: e.memset(k.ap[64:128, :], 0.0), r=[db_.spec()],
                 w=[db_.spec()], tag=f"{kind}zz_{hp}")
            if kind == "k":
                P.op("pool", lambda e, k=db_: e.memset(k.ap[64:65, :], -1.0), r=[db_.spec()],
                     w=[db_.spec()], tag=f"{kind}one_{hp}")
        return stp

    def proj_steps(hp):
        par = hp % 2
        wq, wk, wv = wq2[0], wk2[0], wv2[0]
        c0 = hp * 128
        dma("qpool", wq.ap, din["Wqkv"][:, c0:c0 + 128]
            .rearrange("(g p) n -> p g n", p=128), w=[wq.spec()], tag=f"wq_{hp}")
        dma("qpool", wk.ap, din["Wqkv"][:, D + c0:D + c0 + 128]
            .rearrange("(g p) n -> p g n", p=128), w=[wk.spec()], tag=f"wk_{hp}")
        dma("qpool", wv.ap, din["Wqkv"][:, 2 * D + c0:2 * D + c0 + 128]
            .rearrange("(g p) n -> p g n", p=128), w=[wv.spec()], tag=f"wv_{hp}")
        steps = []
        gctr = [0]

        def grp(kind, wbuf, da, db_, t0):
            def g(kind=kind, wbuf=wbuf, da=da, db_=db_, t0=t0):
                bank = 3 if gctr[0] % 2 == 0 else 7
                gctr[0] += 1
                for i in range(IC):
                    mm(psf_bank(bank), wbuf.ap[:, i, :], NT.ap[:, i, t0:t0 + 512],
                       i == 0, i == IC - 1,
                       r=[wbuf.spec3(i, 0, 128), NT.spec3(i, t0, t0 + 512)],
                       w=[ps_spec(bank)], tag=f"{kind}p_{hp}_{t0}")
                if kind == "v":
                    dve(lambda e, t=t0, b=bank: e.tensor_copy(
                        vts[0].ap[:, t:t + 512], psf_bank(b)),
                        r=[ps_spec(bank)], w=[vts[0].spec(t0, t0 + 512)],
                        tag=f"ve_{hp}_{t0}")
                else:
                    dve(lambda e, d=da, t=t0, b=bank: e.tensor_copy(
                        d.ap[0:64, t:t + 512], psf_bank(b, 0, 64)),
                        r=[ps_spec(bank)], w=[da.spec(t0, t0 + 512)],
                        tag=f"{kind}a_{hp}_{t0}")
                    dve(lambda e, d=db_, t=t0, b=bank: e.tensor_copy(
                        d.ap[64:128, t:t + 512], psf_bank(b, 64, 128)),
                        r=[ps_spec(bank)], w=[db_.spec(t0, t0 + 512)],
                        tag=f"{kind}b_{hp}_{t0}")
            return g

        for t0 in range(0, TOK, 512):
            steps.append(grp("q", wq, qta[par], qtb[par], t0))
        steps.append(headb_move_step(hp, "q"))
        for t0 in range(0, KV, 512):
            steps.append(grp("k", wk, kta[par], ktb[par], t0))
        steps.append(headb_move_step(hp, "k"))
        for t0 in range(0, KV, 512):
            steps.append(grp("v", wv, None, None, t0))
        return steps

    def v_steps(hp):
        par = hp % 2
        vv, vt = v2[par], vts[0]
        steps = []
        for ts_ in range(32):
            def step(ts2=ts_, vv2=vv, vt2=vt, hp2=hp):
                t0 = ts2 * 128
                transp(psbb(7, 0, 128, 0, 128), vt2.ap[:, t0:t0 + 128],
                       r=[vt2.spec(t0, t0 + 128)], w=[ps_spec(7)], tag=f"vt_{hp2}_{ts2}")
                dst = vv2.ap[:, ts2, :].rearrange("p (a b) -> p a b", a=2, b=65)[:, :, 0:64]
                srcp = psbb(7, 0, 128, 0, 128).rearrange("p (a b) -> p a b", a=2, b=64)
                dve(lambda e, d=dst, s2=srcp: e.tensor_copy(d, s2),
                    r=[ps_spec(7)], w=[vv2.spec3(ts2, 0, 130)], tag=f"va_{hp2}_{ts2}")
            steps.append(step)
        return steps

    def norm_steps(hp):
        par = hp % 2
        steps = []
        for qbuf, kbuf in ((qta[par], kta[par]), (qtb[par], ktb[par])):
            for h in range(4):
                def sq_k(kb2=kbuf, hp2=hp, h2=h):
                    dve(lambda e: e.tensor_tensor(
                        nsq.ap, kb2.ap[:, h2 * 1024:(h2 + 1) * 1024],
                        kb2.ap[:, h2 * 1024:(h2 + 1) * 1024], op=ALU.mult),
                        r=[kb2.spec()], w=[nsq.spec()], tag=f"ksq_{hp2}_{h2}")
                steps.append(sq_k)
                for j in range(2):
                    def kn_j(j2=j, hp2=hp, h2=h):
                        mm(PSF.ap()[64:65, 3 * 512:3 * 512 + 512], onesb.ap,
                           nsq.ap[:, j2 * 512:(j2 + 1) * 512], True, True,
                           r=[onesb.spec(), nsq.spec()], w=[ps_spec(3)],
                           tag=f"kn_{hp2}_{h2}_{j2}")
                        dve(lambda e: e.reduce_max(nsc.ap[64:65, 1:2],
                                                   PSF.ap()[64:65, 3 * 512:3 * 512 + 512],
                                                   axis=AX.X),
                            r=[ps_spec(3)], w=[nsc.spec(1, 2)], tag=f"knr_{hp2}_{h2}_{j2}")
                        if h2 == 0 and j2 == 0:
                            dve(lambda e: e.tensor_copy(nsc.ap[64:65, 0:1],
                                                        nsc.ap[64:65, 1:2]),
                                r=[nsc.spec(1, 2)], w=[nsc.spec(0, 1)], tag=f"kni_{hp2}")
                        else:
                            dve(lambda e: e.tensor_tensor(nsc.ap[64:65, 0:1],
                                                          nsc.ap[64:65, 0:1],
                                                          nsc.ap[64:65, 1:2], op=ALU.max),
                                r=[nsc.spec(0, 2)], w=[nsc.spec(0, 1)],
                                tag=f"knm_{hp2}_{h2}_{j2}")
                    steps.append(kn_j)

            def km(hp2=hp):
                act(AF.Ln, nsc.ap[64:65, 2:3], nsc.ap[64:65, 0:1],
                    r=[nsc.spec(0, 1)], w=[nsc.spec(2, 3)], tag=f"km_{hp2}")
            steps.append(km)

            for h in range(2):
                def sq_q(qb2=qbuf, hp2=hp, h2=h):
                    dve(lambda e: e.tensor_tensor(
                        nsq.ap[0:64, :], qb2.ap[0:64, h2 * 1024:(h2 + 1) * 1024],
                        qb2.ap[0:64, h2 * 1024:(h2 + 1) * 1024], op=ALU.mult),
                        r=[qb2.spec()], w=[nsq.spec()], tag=f"qsq_{hp2}_{h2}")
                steps.append(sq_q)
                for j in range(2):
                    def qn_j(j2=j, qb2=qbuf, hp2=hp, h2=h):
                        mm(PSF.ap()[64:65, 3 * 512:3 * 512 + 512], onesb.ap,
                           nsq.ap[:, j2 * 512:(j2 + 1) * 512], True, True,
                           r=[onesb.spec(), nsq.spec()], w=[ps_spec(3)],
                           tag=f"qn_{hp2}_{h2}_{j2}")
                        act(AF.Ln, qstg.ap[64:65, :],
                            PSF.ap()[64:65, 3 * 512:3 * 512 + 512],
                            r=[ps_spec(3)], w=[qstg.spec()], tag=f"ql_{hp2}_{h2}_{j2}")
                        dve(lambda e: e.tensor_scalar(qstg.ap[64:65, :],
                                                      qstg.ap[64:65, :],
                                                      nsc.ap[64:65, 2:3], None,
                                                      op0=ALU.add),
                            r=[qstg.spec(), nsc.spec(2, 3)], w=[qstg.spec()],
                            tag=f"qd_{hp2}_{h2}_{j2}")
                        q0_ = (h2 * 2 + j2) * 512
                        act(AF.Exp, qb2.ap[64:65, q0_:q0_ + 512],
                            qstg.ap[64:65, :], r=[qstg.spec()], w=[qb2.spec()],
                            scale=0.5, tag=f"qw_{hp2}_{h2}_{j2}")
                    steps.append(qn_j)
        return steps

    def st_pv_qb2(hp, qb, steps, pending, len0):
        par = hp % 2
        q0 = qb * 512
        vv = v2[par]
        cfg = []
        for hoff in range(2):
            cfg.append(((qta if hoff == 0 else qtb)[par],
                        (kta if hoff == 0 else ktb)[par],
                        0 if hoff == 0 else 4, 2 if hoff == 0 else 6,
                        ptbs[hoff], 65 * hoff))

        def S(hoff, kc):
            qsrc, ksrc, sb0, ob, ptbx, voff = cfg[hoff]
            bank = sb0 + kc % 2
            mm(psf_bank(bank), ksrc.ap[:, kc * 128:(kc + 1) * 128],
               qsrc.ap[:, q0:q0 + 512], True, True,
               r=[ksrc.spec(kc * 128, (kc + 1) * 128), qsrc.spec(q0, q0 + 512)],
               w=[ps_spec(bank)], tag=f"st_{hp}_{hoff}_{qb}_{kc}")

        def E(hoff, pp):
            qsrc, ksrc, sb0, ob, ptbx, voff = cfg[hoff]
            slot = (2 * pp) % 4
            act(AF.Exp, ptbx.ap[:, slot:slot + 2, :],
                PSF.ap()[:, sb0 * 512:(sb0 + 2) * 512],
                r=[ps_spec(sb0, sb0 + 2)],
                w=[ptbx.spec3(slot, 0, 512), ptbx.spec3(slot + 1, 0, 512)],
                scale=0.125, tag=f"ex_{hp}_{hoff}_{qb}_{pp}")

        def PV(hoff, kc):
            qsrc, ksrc, sb0, ob, ptbx, voff = cfg[hoff]
            slot = kc % 4
            mm(psf_bank(ob, 0, 65), vv.ap[:, kc, voff:voff + 65], ptbx.ap[:, slot, :],
               kc == 0, kc == 31,
               r=[vv.spec3(kc, voff, voff + 65), ptbx.spec3(slot, 0, 512)],
               w=[ps_spec(ob)], tag=f"pv_{hp}_{hoff}_{qb}_{kc}")

        idx0 = qb * 16
        for pp in range(16):
            for hoff in range(2):
                S(hoff, 2 * pp)
                S(hoff, 2 * pp + 1)
            E(0, pp)
            E(1, pp)
            if pp >= 1:
                for hoff in range(2):
                    PV(hoff, 2 * pp - 2)
                    PV(hoff, 2 * pp - 1)
            if pending and pp in (1, 2):
                pending.pop(0)()
            idx = idx0 + pp
            tgt = len0 if idx < 6 else len0 * (63 - idx) // 58
            while steps and len(steps) > tgt:
                steps.pop(0)()
        for hoff in range(2):
            PV(hoff, 30)
            PV(hoff, 31)
        for hoff in range(2):
            qsrc, ksrc, sb0, ob, ptbx, voff = cfg[hoff]
            tmpx, rzx = tmpas[hoff], rz0s[hoff]
            dve(lambda e, b=ob, t=tmpx: e.tensor_copy(t.ap, psf_bank(b, 0, 65)),
                r=[ps_spec(ob)], w=[tmpx.spec()], tag=f"fe_{hp}_{hoff}_{qb}")
            act(AF.Ln, rzx.ap[64:65, :], tmpx.ap[64:65, :],
                r=[tmpx.spec()], w=[rzx.spec()], tag=f"rl_{hp}_{hoff}_{qb}")
            act(AF.Exp, rzx.ap[64:65, :], rzx.ap[64:65, :],
                r=[rzx.spec()], w=[rzx.spec()], scale=-1.0,
                tag=f"rz_{hp}_{hoff}_{qb}")

            def part_b(hp2=hp, hoff2=hoff, qb2=qb, q02=q0, tmpx=tmpx, rzx=rzx):
                cb = 3 if hoff2 == 0 else 7
                mm(psf_bank(cb, 0, 64), ones_f.ap[64:65, 0:64], rzx.ap[64:65, :],
                   True, True, r=[ones_f.spec(), rzx.spec()], w=[ps_spec(cb)],
                   tag=f"bc_{hp2}_{hoff2}_{qb2}")
                if hoff2 == 0:
                    dve(lambda e: e.tensor_tensor(ATT.ap[0:64, hp2, q02:q02 + 512],
                                                  tmpx.ap[0:64, :], psf_bank(cb, 0, 64),
                                                  op=ALU.mult),
                        r=[tmpx.spec(), ps_spec(cb)], w=[ATT.spec3(hp2, q02, q02 + 512)],
                        tag=f"fn_{hp2}_{hoff2}_{qb2}")
                else:
                    dve(lambda e: e.tensor_tensor(nrmb.ap, tmpx.ap[0:64, :],
                                                  psf_bank(cb, 0, 64), op=ALU.mult),
                        r=[tmpx.spec(), ps_spec(cb)], w=[nrmb.spec()],
                        tag=f"fn_{hp2}_{hoff2}_{qb2}")
                    dma("qsp", ATT.ap[64:128, hp2, q02:q02 + 512], nrmb.ap, r=[nrmb.spec()],
                        w=[ATT.spec3(hp2, q02, q02 + 512)], tag=f"fv_{hp2}_{hoff2}_{qb2}")
            pending.append(part_b)

    def make_p0_steps():
        p0 = {}
        for nm, sh in [("g1r", [32, 128]), ("g1i", [32, 128]),
                       ("g2r", [16, 4, 16]), ("g2i", [16, 4, 16]),
                       ("qselb", [32, 512]), ("pseltb", [32, 4, 128]),
                       ("rseltb", [16, 128]), ("sselb", [16, 512]),
                       ("t1s", [32, 512]), ("t2s", [16, 512]),
                       ("e1s", [128, 512]), ("tmpw", [128, 512]),
                       ("wtRf", [128, 4, 512]), ("wtIf", [128, 4, 512])]:
            p0[nm] = B(nm, sh, F32)
        dma("qsp", p0["g1r"].ap, din["G1R"].rearrange("o b c d -> (o b) (c d)"), w=[p0["g1r"].spec()])
        dma("qsp", p0["g1i"].ap, din["G1I"].rearrange("o b c d -> (o b) (c d)"), w=[p0["g1i"].spec()])
        for c in range(4):
            dma("qsp", p0["g2r"].ap[:, c, :], din["G2R"][c].rearrange("e o g -> e (o g)"),
                w=[p0["g2r"].spec3(c, 0, 16)])
            dma("qsp", p0["g2i"].ap[:, c, :], din["G2I"][c].rearrange("e o g -> e (o g)"),
                w=[p0["g2i"].spec3(c, 0, 16)])
        dma("qsp", p0["qselb"].ap, din["qsel"], w=[p0["qselb"].spec()])
        dma("qsp", p0["pseltb"].ap, din["pselt"].rearrange("a (t n) -> a t n", t=4),
            w=[p0["pseltb"].spec()])
        dma("qsp", p0["rseltb"].ap, din["rselt"], w=[p0["rseltb"].spec()])
        dma("qsp", p0["sselb"].ap, din["ssel"], w=[p0["sselb"].spec()])

        steps = []
        for g1, g2, wtf in ((p0["g1r"], p0["g2r"], p0["wtRf"]),
                            (p0["g1i"], p0["g2i"], p0["wtIf"])):
            for c in range(4):
                def stepc(g1=g1, g2=g2, c=c):
                    mm(psf_bank(3, 0, 32), g1.ap[:, c * 32:(c + 1) * 32], p0["qselb"].ap,
                       True, True, r=[g1.spec(c * 32, (c + 1) * 32), p0["qselb"].spec()],
                       w=[ps_spec(3)], tag="t1")
                    dve(lambda e: e.tensor_copy(p0["t1s"].ap, psf_bank(3, 0, 32)),
                        r=[ps_spec(3)], w=[p0["t1s"].spec()], tag="t1e")
                    mm(psf_bank(7, 0, 16), g2.ap[:, c, :], p0["sselb"].ap,
                       True, True, r=[g2.spec3(c, 0, 16), p0["sselb"].spec()],
                       w=[ps_spec(7)], tag="t2")
                    dve(lambda e: e.tensor_copy(p0["t2s"].ap, psf_bank(7, 0, 16)),
                        r=[ps_spec(7)], w=[p0["t2s"].spec()], tag="t2e")
                steps.append(stepc)
                for t in range(4):
                    def stept(wtf=wtf, c=c, t=t):
                        mm(psf_bank(3), p0["pseltb"].ap[:, t, :], p0["t1s"].ap, True, True,
                           r=[p0["pseltb"].spec3(t, 0, 128), p0["t1s"].spec()],
                           w=[ps_spec(3)], tag="e1")
                        mm(psf_bank(7), p0["rseltb"].ap, p0["t2s"].ap, True, True,
                           r=[p0["rseltb"].spec(), p0["t2s"].spec()], w=[ps_spec(7)], tag="e2")
                        dve(lambda e: e.tensor_copy(p0["e1s"].ap, psf_bank(3)),
                            r=[ps_spec(3)], w=[p0["e1s"].spec()], tag="e1e")
                        if c == 0:
                            dve(lambda e, tt=t, wf=wtf: e.tensor_tensor(
                                wf.ap[:, tt, :], p0["e1s"].ap, psf_bank(7), op=ALU.mult),
                                r=[p0["e1s"].spec(), ps_spec(7)], w=[wtf.spec3(t, 0, 512)], tag="wt0")
                        else:
                            dve(lambda e: e.tensor_tensor(p0["tmpw"].ap, p0["e1s"].ap,
                                                          psf_bank(7), op=ALU.mult),
                                r=[p0["e1s"].spec(), ps_spec(7)], w=[p0["tmpw"].spec()], tag="wtm")
                            dve(lambda e, tt=t, wf=wtf: e.tensor_tensor(
                                wf.ap[:, tt, :], wf.ap[:, tt, :], p0["tmpw"].ap, op=ALU.add),
                                r=[p0["tmpw"].spec(), wtf.spec3(t, 0, 512)],
                                w=[wtf.spec3(t, 0, 512)], tag="wta")
                    steps.append(stept)

        def stepfin():
            dma("qpool", wt_scr[0], p0["wtRf"].ap, r=[p0["wtRf"].spec()],
                w=[("hb_wt", 0, 1)], tag="wtRs")
            dma("qpool", wt_scr[1], p0["wtIf"].ap, r=[p0["wtIf"].spec()],
                w=[("hb_wt", 1, 2)], tag="wtIs")
            for v in p0.values():
                v.free()
        steps.append(stepfin)
        return steps

    prep0 = proj_steps(0) + v_steps(0) + norm_steps(0)
    ksm = 0
    for i_, stp in enumerate(prep0):
        stp()
        if i_ % 6 == 5 and ksm < len(sm_list):
            sm_list[ksm]()
            ksm += 1
    while ksm < len(sm_list):
        sm_list[ksm]()
        ksm += 1
    uh.free(); woS.free()
    for bb in ssms:
        bb.free()
    ATT = B("ATT", [128, 8, TOK], BF16)
    kta.append(B("kta1", [128, KV], BF16))
    ktb.append(B("ktb1", [128, KV], BF16))
    qta.append(B("qta1", [128, TOK], BF16))
    qtb.append(B("qtb1", [128, TOK], BF16))
    v2.append(B("v2_1", [128, 32, 130], BF16))
    ptbs = [B(f"ptb{i}", [128, 4, 512], BF16) for i in range(2)]
    tmpas = [B(f"tmpa{i}", [65, 512], F32) for i in range(2)]
    rz0s = [B(f"rz0{i}", [128, 512], F32) for i in range(2)]
    nrmb = B("nrmb", [64, 512], BF16)
    init_kqv(1)
    pending = []
    for hp in range(8):
        if hp < 7:
            steps = proj_steps(hp + 1) + v_steps(hp + 1) + norm_steps(hp + 1)
        else:
            NT.free()
            steps = make_p0_steps()
        len0 = len(steps)
        for qb in range(4):
            st_pv_qb2(hp, qb, steps, pending, len0)
        for stp in steps:
            stp()
    for pb in pending:
        pb()

    for bb in kta + ktb + qta + qtb + v2 + vts + wq2 + wk2 + wv2 + ptbs + tmpas + rz0s:
        bb.free()
    nrmb.free()
    qstg.free(); nsc.free(); nsq.free()

    ACC2 = [B(f"ACC{i}", [128, NS // 2, D], F32) for i in range(2)]

    def ACCa(s):
        return ACC2[s // 8].ap[:, s % 8, :]

    def ACCs(s):
        return ACC2[s // 8].spec3(s % 8, 0, D)
    MIT = B("MIT", [128, IC, TOK], BF16)
    woutS = B("woutS", [128, 8, D], BF16)
    ln2g = B("ln2g", [128, D], F32)
    ln2b = B("ln2b", [128, D], F32)
    ls1t = B("ls1t", [128, D], F32)
    ls2t = B("ls2t", [128, D], F32)
    dma("qpool", woutS.ap, din["Wout"].rearrange("(g p) n -> p g n", p=128),
        w=[woutS.spec()], tag="wout")
    bcast_load("qsp", ln2g, din["ln2_g"])
    bcast_load("qsp", ln2b, din["ln2_b"])
    bcast_load("qsp", ls1t, din["ls1"])
    bcast_load("qsp", ls2t, din["ls2"])
    gr = [B(f"gr{i}", [128, D], F32) for i in range(2)]
    sr = [B(f"sr{i}", [128, D], F32) for i in range(2)]
    nr = [B(f"nr{i}", [128, D], F32) for i in range(2)]
    agbs = [B(f"agb{i}", [128, D], F32) for i in range(2)]
    mints = [B(f"mint{i}", [128, D], BF16) for i in range(2)]
    prbs = [B(f"prb{i}", [128, NE], F32) for i in range(2)]
    t8bs = [B(f"t8b{i}", [128, NE], F32) for i in range(2)]
    mskbs = [B(f"mskb{i}", [128, NE], F32) for i in range(2)]

    def p3_tile_stages(s):
        tl = s * 128
        par = s % 2
        gb, sb_, nb = gr[par], sr[par], nr[par]
        agb = agbs[par]
        mint = mints[par]
        prb, t8b, mskb = prbs[par], t8bs[par], mskbs[par]
        apb = par * 2
        g0 = 40 + 6 * par

        def gp(a, b2):
            return stt.ap[:, g0 + a:g0 + b2], stt.spec(g0 + a, g0 + b2)

        def st_loads():
            dma("qact", gb.ap, sp_gate[tl:tl + 128, :], r=[("hb_gate", tl, tl + 128)],
                w=[gb.spec()], tag=f"grl_{s}")
            dma("qsp", sb_.ap, sp_ssm[tl:tl + 128, :], r=[("hb_ssm", tl, tl + 128)],
                w=[sb_.spec()], tag=f"srl_{s}")
            dma("qpool", nb.ap, sp_n0[tl:tl + 128, :], r=[("hb_n0", tl, tl + 128)],
                w=[nb.spec()], tag=f"nrl_{s}")

        def st_ap():
            for nh in range(2):
                bank = apb + nh
                for cc in range(8):
                    mm(psf_bank(bank), ATT.ap[:, cc, tl:tl + 128],
                       woutS.ap[:, cc, nh * 512:(nh + 1) * 512], cc == 0, cc == 7,
                       r=[ATT.spec3(cc, tl, tl + 128),
                          woutS.spec3(cc, nh * 512, (nh + 1) * 512)],
                       w=[ps_spec(bank)], tag=f"ap_{s}_{nh}")

        stages = [st_loads, st_ap]
        stages.append(lambda: dve(lambda e, g=gb, p=apb, a=agb: e.tensor_tensor(
            a.ap, PSF.ap()[:, p * 512:(p + 2) * 512], g.ap, op=ALU.mult),
            r=[ps_spec(apb, apb + 2), gb.spec()], w=[agb.spec()], tag=f"ag_{s}"))
        stages.append(lambda: dve(lambda e, b=sb_, a=agb: e.tensor_tensor(
            a.ap, a.ap, b.ap, op=ALU.subtract),
            r=[agb.spec(), sb_.spec()], w=[agb.spec()], tag=f"d_{s}"))
        stages.append(lambda: dve(lambda e, b=sb_, ss=s, a=agb: e.scalar_tensor_tensor(
            a.ap, a.ap, alpha.ap[:, ss:ss + 1], b.ap, op0=ALU.mult, op1=ALU.add),
            r=[agb.spec(), alpha.spec(s, s + 1), sb_.spec()], w=[agb.spec()],
            tag=f"bl_{s}"))
        stages.append(lambda: dve(lambda e, a=agb: e.tensor_tensor(
            a.ap, a.ap, ls1t.ap, op=ALU.mult),
            r=[agb.spec(), ls1t.spec()], w=[agb.spec()], tag=f"ls_{s}"))
        stages.append(lambda: dve(lambda e, n=nb, ss=s, a=agb: e.tensor_tensor(
            ACCa(ss), a.ap, n.ap, op=ALU.add),
            r=[agb.spec(), nb.spec()], w=[ACCs(s)], tag=f"acc_{s}"))

        c = 8 * par
        s6 = 16 * par

        def sc(a, b2):
            return stt.ap[:, c + a:c + b2], stt.spec(c + a, c + b2)
        stages.append(lambda: dve(lambda e: e.bn_stats(st6.ap[:, s6:s6 + 6],
                                                       ACCa(s)[:, 0:512]),
                                  r=[ACCs(s)], w=[st6.spec(s6, s6 + 6)], tag=f"mi_{s}b0"))
        stages.append(lambda: dve(lambda e: e.bn_stats(st6.ap[:, s6 + 6:s6 + 12],
                                                       ACCa(s)[:, 512:1024]),
                                  r=[ACCs(s)], w=[st6.spec(s6 + 6, s6 + 12)], tag=f"mi_{s}b1"))
        stages.append(lambda: dve(lambda e: e.bn_aggr(sc(4, 6)[0], st6.ap[:, s6:s6 + 12]),
                                  r=[st6.spec(s6, s6 + 12)], w=[sc(4, 6)[1]], tag=f"mi_{s}ag"))
        stages.append(lambda: act(AF.Sqrt, sc(2, 3)[0], sc(5, 6)[0], r=[sc(5, 6)[1]],
                                  w=[sc(2, 3)[1]], bias=float(EPS), tag=f"mi_{s}sd"))
        stages.append(lambda: dve(lambda e: e.reciprocal(sc(1, 2)[0], sc(2, 3)[0]),
                                  r=[sc(2, 3)[1]], w=[sc(1, 2)[1]], tag=f"mi_{s}rs"))
        stages.append(lambda: dve(lambda e: e.tensor_scalar(
            agb.ap, ACCa(s), sc(4, 5)[0], sc(1, 2)[0],
            op0=ALU.subtract, op1=ALU.mult),
            r=[ACCs(s), sc(4, 5)[1], sc(1, 2)[1]], w=[agb.spec()], tag=f"mi_{s}ap"))
        stages.append(lambda: dve(lambda e: e.tensor_tensor(agb.ap, agb.ap, ln2g.ap,
                                                            op=ALU.mult),
                                  r=[agb.spec(), ln2g.spec()], w=[agb.spec()], tag=f"mi_{s}gm"))
        stages.append(lambda: dve(lambda e, m=mint: e.tensor_tensor(m.ap, agb.ap, ln2b.ap,
                                                                    op=ALU.add),
                                  r=[agb.spec(), ln2b.spec()], w=[mint.spec()], tag=f"mi_{s}ba"))

        def st_tr():
            for ic in range(IC):
                tb_ = 6 + ic % 2
                transp(psbb(tb_, 0, 128, 0, 128), mint.ap[:, ic * 128:(ic + 1) * 128],
                       r=[mint.spec(ic * 128, (ic + 1) * 128)], w=[ps_spec(tb_)],
                       tag=f"mt_{s}_{ic}")
                dve(lambda e, i=ic, t=tl, b=tb_: e.tensor_copy(
                    MIT.ap[:, i, t:t + 128], psbb(b, 0, 128, 0, 128)),
                    r=[ps_spec(tb_)], w=[MIT.spec3(ic, tl, tl + 128)], tag=f"me_{s}_{ic}")
        stages.append(st_tr)

        wgb = 4 + par

        def st_wg():
            for i in range(IC):
                mm(psf_bank(wgb, 0, 128, 0, NE), MIT.ap[:, i, tl:tl + 128],
                   wg_s.ap[:, i, :], i == 0, i == IC - 1,
                   r=[MIT.spec3(i, tl, tl + 128), wg_s.spec3(i, 0, NE)],
                   w=[ps_spec(wgb)], tag=f"wg_{s}")
        stages.append(st_wg)
        stages.append(lambda: dve(lambda e: e.reduce_max(
            gp(0, 1)[0], psf_bank(wgb, 0, 128, 0, NE), axis=AX.X, negate=True),
            r=[ps_spec(wgb)], w=[gp(0, 1)[1]], tag=f"gm_{s}"))
        stages.append(lambda: P.op("act", lambda e: e.activation(
            prb.ap, psf_bank(wgb, 0, 128, 0, NE), AF.Exp, bias=gp(0, 1)[0],
            accum_out=gp(1, 2)[0]),
            r=[ps_spec(wgb), gp(0, 1)[1]], w=[prb.spec(), gp(1, 2)[1]], tag=f"ge_{s}"))
        stages.append(lambda: dve(lambda e: e.reciprocal(gp(2, 3)[0], gp(1, 2)[0]),
                                  r=[gp(1, 2)[1]], w=[gp(2, 3)[1]], tag=f"gr_{s}"))
        stages.append(lambda: dve(lambda e: e.tensor_scalar(
            prb.ap, prb.ap, gp(2, 3)[0], None, op0=ALU.mult),
            r=[prb.spec(), gp(2, 3)[1]], w=[prb.spec()], tag=f"gn_{s}"))
        stages.append(lambda: dve(lambda e: e.max(t8b.ap, prb.ap),
                                  r=[prb.spec()], w=[t8b.spec()], tag=f"g8_{s}"))
        stages.append(lambda: dve(lambda e: e.tensor_tensor(
            gp(3, 4)[0], t8b.ap[:, 0:1], t8b.ap[:, 1:2], op=ALU.add),
            r=[t8b.spec()], w=[gp(3, 4)[1]], tag=f"gd_{s}"))
        stages.append(lambda: dve(lambda e: e.tensor_scalar(
            gp(3, 4)[0], gp(3, 4)[0], 1e-8, None, op0=ALU.add),
            r=[gp(3, 4)[1]], w=[gp(3, 4)[1]], tag=f"gd2_{s}"))
        stages.append(lambda: dve(lambda e: e.reciprocal(gp(4, 5)[0], gp(3, 4)[0]),
                                  r=[gp(3, 4)[1]], w=[gp(4, 5)[1]], tag=f"gd3_{s}"))
        stages.append(lambda: dve(lambda e: e.tensor_scalar(
            mskb.ap, prb.ap, t8b.ap[:, 1:2], None, op0=ALU.is_ge),
            r=[prb.spec(), t8b.spec()], w=[mskb.spec()], tag=f"gk_{s}"))
        stages.append(lambda: dve(lambda e, ss=s: e.scalar_tensor_tensor(
            cwb.ap[:, ss, :], prb.ap, gp(4, 5)[0], mskb.ap,
            op0=ALU.mult, op1=ALU.mult),
            r=[prb.spec(), gp(4, 5)[1], mskb.spec()],
            w=[cwb.spec3(s, 0, NE)], tag=f"gc_{s}"))
        return stages

    for t_ in range(8):
        pair = [p3_tile_stages(2 * t_), p3_tile_stages(2 * t_ + 1)]
        for k in range(len(pair[0])):
            pair[0][k]()
            pair[1][k]()

    ATT.free(); woutS.free()
    for bb in gr + sr + agbs + mints + prbs + t8bs + mskbs:
        bb.free()
    ln2g.free(); ln2b.free(); ls1t.free()
    for bb in nr:
        bb.free()

    HT = B("HT", [128, 16, 1024], BF16)
    w2q = [B(f"w2q{i}", [128, 4, D], BF16) for i in range(4)]
    w1r = [B(f"w1r{i}", [128, 8, 256], BF16) for i in range(2)]
    w3r = [B(f"w3r{i}", [128, 8, 256], BF16) for i in range(2)]
    s1bs = [B(f"s1b{i}", [128, 512], F32) for i in range(2)]
    mtmp = B("mtmp", [128, D], F32)

    def moe_fc(e_, tb):
        def w2_load(ee=e_):
            for qq in range(4):
                dma("qpool", w2q[qq].ap,
                    din["W2"][ee][qq * 512:(qq + 1) * 512]
                    .rearrange("(g p) n -> p g n", p=128),
                    w=[w2q[qq].spec()], tag=f"w2_{ee}_{qq}")
        if True:
            tq0 = tb * 1024
            for fc in range(16):
                f0 = fc * 128
                gidx = ((tb * 16 + fc) // 2) % 2
                wb1, wb3 = w1r[gidx], w3r[gidx]
                off = (fc % 2) * 128
                if fc % 2 == 0:
                    dma("qpool", wb1.ap, din["W1"][e_][:, f0:f0 + 256]
                        .rearrange("(g p) n -> p g n", p=128), w=[wb1.spec()],
                        tag=f"w1_{e_}_{tb}_{fc}")
                    dma("qpool", wb3.ap, din["W3"][e_][:, f0:f0 + 256]
                        .rearrange("(g p) n -> p g n", p=128), w=[wb3.spec()],
                        tag=f"w3_{e_}_{tb}_{fc}")
                if tb == 0 and fc == 2:
                    w2_load()
                for tq in range(2):
                    t0 = tq0 + tq * 512
                    ba = tq % 2
                    for i in range(IC):
                        mm(psf_bank(ba * 2), wb1.ap[:, i, off:off + 128],
                           MIT.ap[:, i, t0:t0 + 512], i == 0, i == IC - 1,
                           r=[wb1.spec3(i, off, off + 128),
                              MIT.spec3(i, t0, t0 + 512)],
                           w=[ps_spec(ba * 2)], tag=f"a1_{e_}_{tb}_{fc}_{tq}")
                    for i in range(IC):
                        mm(psf_bank(ba * 2 + 1), wb3.ap[:, i, off:off + 128],
                           MIT.ap[:, i, t0:t0 + 512], i == 0, i == IC - 1,
                           r=[wb3.spec3(i, off, off + 128),
                              MIT.spec3(i, t0, t0 + 512)],
                           w=[ps_spec(ba * 2 + 1)], tag=f"a3_{e_}_{tb}_{fc}_{tq}")
                    s1x = s1bs[tq]
                    act(AF.Silu, s1x.ap, psf_bank(ba * 2),
                        r=[ps_spec(ba * 2)], w=[s1x.spec()],
                        tag=f"hs_{e_}_{tb}_{fc}_{tq}")
                    dve(lambda e, f=fc, t=tq * 512, b=ba, sx=s1x: e.tensor_tensor(
                        HT.ap[:, f, t:t + 512], sx.ap, psf_bank(b * 2 + 1),
                        op=ALU.mult),
                        r=[s1x.spec(), ps_spec(ba * 2 + 1)],
                        w=[HT.spec3(fc, tq * 512, tq * 512 + 512)],
                        tag=f"hm_{e_}_{tb}_{fc}_{tq}")
    def moe_out(e_, tb, pulls=None):
        if True:
            for j8 in range(8):
                s = tb * 8 + j8
                tl = j8 * 128
                ob_ = 4 if (e_ == NE - 1 and tb == 1) else 4 + (s % 2) * 2
                for nh in range(2):
                    for fc in range(16):
                        w2x = w2q[fc // 4]
                        mm(psf_bank(ob_ + nh), HT.ap[:, fc, tl:tl + 128],
                           w2x.ap[:, fc % 4, nh * 512:(nh + 1) * 512], fc == 0, fc == 15,
                           r=[HT.spec3(fc, tl, tl + 128),
                              w2x.spec3(fc % 4, nh * 512, (nh + 1) * 512)],
                           w=[ps_spec(ob_ + nh)], tag=f"o2_{e_}_{s}_{nh}")
                dve(lambda e, ss=s, ee=e_, p=ob_: e.scalar_tensor_tensor(
                    mtmp.ap, PSF.ap()[:, p * 512:(p + 2) * 512], cwb.ap[:, ss, ee:ee + 1],
                    ls2t.ap, op0=ALU.mult, op1=ALU.mult),
                    r=[ps_spec(ob_, ob_ + 2), cwb.spec3(s, e_, e_ + 1), ls2t.spec()],
                    w=[mtmp.spec()], tag=f"oc_{e_}_{s}")
                dve(lambda e, ss=s: e.tensor_tensor(ACCa(ss), ACCa(ss),
                                                    mtmp.ap, op=ALU.add),
                    r=[mtmp.spec(), ACCs(s)], w=[ACCs(s)],
                    tag=f"oa_{e_}_{s}")
                if pulls:
                    pulls.pop(0)()

    for e_ in range(NE - 1):
        moe_fc(e_, 0)
        moe_out(e_, 0)
        moe_fc(e_, 1)
        moe_out(e_, 1)
    moe_fc(NE - 1, 0)
    moe_out(NE - 1, 0)
    moe_fc(NE - 1, 1)
    MIT.free()
    for bb in w1r + w3r:
        bb.free()

    N1T = B("N1T", [128, IC, TOK], BF16)
    n0r2 = [B(f"n0q{i}", [128, D], F32) for i in range(2)]
    x0r2 = [B(f"x0q{i}", [128, D], F32) for i in range(2)]
    etmps = [B(f"etmp_{i}", [128, D], F32) for i in range(2)]
    n1bs = [B(f"n1b_{i}", [128, D], BF16) for i in range(2)]

    def lns(x_ap, x_spec, out_ap, out_spec, slot, tag):
        c = 8 * slot
        s6 = 16 * slot

        def sc(a, b2):
            return stt.ap[:, c + a:c + b2], stt.spec(c + a, c + b2)
        return [
            lambda: dve(lambda e: e.bn_stats(st6.ap[:, s6:s6 + 6], x_ap[:, 0:512]),
                        r=[x_spec], w=[st6.spec(s6, s6 + 6)], tag=tag + "b0"),
            lambda: dve(lambda e: e.bn_stats(st6.ap[:, s6 + 6:s6 + 12], x_ap[:, 512:1024]),
                        r=[x_spec], w=[st6.spec(s6 + 6, s6 + 12)], tag=tag + "b1"),
            lambda: dve(lambda e: e.bn_aggr(sc(4, 6)[0], st6.ap[:, s6:s6 + 12]),
                        r=[st6.spec(s6, s6 + 12)], w=[sc(4, 6)[1]], tag=tag + "ag"),
            lambda: act(AF.Sqrt, sc(2, 3)[0], sc(5, 6)[0], r=[sc(5, 6)[1]],
                        w=[sc(2, 3)[1]], bias=float(EPS), tag=tag + "sd"),
            lambda: dve(lambda e: e.reciprocal(sc(1, 2)[0], sc(2, 3)[0]),
                        r=[sc(2, 3)[1]], w=[sc(1, 2)[1]], tag=tag + "rs"),
            lambda: dve(lambda e: e.tensor_scalar(out_ap, x_ap, sc(4, 5)[0], sc(1, 2)[0],
                                                  op0=ALU.subtract, op1=ALU.mult),
                        r=[x_spec, sc(4, 5)[1], sc(1, 2)[1]], w=[out_spec], tag=tag + "ap"),
        ]

    def p6_combine_stages(s):
        tl = s * 128
        par = (s + s // 8) % 2
        nb, xb = n0r2[par], x0r2[par]
        etmp, n1b = etmps[par], n1bs[par]

        def st_loads():
            dma("qpool", nb.ap, sp_n0[tl:tl + 128, :], r=[("hb_n0", tl, tl + 128)],
                w=[nb.spec()], tag=f"n0q_{s}")
            dma("qsp", xb.ap, din["x0"][tl:tl + 128, :], w=[xb.spec()], tag=f"x0q_{s}")
        stages = [st_loads]
        stages.append(lambda: dve(lambda e, ss=s, n=nb, t_=etmp: e.scalar_tensor_tensor(
            t_.ap, ACCa(ss), wsum.ap[:, ss:ss + 1], n.ap,
            op0=ALU.mult, op1=ALU.add),
            r=[ACCs(s), wsum.spec(s, s + 1), nb.spec()],
            w=[etmp.spec()], tag=f"bo_{s}"))
        stages += lns(etmp.ap, etmp.spec(), nb.ap, nb.spec(), par, f"l1_{s}")
        stages.append(lambda: dve(lambda e, ss=s, x=xb, t_=nb: e.tensor_tensor(
            ACCa(ss), t_.ap, x.ap, op=ALU.add),
            r=[nb.spec(), xb.spec()], w=[ACCs(s)], tag=f"x1_{s}"))
        stages += lns(ACCa(s), ACCs(s), n1b.ap, n1b.spec(), 2 + par, f"l2_{s}")

        def st_tr():
            for ic in range(IC):
                tb_ = 6 + ic % 2
                transp(psbb(tb_, 0, 128, 0, 128), n1b.ap[:, ic * 128:(ic + 1) * 128],
                       r=[n1b.spec(ic * 128, (ic + 1) * 128)], w=[ps_spec(tb_)],
                       tag=f"n1t_{s}_{ic}")
                dve(lambda e, i=ic, t=tl, b=tb_: e.tensor_copy(
                    N1T.ap[:, i, t:t + 128], psbb(b, 0, 128, 0, 128)),
                    r=[ps_spec(tb_)], w=[N1T.spec3(ic, tl, tl + 128)], tag=f"n1e_{s}_{ic}")
        stages.append(st_tr)
        return stages

    def p6_combine_pair(s0, s1):
        ca_, cb2 = p6_combine_stages(s0), p6_combine_stages(s1)
        for k in range(len(ca_)):
            ca_[k]()
            cb2[k]()

    moe_out(NE - 1, 1, pulls=[(lambda j8=j8: p6_combine_pair(j8, 8 + j8))
                              for j8 in range(8)])

    HT.free(); mtmp.free()
    for bb in w2q:
        bb.free()
    for bb in s1bs:
        bb.free()

    wtRb = B("wtRb", [128, 4, 512], BF16)
    wtIb = B("wtIb", [128, 4, 512], BF16)
    wtInb = B("wtInb", [128, 4, 512], BF16)
    biasRt = B("biasRt", [128, 512], F32)
    biasIt = B("biasIt", [128, 512], F32)
    bcast_load("qsp", biasRt, din["bias_r"])
    bcast_load("qsp", biasIt, din["bias_i"])
    dma("qsp", wtRb.ap, wt_scr[0], r=[("hb_wt", 0, 1)], w=[wtRb.spec()], tag="wtRl")
    dma("qsp", wtIb.ap, wt_scr[1], r=[("hb_wt", 1, 2)], w=[wtIb.spec()], tag="wtIl")
    dve(lambda e: e.tensor_scalar(wtInb.ap, wtIb.ap, -1.0, None, op0=ALU.mult),
        r=[wtIb.spec()], w=[wtInb.spec()], tag="wtInb")

    ffss = [B(f"ffs_{i}", [128, D], F32) for i in range(3)]
    fts = [B(f"fts_{i}", [128, D], F32) for i in range(3)]
    outs = [B(f"outs{i}", [128, D], F32) for i in range(3)]

    def p6_ffn_stages(s, wide=2):
        tl = s * 128
        etmp, ffs = fts[s % wide], ffss[s % wide]
        fb0 = (s % wide) * 2

        def st_fr():
            for k, wt in ((0, wtRb), (4, wtInb)):
                for i4 in range(4):
                    mm(psf_bank(fb0), N1T.ap[:, k // 4 * 4 + i4, tl:tl + 128],
                       wt.ap[:, i4, :], (k + i4) == 0, (k + i4) == 7,
                       r=[N1T.spec3(k // 4 * 4 + i4, tl, tl + 128), wt.spec3(i4, 0, 512)],
                       w=[ps_spec(fb0)], tag=f"fr_{s}_{k}_{i4}")

        def st_fi():
            for k, wt in ((0, wtIb), (4, wtRb)):
                for i4 in range(4):
                    mm(psf_bank(fb0 + 1), N1T.ap[:, k // 4 * 4 + i4, tl:tl + 128],
                       wt.ap[:, i4, :], (k + i4) == 0, (k + i4) == 7,
                       r=[N1T.spec3(k // 4 * 4 + i4, tl, tl + 128), wt.spec3(i4, 0, 512)],
                       w=[ps_spec(fb0 + 1)], tag=f"fi_{s}_{k}_{i4}")
        stages = [st_fr, st_fi]
        stages.append(lambda: dve(lambda e, p=fb0, f_=ffs: e.tensor_tensor(
            f_.ap[:, 0:512], psf_bank(p), biasRt.ap, op=ALU.add),
            r=[ps_spec(fb0), biasRt.spec()], w=[ffs.spec(0, 512)], tag=f"fb_{s}"))
        stages.append(lambda: dve(lambda e, p=fb0, f_=ffs: e.tensor_tensor(
            f_.ap[:, 512:1024], psf_bank(p + 1), biasIt.ap, op=ALU.add),
            r=[ps_spec(fb0 + 1), biasIt.spec()], w=[ffs.spec(512, 1024)], tag=f"fb2_{s}"))
        stages += lns(ffs.ap, ffs.spec(), etmp.ap, etmp.spec(),
                      (4 + s % 2) if wide == 2 else (3 + s % 3), f"l3_{s}")

        def st_out():
            ob = outs[s % 3]
            dve(lambda e, ss=s, o=ob, t_=etmp: e.tensor_tensor(
                o.ap, t_.ap, ACCa(ss), op=ALU.add),
                r=[etmp.spec(), ACCs(s)], w=[ob.spec()], tag=f"o_{s}")
            dma("qact", out_d[tl:tl + 128, :], ob.ap, r=[ob.spec()],
                w=[("hb_out", tl, tl + 128)], tag=f"od_{s}")
        stages.append(st_out)
        return stages

    def p6_ffn_pair(s0, s1):
        fa, fb_ = p6_ffn_stages(s0), p6_ffn_stages(s1)
        for k in range(len(fa)):
            fa[k]()
            fb_[k]()

    def p6_ffn_multi(ss_list):
        chains = [p6_ffn_stages(s_, 3) for s_ in ss_list]
        for k in range(len(chains[0])):
            for ch in chains:
                ch[k]()

    for g0 in range(0, 16, 3):
        p6_ffn_multi(list(range(g0, min(g0 + 3, 16))))

    P.finalize()
    with nc.Block("init") as blk:
        @blk.gpsimd
        def _(e):
            for s in sems.values():
                e.sem_clear(s)
    with nc.Block("main") as blk:
        @blk.tensor
        def _(e):
            P.emit_engine("pe", e, sems)

        @blk.scalar
        def _(e):
            P.emit_engine("act", e, sems)
            for s, v in P.slot_total.items():
                if s.startswith("qact"):
                    e.wait_ge(sems[s], v)

        @blk.vector
        def _(e):
            P.emit_engine("dve", e, sems)

        @blk.gpsimd
        def _(e):
            P.emit_engine("pool", e, sems)
            for s, v in P.slot_total.items():
                if s.startswith("qpool"):
                    e.wait_ge(sems[s], v)

        @blk.sync
        def _(e):
            P.emit_engine("sp", e, sems)
            for s, v in P.slot_total.items():
                e.wait_ge(sems[s], v)
    st.close()
    return nc


def host_constants():
    ident = np.eye(128, dtype=BFDT)
    qsel = np.kron(np.eye(32, dtype=np.float32), np.ones((1, 16), np.float32))
    pselt = np.zeros((32, 512), np.float32)
    for t in range(4):
        blk = np.kron(np.eye(32, dtype=np.float32)[:, 8 * t:8 * t + 8],
                      np.ones((1, 16), np.float32))
        pselt[:, t * 128:(t + 1) * 128] = blk
    rselt = np.concatenate([np.eye(16, dtype=np.float32)] * 8, axis=1)
    ssel = np.tile(np.eye(16, dtype=np.float32), (1, 32)).astype(np.float32)
    return ident, qsel, pselt, rselt, ssel


_NC_CACHE = {}


def kernel(**inputs):
    x = np.ascontiguousarray(np.asarray(inputs["L0_attn_x"], dtype=np.float32))
    ident, qsel, pselt, rselt, ssel = host_constants()
    wmap = {
        "Wr": "L0_attn_Wr", "br": "L0_attn_br", "step_bias": "L0_attn_step_bias",
        "ln1_g": "L0_attn_ln1_g", "ln1_b": "L0_attn_ln1_b",
        "ln2_g": "L0_attn_ln2_g", "ln2_b": "L0_attn_ln2_b",
        "ls1": "L0_attn_ls1", "ls2": "L0_attn_ls2",
        "Wqkv": "L0_attn_Wqkv", "Wout": "L0_attn_Wout", "Wgate": "L0_attn_Wgate",
        "Win": "L0_attn_Win_ssm", "conv_w": "L0_attn_conv_w", "conv_b": "L0_attn_conv_b",
        "Wout_ssm": "L0_attn_Wout_ssm", "Wg": "L0_attn_Wg",
        "W1": "L0_attn_W1", "W2": "L0_attn_W2", "W3": "L0_attn_W3",
        "G1R": "L0_ffn_G1_R", "G2R": "L0_ffn_G2_R",
        "G1I": "L0_ffn_G1_I", "G2I": "L0_ffn_G2_I",
        "bias_r": "L0_ffn_bias_real", "bias_i": "L0_ffn_bias_imag",
    }
    weights = {k: np.ascontiguousarray(np.asarray(inputs[v], dtype=np.float32))
               for k, v in wmap.items()}
    in_maps = []
    for c in range(8):
        b, h = c // 2, c % 2
        x0 = np.ascontiguousarray(x[b, h * TOK:(h + 1) * TOK])
        xsib = np.ascontiguousarray(x[b, (1 - h) * TOK:(2 - h) * TOK])
        if h == 1:
            xp3 = np.ascontiguousarray(x[b, TOK - 3:TOK])
            cf = np.ones((1, 1), np.float32)
        else:
            xp3 = np.zeros((3, D), np.float32)
            cf = np.zeros((1, 1), np.float32)
        m = {"x0": x0, "xsib": xsib, "xp3": xp3, "cflag": cf,
             "ident": ident, "qsel": qsel, "pselt": pselt, "rselt": rselt,
             "ssel": ssel}
        m.update(weights)
        in_maps.append(m)
    if "nc" not in _NC_CACHE:
        _NC_CACHE["nc"] = build_nc()
    nc = _NC_CACHE["nc"]
    res = run_bass_kernel_spmd(nc, in_maps, core_ids=list(range(8)))
    out = np.empty((4, KV, D), np.float32)
    for c in range(8):
        b, h = c // 2, c % 2
        out[b, h * TOK:(h + 1) * TOK] = np.asarray(res.results[c]["out"],
                                                   dtype=np.float32)
    return out


# version 42
# speedup vs baseline: 1.0907x; 1.0352x over previous
import contextlib
import numpy as np
import ml_dtypes
import concourse.bass as bass
import concourse.mybir as mybir
from concourse.bass_utils import run_bass_kernel_spmd

F32 = mybir.dt.float32
BF16 = mybir.dt.bfloat16
U8 = mybir.dt.uint8
AF = mybir.ActivationFunctionType
ALU = mybir.AluOpType
AX = mybir.AxisListType
BFDT = ml_dtypes.bfloat16

D = 1024
TOK = 2048
KV = 4096
NS = 16
IC = 8
H = 16
NE = 8
FF = 2048
EPS = 1e-5
SB_LIMIT = 207 * 1024

ENGINES = ("pe", "act", "dve", "pool", "sp")
QUEUES = {"qsp": "sp", "qact": "act", "qpool": "pool"}
NQSLOT = 8
QSLOT_NAMES = [f"{q}{i}" for q in QUEUES for i in range(NQSLOT)]
ISZ = {F32: 4, BF16: 2}


class Prog:

    def __init__(self):
        self.ops = []
        self.segs = {}

    def _access(self, space, lo, hi, opid, is_write):
        segs = self.segs.setdefault(space, [])
        deps = set()
        rest = []
        for seg in segs:
            slo, shi, w, rd = seg
            if shi <= lo or slo >= hi:
                rest.append(seg)
                continue
            if w >= 0:
                deps.add(w)
            if is_write:
                deps.update(rd)
            if slo < lo:
                rest.append([slo, lo, w, list(rd)])
            if shi > hi:
                rest.append([hi, shi, w, list(rd)])
            if not is_write:
                plo, phi = max(slo, lo), min(shi, hi)
                rest.append([plo, phi, w, rd + [opid]])
        if is_write:
            rest.append([lo, hi, opid, []])
        self.segs[space] = rest
        return deps

    def op(self, engine, emit, r=(), w=(), q=None, tag=""):
        i = len(self.ops)
        deps = set()
        for (sp, lo, hi) in r:
            deps |= self._access(sp, lo, hi, i, False)
        for (sp, lo, hi) in w:
            deps |= self._access(sp, lo, hi, i, True)
        deps.discard(i)
        self.ops.append({"eng": engine, "emit": emit, "deps": sorted(deps),
                         "q": q, "tag": tag, "event": None, "waits": None,
                         "prewait": None})
        return i

    def finalize(self):
        n = len(self.ops)
        needed = [False] * n
        for o in self.ops:
            for d in o["deps"]:
                if o["eng"] == "pe" and self.ops[d]["eng"] == "pe" \
                        and self.ops[d]["q"] is None and o["q"] is None:
                    continue
                needed[d] = True
        inc_count = {e: 0 for e in ENGINES}
        q_idx = {q: 0 for q in QUEUES}
        slot_last = {}
        for i, o in enumerate(self.ops):
            if o["q"] is not None:
                k = q_idx[o["q"]]
                q_idx[o["q"]] += 1
                slot, cyc = k % NQSLOT, k // NQSLOT
                sname = f"{o['q']}{slot}"
                o["event"] = (sname, 16 * (cyc + 1))
                o["prewait"] = (sname, 16 * cyc, slot_last.get(sname)) if cyc else None
                slot_last[sname] = i
            elif needed[i]:
                e = o["eng"]
                inc_count[e] += 1
                o["event"] = (e, inc_count[e])
        self.slot_total = {s: (self.ops[i]["event"][1]) for s, i in slot_last.items()}
        clock = {e: {} for e in ENGINES}
        opclock = [None] * n
        for i, o in enumerate(self.ops):
            E = o["eng"]
            ck = clock[E]
            waits = {}

            def need(s, v, src_op):
                if ck.get(s, 0) < v:
                    waits[s] = max(waits.get(s, 0), v)
                    ck[s] = v
                    dck = opclock[src_op] if src_op is not None else None
                    if dck:
                        for s2, v2 in dck.items():
                            if ck.get(s2, 0) < v2:
                                ck[s2] = v2

            pw = o.get("prewait")
            if pw:
                need(pw[0], pw[1], pw[2])
            for d in o["deps"]:
                dop = self.ops[d]
                if E == "pe" and dop["eng"] == "pe" and dop["q"] is None \
                        and o["q"] is None:
                    dck = opclock[d]
                    if dck:
                        for s2, v2 in dck.items():
                            if ck.get(s2, 0) < v2:
                                ck[s2] = v2
                    continue
                s, v = dop["event"]
                need(s, v, d)
            o["waits"] = sorted(waits.items())
            myck = dict(ck)
            if o["event"] is not None:
                s, v = o["event"]
                myck[s] = max(myck.get(s, 0), v)
            opclock[i] = myck

    def emit_engine(self, eng_name, e, sems):
        for o in self.ops:
            if o["eng"] != eng_name:
                continue
            for (s, v) in o["waits"]:
                e.wait_ge(sems[s], v)
            inst = o["emit"](e)
            if o["event"] is not None:
                s, _ = o["event"]
                inst.then_inc(sems[s], 16 if o["q"] is not None else 1)


class Alloc:
    def __init__(self, limit):
        self.free_segs = [(0, limit)]
        self.peak = 0

    def alloc(self, nbytes, align=64):
        nbytes = (nbytes + align - 1) // align * align
        for idx, (lo, hi) in enumerate(self.free_segs):
            lo_a = (lo + align - 1) // align * align
            if hi - lo_a >= nbytes:
                new = []
                if lo < lo_a:
                    new.append((lo, lo_a))
                if lo_a + nbytes < hi:
                    new.append((lo_a + nbytes, hi))
                self.free_segs[idx:idx + 1] = new
                self.peak = max(self.peak, lo_a + nbytes)
                return lo_a
        raise MemoryError(f"sbuf arena exhausted (want {nbytes})")

    def dealloc(self, off, nbytes, align=64):
        nbytes = (nbytes + align - 1) // align * align
        self.free_segs.append((off, off + nbytes))
        self.free_segs.sort()
        merged = []
        for seg in self.free_segs:
            if merged and merged[-1][1] >= seg[0]:
                merged[-1] = (merged[-1][0], max(merged[-1][1], seg[1]))
            else:
                merged.append(list(seg))
        self.free_segs = [tuple(s) for s in merged]


class Buf:
    def __init__(self, al, af32, abf, name, shape, dtype):
        self.name, self.shape, self.dtype = name, list(shape), dtype
        isz = ISZ[dtype]
        self.isz = isz
        per_part = int(np.prod(shape[1:])) * isz
        self.nbytes = per_part
        self.off = al.alloc(per_part)
        self.al = al
        base = af32 if dtype == F32 else abf
        flat = base.ap()[0:shape[0], self.off // isz: self.off // isz + int(np.prod(shape[1:]))]
        if len(shape) == 2:
            self.ap = flat
        elif len(shape) == 3:
            self.ap = flat.rearrange("p (a b) -> p a b", a=shape[1], b=shape[2])
        else:
            raise ValueError(shape)

    def spec(self, lo_elem=0, hi_elem=None):
        hi_elem = int(np.prod(self.shape[1:])) if hi_elem is None else hi_elem
        return ("sb", self.off + lo_elem * self.isz, self.off + hi_elem * self.isz)

    def spec3(self, a, e0, e1):
        inner = self.shape[2]
        return self.spec(a * inner + e0, a * inner + e1)

    def free(self):
        self.al.dealloc(self.off, self.nbytes)


def ps_spec(b0, b1=None):
    return ("ps", b0, (b0 + 1) if b1 is None else b1)


def build_nc():
    nc = bass.Bass("TRN2", target_bir_lowering=False)

    din = {}

    def dram_in(name, shape, dt=F32):
        din[name] = nc.dram_tensor(name, list(shape), dt, kind="ExternalInput").ap()

    dram_in("x0", [TOK, D]); dram_in("xsib", [TOK, D]); dram_in("xp3", [3, D])
    dram_in("cflag", [1, 1])
    dram_in("Wr", [D, 2]); dram_in("br", [2]); dram_in("step_bias", [3])
    dram_in("ln1_g", [D]); dram_in("ln1_b", [D]); dram_in("ln2_g", [D]); dram_in("ln2_b", [D])
    dram_in("ls1", [D]); dram_in("ls2", [D])
    dram_in("Wqkv", [D, 3 * D]); dram_in("Wout", [D, D]); dram_in("Wgate", [D, D])
    dram_in("Win", [D, 2 * D]); dram_in("conv_w", [D, 1, 4]); dram_in("conv_b", [D])
    dram_in("Wout_ssm", [D, D]); dram_in("Wg", [D, NE])
    dram_in("W1", [NE, D, FF]); dram_in("W2", [NE, FF, D]); dram_in("W3", [NE, D, FF])
    dram_in("G1R", [1, 32, 4, 32]); dram_in("G2R", [4, 16, 1, 16])
    dram_in("G1I", [1, 32, 4, 32]); dram_in("G2I", [4, 16, 1, 16])
    dram_in("bias_r", [512]); dram_in("bias_i", [512])
    dram_in("ident", [128, 128], BF16)
    dram_in("qsel", [32, 512]); dram_in("pselt", [32, 512])
    dram_in("rselt", [16, 128]); dram_in("ssel", [16, 512])

    out_d = nc.dram_tensor("out", [TOK, D], F32, kind="ExternalOutput").ap()
    sp_n0 = nc.dram_tensor("sp_n0", [TOK, D], F32, kind="Internal").ap()
    sp_gate = nc.dram_tensor("sp_gate", [TOK, D], F32, kind="Internal").ap()
    sp_ssm = nc.dram_tensor("sp_ssm", [TOK, D], F32, kind="Internal").ap()
    nm_scr = nc.dram_tensor("nm_scr", [H, 16, 128], BF16, kind="Internal").ap()
    wt_scr = nc.dram_tensor("wt_scr", [2, 128, 4, 512], BF16, kind="Internal").ap()
    sgz_scr = nc.dram_tensor("sgz_scr", [8, 128, TOK], BF16, kind="Internal").ap()

    st = contextlib.ExitStack()
    sems = {}
    for s in list(ENGINES) + QSLOT_NAMES:
        sems[s] = st.enter_context(nc.semaphore("sem_" + s))
    ARENA = st.enter_context(nc.sbuf_tensor("arena", [128, SB_LIMIT], U8))
    PSF = st.enter_context(nc.psum_tensor("psf", [128, 4096], F32))
    PSBV = PSF.bitcast(BF16)
    AF32 = ARENA.bitcast(F32)
    ABF = ARENA.bitcast(BF16)
    al = Alloc(SB_LIMIT)
    P = Prog()

    def B(name, shape, dtype):
        return Buf(al, AF32, ABF, name, shape, dtype)

    def psf_bank(b, p0=0, p1=128, c0=0, c1=512):
        return PSF.ap()[p0:p1, b * 512 + c0: b * 512 + c1]

    def psbb(b, p0=0, p1=128, c0=0, c1=1024):
        return PSBV.ap()[p0:p1, b * 1024 + c0: b * 1024 + c1]

    def dma(q, dst, src, r=(), w=(), tag="", slow=False):
        eng = QUEUES[q]
        if slow:
            def em(e, d=dst, s=src):
                return e.dma_start(d, s, allow_slow_non_contiguous=True)
        else:
            def em(e, d=dst, s=src):
                return e.dma_start(d, s)
        return P.op(eng, em, r=r, w=w, q=q, tag=tag)

    def bcast_load(q, dstbuf, src1d, tag="bc"):
        src = src1d.rearrange("(o n) -> o n", o=1).partition_broadcast(128)
        dma(q, dstbuf.ap, src, w=[dstbuf.spec()], tag=tag)

    def mm(out_ap, lhsT, rhs, start, stop, r, w, tag=""):
        return P.op("pe", lambda e, o=out_ap, l=lhsT, x=rhs, s=start, t=stop:
                    e.matmul(o, l, x, start=s, stop=t), r=r, w=w, tag=tag)

    const_pool = {}

    def get_const(v):
        if v not in const_pool:
            cb = Buf(al, AF32, ABF, f"const_{len(const_pool)}", [128, 1], F32)
            P.op("dve", lambda e, c=cb, vv=v: e.memset(c.ap, vv), r=[], w=[cb.spec()],
                 tag=f"cst_{v}")
            nc.const_aps.register(F32, float(v), cb.ap)
            const_pool[v] = cb
        return const_pool[v]

    def act(fn, out_ap, in_ap, r, w, bias=0.0, scale=1.0, accum=None, tag=""):
        rr = list(r)
        if isinstance(bias, float) and fn != AF.Copy:
            rr.append(get_const(bias).spec())
        ww = list(w) + ([] if accum is None else [])
        return P.op("act", lambda e, o=out_ap, i=in_ap, f=fn, b=bias, s=scale, a=accum:
                    e.activation(o, i, f, bias=b, scale=s, accum_out=a), r=rr, w=ww, tag=tag)

    def dve(fn, r, w, tag=""):
        return P.op("dve", fn, r=r, w=w, tag=tag)

    identb = B("identb", [128, 128], BF16)
    ones_f = B("ones_f", [128, 64], F32)
    convw = B("convw", [128, 8, 4], F32)
    convb = B("convb", [128, 8], F32)
    wr_s = B("wr_s", [128, 8, 2], BF16)
    wg_s = B("wg_s", [128, 8, 8], BF16)
    brt = B("brt", [128, 2], F32)
    sbt = B("sbt", [128, 3], F32)
    b1st = B("b1st", [128, 3], F32)
    cft = B("cft", [128, 1], F32)
    alpha = B("alpha", [128, 16], F32)
    wsum = B("wsum", [128, 16], F32)
    cwb = B("cwb", [128, 16, 8], F32)
    stt = B("stt", [128, 64], F32)
    st6 = B("st6", [128, 96], F32)
    hlt = B("hlt", [128, 3], F32)
    np3 = B("np3", [128, 8, 3], BF16)
    onesb = B("onesb", [128, 1], BF16)
    P.op("dve", lambda e: e.memset(onesb.ap, 1.0), r=[], w=[onesb.spec()], tag="onesb")
    for v_ in (0.0, float(EPS), float(EPS * EPS)):
        get_const(v_)

    def transp(out_ap, in_ap, r, w, tag="tr"):
        kdim = in_ap.shape[0]
        return P.op("pe", lambda e, o=out_ap, i=in_ap, d=identb.ap[0:kdim, 0:kdim]:
                    e.transpose(o, i, d), r=list(r) + [identb.spec()], w=w, tag=tag)

    dma("qsp", identb.ap, din["ident"], w=[identb.spec()], tag="ident")
    P.op("dve", lambda e: e.memset(ones_f.ap, 1.0), w=[ones_f.spec()], tag="ones")
    dma("qsp", convw.ap, din["conv_w"].rearrange("(g p) o k -> p g (o k)", p=128),
        w=[convw.spec()], tag="convw")
    dma("qsp", convb.ap, din["conv_b"].rearrange("(g p) -> p g", p=128),
        w=[convb.spec()], tag="convb", slow=True)
    dma("qpool", wr_s.ap, din["Wr"].rearrange("(g p) n -> p g n", p=128),
        w=[wr_s.spec()], tag="wr")
    dma("qpool", wg_s.ap, din["Wg"].rearrange("(g p) n -> p g n", p=128),
        w=[wg_s.spec()], tag="wgs")
    bcast_load("qsp", brt, din["br"])
    bcast_load("qsp", sbt, din["step_bias"])
    dma("qsp", cft.ap, din["cflag"].partition_broadcast(128), w=[cft.spec()], tag="cflag")
    dve(lambda e: e.tensor_scalar(b1st.ap, sbt.ap, brt.ap[:, 1:2], None, op0=ALU.add),
        r=[sbt.spec(), brt.spec()], w=[b1st.spec()], tag="b1st")

    def layer_norm(x_ap, x_spec, out_ap, out_spec, g=None, pdim=128, tag="ln", slot=0, aff_eng="dve"):
        sl = slice(0, pdim)
        c = 8 * slot
        s6 = 16 * slot

        def sc(a, b2):
            return stt.ap[sl, c + a:c + b2], stt.spec(c + a, c + b2)

        dve(lambda e: e.bn_stats(st6.ap[sl, s6:s6 + 6], x_ap[:, 0:512]),
            r=[x_spec], w=[st6.spec(s6, s6 + 6)], tag=tag + "b0")
        dve(lambda e: e.bn_stats(st6.ap[sl, s6 + 6:s6 + 12], x_ap[:, 512:1024]),
            r=[x_spec], w=[st6.spec(s6 + 6, s6 + 12)], tag=tag + "b1")
        dve(lambda e: e.bn_aggr(sc(4, 6)[0], st6.ap[sl, s6:s6 + 12]),
            r=[st6.spec(s6, s6 + 12)], w=[sc(4, 6)[1]], tag=tag + "ag")
        act(AF.Sqrt, sc(2, 3)[0], sc(5, 6)[0], r=[sc(5, 6)[1]],
            w=[sc(2, 3)[1]], bias=float(EPS), tag=tag + "sd")
        dve(lambda e: e.reciprocal(sc(1, 2)[0], sc(2, 3)[0]),
            r=[sc(2, 3)[1]], w=[sc(1, 2)[1]], tag=tag + "rs")
        if g is None:
            dve(lambda e: e.tensor_scalar(out_ap, x_ap, sc(4, 5)[0], sc(1, 2)[0],
                                          op0=ALU.subtract, op1=ALU.mult),
                r=[x_spec, sc(4, 5)[1], sc(1, 2)[1]], w=[out_spec], tag=tag + "ap")
        else:
            g_ap, g_spec, b_ap, b_spec, tmp_ap, tmp_spec = g
            dve(lambda e: e.tensor_scalar(tmp_ap, x_ap, sc(4, 5)[0], sc(1, 2)[0],
                                          op0=ALU.subtract, op1=ALU.mult),
                r=[x_spec, sc(4, 5)[1], sc(1, 2)[1]], w=[tmp_spec], tag=tag + "ap")
            P.op(aff_eng, lambda e: e.tensor_tensor(tmp_ap, tmp_ap, g_ap, op=ALU.mult),
                 r=[tmp_spec, g_spec], w=[tmp_spec], tag=tag + "gm")
            P.op(aff_eng, lambda e: e.tensor_tensor(out_ap, tmp_ap, b_ap, op=ALU.add),
                 r=[tmp_spec, b_spec], w=[out_spec], tag=tag + "ba")

    def layer_norm2_stages(x_ap, x_spec, n0_ap, n0_spec, nrm_ap, nrm_spec, g,
                           tag="ln2", slot=0):
        sl = slice(0, 128)
        c = 8 * slot
        s6 = 16 * slot

        def sc(a, b2):
            return stt.ap[sl, c + a:c + b2], stt.spec(c + a, c + b2)
        g_ap, g_spec, b_ap, b_spec, tmp_ap, tmp_spec = g
        return [
            lambda: dve(lambda e: e.bn_stats(st6.ap[sl, s6:s6 + 6], x_ap[:, 0:512]),
                        r=[x_spec], w=[st6.spec(s6, s6 + 6)], tag=tag + "b0"),
            lambda: dve(lambda e: e.bn_stats(st6.ap[sl, s6 + 6:s6 + 12], x_ap[:, 512:1024]),
                        r=[x_spec], w=[st6.spec(s6 + 6, s6 + 12)], tag=tag + "b1"),
            lambda: dve(lambda e: e.bn_aggr(sc(4, 6)[0], st6.ap[sl, s6:s6 + 12]),
                        r=[st6.spec(s6, s6 + 12)], w=[sc(4, 6)[1]], tag=tag + "ag"),
            lambda: act(AF.Sqrt, sc(2, 3)[0], sc(5, 6)[0], r=[sc(5, 6)[1]],
                        w=[sc(2, 3)[1]], bias=float(EPS), tag=tag + "sd"),
            lambda: act(AF.Sqrt, sc(3, 4)[0], sc(5, 6)[0], r=[sc(5, 6)[1]],
                        w=[sc(3, 4)[1]], bias=float(EPS * EPS), scale=float(1.0 + EPS),
                        tag=tag + "s2"),
            lambda: dve(lambda e: e.reciprocal(sc(1, 2)[0], sc(2, 3)[0]),
                        r=[sc(2, 3)[1]], w=[sc(1, 2)[1]], tag=tag + "rs"),
            lambda: dve(lambda e: e.reciprocal(sc(6, 7)[0], sc(3, 4)[0]),
                        r=[sc(3, 4)[1]], w=[sc(6, 7)[1]], tag=tag + "r2"),
            lambda: dve(lambda e: e.tensor_scalar(n0_ap, x_ap, sc(4, 5)[0], sc(1, 2)[0],
                                                  op0=ALU.subtract, op1=ALU.mult),
                        r=[x_spec, sc(4, 5)[1], sc(1, 2)[1]], w=[n0_spec], tag=tag + "ap"),
            lambda: dve(lambda e: e.tensor_scalar(tmp_ap, x_ap, sc(4, 5)[0], sc(6, 7)[0],
                                                  op0=ALU.subtract, op1=ALU.mult),
                        r=[x_spec, sc(4, 5)[1], sc(6, 7)[1]], w=[tmp_spec], tag=tag + "a2"),
            lambda: dve(lambda e: e.tensor_tensor(tmp_ap, tmp_ap, g_ap, op=ALU.mult),
                        r=[tmp_spec, g_spec], w=[tmp_spec], tag=tag + "gm"),
            lambda: dve(lambda e: e.tensor_tensor(nrm_ap, tmp_ap, b_ap, op=ALU.add),
                        r=[tmp_spec, b_spec], w=[nrm_spec], tag=tag + "ba"),
        ]

    def layer_norm2(x_ap, x_spec, n0_ap, n0_spec, nrm_ap, nrm_spec, g, pdim=128,
                    tag="ln2", slot=0, aff_eng="pool"):
        sl = slice(0, pdim)
        c = 8 * slot
        s6 = 16 * slot

        def sc(a, b2):
            return stt.ap[sl, c + a:c + b2], stt.spec(c + a, c + b2)

        dve(lambda e: e.bn_stats(st6.ap[sl, s6:s6 + 6], x_ap[:, 0:512]),
            r=[x_spec], w=[st6.spec(s6, s6 + 6)], tag=tag + "b0")
        dve(lambda e: e.bn_stats(st6.ap[sl, s6 + 6:s6 + 12], x_ap[:, 512:1024]),
            r=[x_spec], w=[st6.spec(s6 + 6, s6 + 12)], tag=tag + "b1")
        dve(lambda e: e.bn_aggr(sc(4, 6)[0], st6.ap[sl, s6:s6 + 12]),
            r=[st6.spec(s6, s6 + 12)], w=[sc(4, 6)[1]], tag=tag + "ag")
        act(AF.Sqrt, sc(2, 3)[0], sc(5, 6)[0], r=[sc(5, 6)[1]],
            w=[sc(2, 3)[1]], bias=float(EPS), tag=tag + "sd")
        act(AF.Sqrt, sc(3, 4)[0], sc(5, 6)[0], r=[sc(5, 6)[1]],
            w=[sc(3, 4)[1]], bias=float(EPS * EPS), scale=float(1.0 + EPS),
            tag=tag + "s2")
        dve(lambda e: e.reciprocal(sc(1, 2)[0], sc(2, 3)[0]),
            r=[sc(2, 3)[1]], w=[sc(1, 2)[1]], tag=tag + "rs")
        dve(lambda e: e.reciprocal(sc(6, 7)[0], sc(3, 4)[0]),
            r=[sc(3, 4)[1]], w=[sc(6, 7)[1]], tag=tag + "r2")
        dve(lambda e: e.tensor_scalar(n0_ap, x_ap, sc(4, 5)[0], sc(1, 2)[0],
                                      op0=ALU.subtract, op1=ALU.mult),
            r=[x_spec, sc(4, 5)[1], sc(1, 2)[1]], w=[n0_spec], tag=tag + "ap")
        g_ap, g_spec, b_ap, b_spec, tmp_ap, tmp_spec = g
        dve(lambda e: e.tensor_scalar(tmp_ap, x_ap, sc(4, 5)[0], sc(6, 7)[0],
                                      op0=ALU.subtract, op1=ALU.mult),
            r=[x_spec, sc(4, 5)[1], sc(6, 7)[1]], w=[tmp_spec], tag=tag + "a2")
        P.op(aff_eng, lambda e: e.tensor_tensor(tmp_ap, tmp_ap, g_ap, op=ALU.mult),
             r=[tmp_spec, g_spec], w=[tmp_spec], tag=tag + "gm")
        dve(lambda e: e.tensor_tensor(nrm_ap, tmp_ap, b_ap, op=ALU.add),
            r=[tmp_spec, b_spec], w=[nrm_spec], tag=tag + "ba")

    NT = B("NT", [128, IC, KV], BF16)
    XS = B("XS", [128, 8, 2052], BF16)
    zrs = [B(f"zr{i}", [128, 512], BF16) for i in range(2)]
    ln1g = B("ln1g", [128, D], F32)
    ln1b = B("ln1b", [128, D], F32)
    bcast_load("qsp", ln1g, din["ln1_g"])
    bcast_load("qsp", ln1b, din["ln1_b"])
    lntmp = B("lntmp", [128, D], F32)
    lntmp2 = B("lntmp2", [128, D], F32)
    x0p = B("x0p", [3, D], F32)
    nrmp = B("nrmp", [3, D], BF16)
    dma("qsp", x0p.ap, din["xp3"], w=[x0p.spec()], tag="xp3")
    layer_norm2(x0p.ap, x0p.spec(), lntmp.ap[0:3, :], lntmp.spec(),
                nrmp.ap, nrmp.spec(), pdim=3,
                g=(ln1g.ap[0:3, :], ln1g.spec(), ln1b.ap[0:3, :], ln1b.spec(),
                   lntmp2.ap[0:3, :], lntmp2.spec()), tag="p3")
    for ic in range(IC):
        tb_ = 6 + ic % 2
        transp(psbb(tb_, 0, 128, 0, 3), nrmp.ap[:, ic * 128:(ic + 1) * 128],
               r=[nrmp.spec(ic * 128, (ic + 1) * 128)], w=[ps_spec(tb_)], tag="p3t")
        dve(lambda e, i=ic, b=tb_: e.tensor_copy(np3.ap[:, i, :], psbb(b, 0, 128, 0, 3)),
            r=[ps_spec(tb_)], w=[np3.spec3(ic, 0, 3)], tag="p3e")

    x0p.free(); nrmp.free()
    x0r = [B(f"x0r{i}", [128, D], F32) for i in range(4)]
    n0r = [B(f"n0r{i}", [128, D], F32) for i in range(4)]
    nrmr = [B(f"nrm{i}", [128, D], BF16) for i in range(4)]
    lnt4 = [B(f"lnt4_{i}", [128, D], F32) for i in range(4)]
    gts = [B(f"gts{i}", [128, D], F32) for i in range(2)]
    rv = B("rv", [128, 16, 2], F32)
    winr = [B(f"winr{i}", [128, 8, 256], BF16) for i in range(2)]
    wgateS = B("wgateS", [128, 8, D], BF16)
    dma("qpool", wgateS.ap, din["Wgate"].rearrange("(g p) n -> p g n", p=128),
        w=[wgateS.spec()], tag="wgate")
    dve(lambda e: e.memset(XS.ap[:, :, 0:4], 0.0), r=[],
        w=[XS.spec3(c, 0, 4) for c in range(8)], tag="xspad")

    def p1_ln(c, comp_q):
        own = c < 4
        src = din["x0"] if own else din["xsib"]
        chains = []
        info = []
        for j in range(4):
            sg = c * 4 + j
            row0 = (c % 4) * 512 + j * 128
            xb = x0r[sg % 4]
            dma("qsp" if own else "qact", xb.ap, src[row0:row0 + 128, :],
                w=[xb.spec()], tag=f"x0_{sg}")
            n0b = n0r[sg % 4]
            nb = nrmr[sg % 4]
            ltb = lnt4[sg % 4]
            chains.append(layer_norm2_stages(
                xb.ap, xb.spec(), n0b.ap, n0b.spec(), nb.ap, nb.spec(),
                g=(ln1g.ap, ln1g.spec(), ln1b.ap, ln1b.spec(),
                   ltb.ap, ltb.spec()), tag=f"n_{sg}", slot=sg % 4))
            info.append((sg, row0, n0b, nb))
        for k in range(11):
            for ch in chains:
                ch[k]()
            if comp_q and k % 3 == 2:
                comp_q.pop(0)()
        for (sg, row0, n0b, nb) in info:
            if own:
                dma("qact", sp_n0[row0:row0 + 128, :], n0b.ap, r=[n0b.spec()],
                    w=[("hb_n0", row0, row0 + 128)], tag=f"spn0_{sg}")
            tcol = sg * 128 if own else 2048 + (sg - 16) * 128
            for ic in range(IC):
                tb_ = 6 + ic % 2
                transp(psbb(tb_, 0, 128, 0, 128), nb.ap[:, ic * 128:(ic + 1) * 128],
                       r=[nb.spec(ic * 128, (ic + 1) * 128)], w=[ps_spec(tb_)],
                       tag=f"tr_{sg}_{ic}")
                dve(lambda e, i=ic, t=tcol, b=tb_, nb2=nb: e.tensor_copy(
                    NT.ap[:, i, t:t + 128], psbb(b, 0, 128, 0, 128)),
                    r=[ps_spec(tb_)], w=[NT.spec3(ic, tcol, tcol + 128)], tag=f"te_{sg}_{ic}")
            for _ in range(2):
                if comp_q:
                    comp_q.pop(0)()

    def p1_compute(c):
        steps = []
        t0 = c * 512
        nt_r = [NT.spec3(i, t0, t0 + 512) for i in range(IC)]
        for cc in range(16):
            def step_cc(cc=cc):
                wb = winr[(cc // 2) % 2]
                off = (cc % 2) * 128
                if cc % 2 == 0:
                    dma("qpool", wb.ap, din["Win"][:, cc * 128:(cc + 2) * 128]
                        .rearrange("(g p) n -> p g n", p=128), w=[wb.spec()],
                        tag=f"win_{c}_{cc}")
                bank = cc % 2
                for i in range(IC):
                    mm(psf_bank(bank), wb.ap[:, i, off:off + 128],
                       NT.ap[:, i, t0:t0 + 512],
                       i == 0, i == IC - 1, r=[wb.spec3(i, off, off + 128), nt_r[i]],
                       w=[ps_spec(bank)], tag=f"xz_{c}_{cc}")
                if cc < 8:
                    dve(lambda e, ccc=cc, b=bank, tt=t0: e.tensor_copy(
                        XS.ap[:, ccc, 4 + tt: 4 + tt + 512], psf_bank(b)),
                        r=[ps_spec(bank)], w=[XS.spec3(cc, 4 + t0, 4 + t0 + 512)],
                        tag=f"xse_{c}_{cc}")
                else:
                    zr = zrs[cc % 2]
                    dve(lambda e, b=bank, z=zr: e.tensor_copy(z.ap, psf_bank(b)),
                        r=[ps_spec(bank)], w=[zr.spec()], tag=f"sgz_{c}_{cc}")
                    dma("qact", sgz_scr[cc - 8][:, t0:t0 + 512], zr.ap,
                        r=[zr.spec()], w=[("hb_z", (cc - 8) * 4 + t0 // 512,
                                          (cc - 8) * 4 + t0 // 512 + 1)],
                        tag=f"zs_{c}_{cc}")
                if c == 0 and cc < 8:
                    for i in range(IC):
                        mm(psf_bank(4, 0, 128, 0, 3), wb.ap[:, i, off:off + 128],
                           np3.ap[:, i, :],
                           i == 0, i == IC - 1,
                           r=[wb.spec3(i, off, off + 128), np3.spec3(i, 0, 3)],
                           w=[ps_spec(4)], tag=f"ctx_{cc}")
                    dve(lambda e, ccc=cc: e.tensor_scalar(
                        XS.ap[:, ccc, 1:4], psf_bank(4, 0, 128, 0, 3), cft.ap, None,
                        op0=ALU.mult), r=[ps_spec(4), cft.spec()],
                        w=[XS.spec3(cc, 1, 4)], tag=f"ctxe_{cc}")
            steps.append(step_cc)
        for j in range(4):
            def step_j(j=j):
                sg2 = c * 4 + j
                tl = sg2 * 128
                for nh in range(2):
                    for i in range(IC):
                        mm(psf_bank(2 + nh), NT.ap[:, i, tl:tl + 128],
                           wgateS.ap[:, i, nh * 512:(nh + 1) * 512], i == 0, i == IC - 1,
                           r=[NT.spec3(i, tl, tl + 128),
                              wgateS.spec3(i, nh * 512, (nh + 1) * 512)],
                           w=[ps_spec(2 + nh)], tag=f"gate_{sg2}_{nh}")
                gb = gts[sg2 % 2]
                act(AF.Sigmoid, gb.ap, PSF.ap()[:, 2 * 512: 4 * 512],
                    r=[ps_spec(2), ps_spec(3)], w=[gb.spec()], tag=f"gsig_{sg2}")
                dma("qact", sp_gate[tl:tl + 128, :], gb.ap, r=[gb.spec()],
                    w=[("hb_gate", tl, tl + 128)], tag=f"gsp_{sg2}")
                for i in range(IC):
                    mm(psf_bank(4, 0, 128, 0, 2), NT.ap[:, i, tl:tl + 128],
                       wr_s.ap[:, i, :], i == 0, i == IC - 1,
                       r=[NT.spec3(i, tl, tl + 128), wr_s.spec3(i, 0, 2)],
                       w=[ps_spec(4)], tag=f"r_{sg2}")
                dve(lambda e, ss=sg2: e.tensor_copy(rv.ap[:, ss, :],
                                                    psf_bank(4, 0, 128, 0, 2)),
                    r=[ps_spec(4)], w=[rv.spec3(sg2, 0, 2)], tag=f"rv_{sg2}")
            steps.append(step_j)
        return steps

    def p1_halt(sg2):
        if True:
            if True:
                act(AF.Sigmoid, alpha.ap[:, sg2:sg2 + 1], rv.ap[:, sg2, 0:1],
                    r=[rv.spec3(sg2, 0, 2), brt.spec()], w=[alpha.spec(sg2, sg2 + 1)],
                    bias=brt.ap[:, 0:1], tag=f"al_{sg2}")
                for k in range(3):
                    act(AF.Sigmoid, hlt.ap[:, k:k + 1], rv.ap[:, sg2, 1:2],
                        r=[rv.spec3(sg2, 0, 2), b1st.spec()], w=[hlt.spec(k, k + 1)],
                        bias=b1st.ap[:, k:k + 1], tag=f"h_{sg2}_{k}")
                ca, cb_, cc_, cd = 32, 33, 34, 35
                dve(lambda e: e.tensor_scalar(stt.ap[:, ca:ca + 1], hlt.ap[:, 0:1],
                                              -1.0, 1.0, op0=ALU.mult, op1=ALU.add),
                    r=[hlt.spec(0, 1)], w=[stt.spec(ca, ca + 1)], tag=f"w0_{sg2}")
                dve(lambda e: e.tensor_scalar(stt.ap[:, ca:ca + 1], stt.ap[:, ca:ca + 1],
                                              0.0, None, op0=ALU.max),
                    r=[stt.spec(ca, ca + 1)], w=[stt.spec(ca, ca + 1)], tag=f"w1_{sg2}")
                dve(lambda e: e.tensor_tensor(stt.ap[:, cb_:cb_ + 1], hlt.ap[:, 1:2],
                                              stt.ap[:, ca:ca + 1], op=ALU.mult),
                    r=[hlt.spec(1, 2), stt.spec(ca, ca + 1)],
                    w=[stt.spec(cb_, cb_ + 1)], tag=f"w2_{sg2}")
                dve(lambda e: e.tensor_tensor(stt.ap[:, cc_:cc_ + 1], stt.ap[:, ca:ca + 1],
                                              stt.ap[:, cb_:cb_ + 1], op=ALU.subtract),
                    r=[stt.spec(ca, cb_ + 1)], w=[stt.spec(cc_, cc_ + 1)], tag=f"w3_{sg2}")
                dve(lambda e: e.tensor_scalar(stt.ap[:, cc_:cc_ + 1], stt.ap[:, cc_:cc_ + 1],
                                              0.0, None, op0=ALU.max),
                    r=[stt.spec(cc_, cc_ + 1)], w=[stt.spec(cc_, cc_ + 1)], tag=f"w4_{sg2}")
                dve(lambda e: e.tensor_tensor(stt.ap[:, cd:cd + 1], hlt.ap[:, 0:1],
                                              stt.ap[:, cb_:cb_ + 1], op=ALU.add),
                    r=[hlt.spec(0, 1), stt.spec(cb_, cb_ + 1)],
                    w=[stt.spec(cd, cd + 1)], tag=f"w5_{sg2}")
                dve(lambda e, ss=sg2: e.tensor_tensor(wsum.ap[:, ss:ss + 1],
                                                      stt.ap[:, cd:cd + 1],
                                                      stt.ap[:, cc_:cc_ + 1], op=ALU.add),
                    r=[stt.spec(cc_, cd + 1)], w=[wsum.spec(sg2, sg2 + 1)], tag=f"w6_{sg2}")

    p1_order = (0, 4, 1, 5, 2, 6, 3, 7)
    comp_q = []
    for c_ in p1_order:
        p1_ln(c_, comp_q)
        if c_ < 4:
            comp_q.extend(p1_compute(c_))
    while comp_q:
        comp_q.pop(0)()

    ln1g.free(); ln1b.free()
    for bb in x0r + n0r + nrmr + gts + winr + lnt4 + zrs:
        bb.free()
    wgateS.free(); lntmp.free(); lntmp2.free()

    y1s = [B(f"y1_{i}", [128, 1024], F32) for i in range(4)]
    y2s = [B(f"y2_{i}", [128, 1024], F32) for i in range(4)]
    silys = [B(f"sily{i}", [128, 1024], BF16) for i in range(4)]
    uh = B("uh", [128, 8, 1024], BF16)
    woS = B("woS", [128, 8, D], BF16)
    ssms = [B(f"ssms{i}", [128, D], F32) for i in range(2)]
    dma("qpool", woS.ap, din["Wout_ssm"].rearrange("(g p) n -> p g n", p=128),
        w=[woS.spec()], tag="wout_ssm")

    zls = [B(f"zl{i}", [128, 1024], BF16) for i in range(4)]
    halt_q = list(range(NS))
    sm_list = []
    for th in range(2):
        tb = th * 1024
        def conv_stages(ccc):
            y1, y2, sily = y1s[ccc % 4], y2s[ccc % 4], silys[ccc % 4]
            zl = zls[ccc % 4]

            def st0():
                p1_halt(halt_q.pop(0))
                dma("qact", zl.ap, sgz_scr[ccc][:, tb:tb + 1024],
                    r=[("hb_z", ccc * 4 + th * 2, ccc * 4 + th * 2 + 2)],
                    w=[zl.spec()], tag=f"zl_{th}_{ccc}")
            stg = [st0]
            stg.append(lambda: act(AF.Sigmoid, zl.ap, zl.ap, r=[zl.spec()],
                                   w=[zl.spec()], tag=f"sgs_{th}_{ccc}"))
            stg.append(lambda: dve(lambda e, cc2=ccc, t=tb, y=y1: e.tensor_scalar(
                y.ap, XS.ap[:, cc2, 1 + t:1 + t + 1024], convw.ap[:, cc2, 0:1], None,
                op0=ALU.mult),
                r=[XS.spec3(ccc, 1 + tb, 1 + tb + 1024), convw.spec()],
                w=[y1.spec()], tag=f"cv0_{th}_{ccc}"))
            pairs = [(y1, y2), (y2, y1), (y1, y2)]
            for j in range(1, 4):
                si, di_ = pairs[j - 1]
                stg.append(lambda j=j, si=si, di_=di_: dve(
                    lambda e, cc2=ccc, t=tb, jj=j, s=si, d=di_: e.scalar_tensor_tensor(
                        d.ap, XS.ap[:, cc2, 1 + t + jj:1 + t + jj + 1024],
                        convw.ap[:, cc2, jj:jj + 1], s.ap, op0=ALU.mult, op1=ALU.add),
                    r=[XS.spec3(ccc, 1 + tb + j, 1 + tb + j + 1024), convw.spec(),
                       si.spec()], w=[di_.spec()], tag=f"cv{j}_{th}_{ccc}"))
            stg.append(lambda: act(AF.Silu, sily.ap, y2.ap,
                                   r=[y2.spec(), convb.spec()], w=[sily.spec()],
                                   bias=convb.ap[:, ccc:ccc + 1], tag=f"si_{th}_{ccc}"))
            stg.append(lambda: P.op("pool", lambda e, cc2=ccc, sl_=sily, z=zl: e.tensor_tensor(
                uh.ap[:, cc2, :], sl_.ap, z.ap, op=ALU.mult),
                r=[sily.spec(), zl.spec()],
                w=[uh.spec3(ccc, 0, 1024)], tag=f"u_{th}_{ccc}"))
            return stg

        for c4_ in range(2):
            quads = [conv_stages(4 * c4_ + i) for i in range(4)]
            for k in range(len(quads[0])):
                for q_ in quads:
                    q_[k]()
        for j8 in range(8):
            def sm_unit(th=th, j8=j8):
                sg = th * 8 + j8
                tl = j8 * 128
                pb0 = 0 if sg % 2 == 0 else 4
                for cc2 in range(8):
                    for nh in range(2):
                        mm(psf_bank(pb0 + nh), uh.ap[:, cc2, tl:tl + 128],
                           woS.ap[:, cc2, nh * 512:(nh + 1) * 512], cc2 == 0, cc2 == 7,
                           r=[uh.spec3(cc2, tl, tl + 128),
                              woS.spec3(cc2, nh * 512, (nh + 1) * 512)],
                           w=[ps_spec(pb0 + nh)], tag=f"sm_{sg}_{nh}")
                sb_ = ssms[sg % 2]
                dve(lambda e, b=sb_, p=pb0: e.tensor_copy(b.ap, PSF.ap()[:, p * 512:(p + 2) * 512]),
                    r=[ps_spec(pb0, pb0 + 2)], w=[sb_.spec()], tag=f"sme_{sg}")
                row = sg * 128
                dma("qact", sp_ssm[row:row + 128, :], sb_.ap, r=[sb_.spec()],
                    w=[("hb_ssm", row, row + 128)], tag=f"ssp_{sg}")
            sm_list.append(sm_unit)

    XS.free()
    for bb in y1s + y2s + silys + zls:
        bb.free()

    kta = [B("kta0", [128, KV], BF16)]
    ktb = [B("ktb0", [128, KV], BF16)]
    qta = [B("qta0", [128, TOK], BF16)]
    qtb = [B("qtb0", [128, TOK], BF16)]
    v2 = [B("v2_0", [128, 32, 130], BF16)]
    vts = [B("vts_0", [128, KV], BF16)]
    wq2 = [B("wq2_0", [128, 8, 128], BF16)]
    wk2 = [B("wk2_0", [128, 8, 128], BF16)]
    wv2 = [B("wv2_0", [128, 8, 128], BF16)]
    nsq = B("nsq", [128, 1024], BF16)
    qstg = B("qstg", [128, 512], F32)
    nsc = B("nsc", [128, 8], F32)

    def init_kqv(par_):
        P.op("dve", lambda e, k=kta[par_]: e.memset(k.ap[64:128, :], 0.0),
             w=[kta[par_].spec()], tag=f"kz_{par_}")
        P.op("dve", lambda e, k=kta[par_]: e.memset(k.ap[64:65, :], -1.0),
             r=[kta[par_].spec()], w=[kta[par_].spec()], tag=f"k1_{par_}")
        P.op("dve", lambda e, k=qta[par_]: e.memset(k.ap[64:128, :], 0.0),
             w=[qta[par_].spec()], tag=f"qz_{par_}")
        P.op("dve", lambda e, v=v2[par_]: e.memset(v.ap, 0.0), w=[v2[par_].spec()],
             tag=f"vz_{par_}")
        P.op("dve", lambda e, v=v2[par_]: e.memset(v.ap[:, :, 64:65], 1.0),
             r=[v2[par_].spec()], w=[v2[par_].spec()], tag=f"vo1_{par_}")
        P.op("dve", lambda e, v=v2[par_]: e.memset(v.ap[:, :, 129:130], 1.0),
             r=[v2[par_].spec()], w=[v2[par_].spec()], tag=f"vo2_{par_}")

    init_kqv(0)

    def headb_move_step(hp, kind):
        par = hp % 2
        db_ = (qtb if kind == "q" else ktb)[par]

        def stp(db_=db_, kind=kind, hp=hp):
            dma("qsp", db_.ap[0:64, :], db_.ap[64:128, :], r=[db_.spec()],
                w=[db_.spec()], tag=f"{kind}mv_{hp}")
            P.op("pool", lambda e, k=db_: e.memset(k.ap[64:128, :], 0.0), r=[db_.spec()],
                 w=[db_.spec()], tag=f"{kind}zz_{hp}")
            if kind == "k":
                P.op("pool", lambda e, k=db_: e.memset(k.ap[64:65, :], -1.0), r=[db_.spec()],
                     w=[db_.spec()], tag=f"{kind}one_{hp}")
        return stp

    def proj_steps(hp):
        par = hp % 2
        wq, wk, wv = wq2[0], wk2[0], wv2[0]
        c0 = hp * 128
        dma("qpool", wq.ap, din["Wqkv"][:, c0:c0 + 128]
            .rearrange("(g p) n -> p g n", p=128), w=[wq.spec()], tag=f"wq_{hp}")
        dma("qpool", wk.ap, din["Wqkv"][:, D + c0:D + c0 + 128]
            .rearrange("(g p) n -> p g n", p=128), w=[wk.spec()], tag=f"wk_{hp}")
        dma("qpool", wv.ap, din["Wqkv"][:, 2 * D + c0:2 * D + c0 + 128]
            .rearrange("(g p) n -> p g n", p=128), w=[wv.spec()], tag=f"wv_{hp}")
        steps = []
        gctr = [0]

        def grp(kind, wbuf, da, db_, t0):
            def g(kind=kind, wbuf=wbuf, da=da, db_=db_, t0=t0):
                bank = 3 if gctr[0] % 2 == 0 else 7
                gctr[0] += 1
                for i in range(IC):
                    mm(psf_bank(bank), wbuf.ap[:, i, :], NT.ap[:, i, t0:t0 + 512],
                       i == 0, i == IC - 1,
                       r=[wbuf.spec3(i, 0, 128), NT.spec3(i, t0, t0 + 512)],
                       w=[ps_spec(bank)], tag=f"{kind}p_{hp}_{t0}")
                if kind == "v":
                    dve(lambda e, t=t0, b=bank: e.tensor_copy(
                        vts[0].ap[:, t:t + 512], psf_bank(b)),
                        r=[ps_spec(bank)], w=[vts[0].spec(t0, t0 + 512)],
                        tag=f"ve_{hp}_{t0}")
                else:
                    dve(lambda e, d=da, t=t0, b=bank: e.tensor_copy(
                        d.ap[0:64, t:t + 512], psf_bank(b, 0, 64)),
                        r=[ps_spec(bank)], w=[da.spec(t0, t0 + 512)],
                        tag=f"{kind}a_{hp}_{t0}")
                    dve(lambda e, d=db_, t=t0, b=bank: e.tensor_copy(
                        d.ap[64:128, t:t + 512], psf_bank(b, 64, 128)),
                        r=[ps_spec(bank)], w=[db_.spec(t0, t0 + 512)],
                        tag=f"{kind}b_{hp}_{t0}")
            return g

        for t0 in range(0, TOK, 512):
            steps.append(grp("q", wq, qta[par], qtb[par], t0))
        steps.append(headb_move_step(hp, "q"))
        for t0 in range(0, KV, 512):
            steps.append(grp("k", wk, kta[par], ktb[par], t0))
        steps.append(headb_move_step(hp, "k"))
        for t0 in range(0, KV, 512):
            steps.append(grp("v", wv, None, None, t0))
        return steps

    def v_steps(hp):
        par = hp % 2
        vv, vt = v2[par], vts[0]
        steps = []
        for ts_ in range(32):
            def step(ts2=ts_, vv2=vv, vt2=vt, hp2=hp):
                t0 = ts2 * 128
                transp(psbb(7, 0, 128, 0, 128), vt2.ap[:, t0:t0 + 128],
                       r=[vt2.spec(t0, t0 + 128)], w=[ps_spec(7)], tag=f"vt_{hp2}_{ts2}")
                dst = vv2.ap[:, ts2, :].rearrange("p (a b) -> p a b", a=2, b=65)[:, :, 0:64]
                srcp = psbb(7, 0, 128, 0, 128).rearrange("p (a b) -> p a b", a=2, b=64)
                dve(lambda e, d=dst, s2=srcp: e.tensor_copy(d, s2),
                    r=[ps_spec(7)], w=[vv2.spec3(ts2, 0, 130)], tag=f"va_{hp2}_{ts2}")
            steps.append(step)
        return steps

    def norm_steps(hp):
        par = hp % 2
        steps = []
        for qbuf, kbuf in ((qta[par], kta[par]), (qtb[par], ktb[par])):
            for h in range(4):
                def sq_k(kb2=kbuf, hp2=hp, h2=h):
                    dve(lambda e: e.tensor_tensor(
                        nsq.ap, kb2.ap[:, h2 * 1024:(h2 + 1) * 1024],
                        kb2.ap[:, h2 * 1024:(h2 + 1) * 1024], op=ALU.mult),
                        r=[kb2.spec()], w=[nsq.spec()], tag=f"ksq_{hp2}_{h2}")
                steps.append(sq_k)
                for j in range(2):
                    def kn_j(j2=j, hp2=hp, h2=h):
                        mm(PSF.ap()[64:65, 3 * 512:3 * 512 + 512], onesb.ap,
                           nsq.ap[:, j2 * 512:(j2 + 1) * 512], True, True,
                           r=[onesb.spec(), nsq.spec()], w=[ps_spec(3)],
                           tag=f"kn_{hp2}_{h2}_{j2}")
                        dve(lambda e: e.reduce_max(nsc.ap[64:65, 1:2],
                                                   PSF.ap()[64:65, 3 * 512:3 * 512 + 512],
                                                   axis=AX.X),
                            r=[ps_spec(3)], w=[nsc.spec(1, 2)], tag=f"knr_{hp2}_{h2}_{j2}")
                        if h2 == 0 and j2 == 0:
                            dve(lambda e: e.tensor_copy(nsc.ap[64:65, 0:1],
                                                        nsc.ap[64:65, 1:2]),
                                r=[nsc.spec(1, 2)], w=[nsc.spec(0, 1)], tag=f"kni_{hp2}")
                        else:
                            dve(lambda e: e.tensor_tensor(nsc.ap[64:65, 0:1],
                                                          nsc.ap[64:65, 0:1],
                                                          nsc.ap[64:65, 1:2], op=ALU.max),
                                r=[nsc.spec(0, 2)], w=[nsc.spec(0, 1)],
                                tag=f"knm_{hp2}_{h2}_{j2}")
                    steps.append(kn_j)

            def km(hp2=hp):
                act(AF.Ln, nsc.ap[64:65, 2:3], nsc.ap[64:65, 0:1],
                    r=[nsc.spec(0, 1)], w=[nsc.spec(2, 3)], tag=f"km_{hp2}")
            steps.append(km)

            for h in range(2):
                def sq_q(qb2=qbuf, hp2=hp, h2=h):
                    dve(lambda e: e.tensor_tensor(
                        nsq.ap[0:64, :], qb2.ap[0:64, h2 * 1024:(h2 + 1) * 1024],
                        qb2.ap[0:64, h2 * 1024:(h2 + 1) * 1024], op=ALU.mult),
                        r=[qb2.spec()], w=[nsq.spec()], tag=f"qsq_{hp2}_{h2}")
                steps.append(sq_q)
                for j in range(2):
                    def qn_j(j2=j, qb2=qbuf, hp2=hp, h2=h):
                        mm(PSF.ap()[64:65, 3 * 512:3 * 512 + 512], onesb.ap,
                           nsq.ap[:, j2 * 512:(j2 + 1) * 512], True, True,
                           r=[onesb.spec(), nsq.spec()], w=[ps_spec(3)],
                           tag=f"qn_{hp2}_{h2}_{j2}")
                        act(AF.Ln, qstg.ap[64:65, :],
                            PSF.ap()[64:65, 3 * 512:3 * 512 + 512],
                            r=[ps_spec(3)], w=[qstg.spec()], tag=f"ql_{hp2}_{h2}_{j2}")
                        dve(lambda e: e.tensor_scalar(qstg.ap[64:65, :],
                                                      qstg.ap[64:65, :],
                                                      nsc.ap[64:65, 2:3], None,
                                                      op0=ALU.add),
                            r=[qstg.spec(), nsc.spec(2, 3)], w=[qstg.spec()],
                            tag=f"qd_{hp2}_{h2}_{j2}")
                        q0_ = (h2 * 2 + j2) * 512
                        act(AF.Exp, qb2.ap[64:65, q0_:q0_ + 512],
                            qstg.ap[64:65, :], r=[qstg.spec()], w=[qb2.spec()],
                            scale=0.5, tag=f"qw_{hp2}_{h2}_{j2}")
                    steps.append(qn_j)
        return steps

    def st_pv_qb2(hp, qb, steps, pending, len0):
        par = hp % 2
        q0 = qb * 512
        vv = v2[par]
        cfg = []
        for hoff in range(2):
            cfg.append(((qta if hoff == 0 else qtb)[par],
                        (kta if hoff == 0 else ktb)[par],
                        0 if hoff == 0 else 4, 2 if hoff == 0 else 6,
                        ptbs[hoff], 65 * hoff))

        def S(hoff, kc):
            qsrc, ksrc, sb0, ob, ptbx, voff = cfg[hoff]
            bank = sb0 + kc % 2
            mm(psf_bank(bank), ksrc.ap[:, kc * 128:(kc + 1) * 128],
               qsrc.ap[:, q0:q0 + 512], True, True,
               r=[ksrc.spec(kc * 128, (kc + 1) * 128), qsrc.spec(q0, q0 + 512)],
               w=[ps_spec(bank)], tag=f"st_{hp}_{hoff}_{qb}_{kc}")

        def E(hoff, pp):
            qsrc, ksrc, sb0, ob, ptbx, voff = cfg[hoff]
            slot = (2 * pp) % 4
            act(AF.Exp, ptbx.ap[:, slot:slot + 2, :],
                PSF.ap()[:, sb0 * 512:(sb0 + 2) * 512],
                r=[ps_spec(sb0, sb0 + 2)],
                w=[ptbx.spec3(slot, 0, 512), ptbx.spec3(slot + 1, 0, 512)],
                scale=0.125, tag=f"ex_{hp}_{hoff}_{qb}_{pp}")

        def PV(hoff, kc):
            qsrc, ksrc, sb0, ob, ptbx, voff = cfg[hoff]
            slot = kc % 4
            mm(psf_bank(ob, 0, 65), vv.ap[:, kc, voff:voff + 65], ptbx.ap[:, slot, :],
               kc == 0, kc == 31,
               r=[vv.spec3(kc, voff, voff + 65), ptbx.spec3(slot, 0, 512)],
               w=[ps_spec(ob)], tag=f"pv_{hp}_{hoff}_{qb}_{kc}")

        idx0 = qb * 16
        for pp in range(16):
            for hoff in range(2):
                S(hoff, 2 * pp)
                S(hoff, 2 * pp + 1)
            E(0, pp)
            E(1, pp)
            if pp >= 1:
                for hoff in range(2):
                    PV(hoff, 2 * pp - 2)
                    PV(hoff, 2 * pp - 1)
            if pending and pp in (1, 2):
                pending.pop(0)()
            idx = idx0 + pp
            tgt = len0 if idx < 6 else len0 * (63 - idx) // 58
            while steps and len(steps) > tgt:
                steps.pop(0)()
        for hoff in range(2):
            PV(hoff, 30)
            PV(hoff, 31)
        for hoff in range(2):
            qsrc, ksrc, sb0, ob, ptbx, voff = cfg[hoff]
            tmpx, rzx = tmpas[hoff], rz0s[hoff]
            dve(lambda e, b=ob, t=tmpx: e.tensor_copy(t.ap, psf_bank(b, 0, 65)),
                r=[ps_spec(ob)], w=[tmpx.spec()], tag=f"fe_{hp}_{hoff}_{qb}")
            act(AF.Ln, rzx.ap[64:65, :], tmpx.ap[64:65, :],
                r=[tmpx.spec()], w=[rzx.spec()], tag=f"rl_{hp}_{hoff}_{qb}")
            act(AF.Exp, rzx.ap[64:65, :], rzx.ap[64:65, :],
                r=[rzx.spec()], w=[rzx.spec()], scale=-1.0,
                tag=f"rz_{hp}_{hoff}_{qb}")

            def part_b(hp2=hp, hoff2=hoff, qb2=qb, q02=q0, tmpx=tmpx, rzx=rzx):
                cb = 3 if hoff2 == 0 else 7
                mm(psf_bank(cb, 0, 64), ones_f.ap[64:65, 0:64], rzx.ap[64:65, :],
                   True, True, r=[ones_f.spec(), rzx.spec()], w=[ps_spec(cb)],
                   tag=f"bc_{hp2}_{hoff2}_{qb2}")
                if hoff2 == 0:
                    dve(lambda e: e.tensor_tensor(ATT.ap[0:64, hp2, q02:q02 + 512],
                                                  tmpx.ap[0:64, :], psf_bank(cb, 0, 64),
                                                  op=ALU.mult),
                        r=[tmpx.spec(), ps_spec(cb)], w=[ATT.spec3(hp2, q02, q02 + 512)],
                        tag=f"fn_{hp2}_{hoff2}_{qb2}")
                else:
                    dve(lambda e: e.tensor_tensor(nrmb.ap, tmpx.ap[0:64, :],
                                                  psf_bank(cb, 0, 64), op=ALU.mult),
                        r=[tmpx.spec(), ps_spec(cb)], w=[nrmb.spec()],
                        tag=f"fn_{hp2}_{hoff2}_{qb2}")
                    dma("qsp", ATT.ap[64:128, hp2, q02:q02 + 512], nrmb.ap, r=[nrmb.spec()],
                        w=[ATT.spec3(hp2, q02, q02 + 512)], tag=f"fv_{hp2}_{hoff2}_{qb2}")
            pending.append(part_b)

    def make_p0_steps():
        p0 = {}
        for nm, sh in [("g1r", [32, 128]), ("g1i", [32, 128]),
                       ("g2r", [16, 4, 16]), ("g2i", [16, 4, 16]),
                       ("qselb", [32, 512]), ("pseltb", [32, 4, 128]),
                       ("rseltb", [16, 128]), ("sselb", [16, 512]),
                       ("t1s", [32, 512]), ("t2s", [16, 512]),
                       ("e1s", [128, 512]), ("tmpw", [128, 512]),
                       ("wtRf", [128, 4, 512]), ("wtIf", [128, 4, 512])]:
            p0[nm] = B(nm, sh, F32)
        dma("qsp", p0["g1r"].ap, din["G1R"].rearrange("o b c d -> (o b) (c d)"), w=[p0["g1r"].spec()])
        dma("qsp", p0["g1i"].ap, din["G1I"].rearrange("o b c d -> (o b) (c d)"), w=[p0["g1i"].spec()])
        for c in range(4):
            dma("qsp", p0["g2r"].ap[:, c, :], din["G2R"][c].rearrange("e o g -> e (o g)"),
                w=[p0["g2r"].spec3(c, 0, 16)])
            dma("qsp", p0["g2i"].ap[:, c, :], din["G2I"][c].rearrange("e o g -> e (o g)"),
                w=[p0["g2i"].spec3(c, 0, 16)])
        dma("qsp", p0["qselb"].ap, din["qsel"], w=[p0["qselb"].spec()])
        dma("qsp", p0["pseltb"].ap, din["pselt"].rearrange("a (t n) -> a t n", t=4),
            w=[p0["pseltb"].spec()])
        dma("qsp", p0["rseltb"].ap, din["rselt"], w=[p0["rseltb"].spec()])
        dma("qsp", p0["sselb"].ap, din["ssel"], w=[p0["sselb"].spec()])

        steps = []
        for g1, g2, wtf in ((p0["g1r"], p0["g2r"], p0["wtRf"]),
                            (p0["g1i"], p0["g2i"], p0["wtIf"])):
            for c in range(4):
                def stepc(g1=g1, g2=g2, c=c):
                    mm(psf_bank(3, 0, 32), g1.ap[:, c * 32:(c + 1) * 32], p0["qselb"].ap,
                       True, True, r=[g1.spec(c * 32, (c + 1) * 32), p0["qselb"].spec()],
                       w=[ps_spec(3)], tag="t1")
                    dve(lambda e: e.tensor_copy(p0["t1s"].ap, psf_bank(3, 0, 32)),
                        r=[ps_spec(3)], w=[p0["t1s"].spec()], tag="t1e")
                    mm(psf_bank(7, 0, 16), g2.ap[:, c, :], p0["sselb"].ap,
                       True, True, r=[g2.spec3(c, 0, 16), p0["sselb"].spec()],
                       w=[ps_spec(7)], tag="t2")
                    dve(lambda e: e.tensor_copy(p0["t2s"].ap, psf_bank(7, 0, 16)),
                        r=[ps_spec(7)], w=[p0["t2s"].spec()], tag="t2e")
                steps.append(stepc)
                for t in range(4):
                    def stept(wtf=wtf, c=c, t=t):
                        mm(psf_bank(3), p0["pseltb"].ap[:, t, :], p0["t1s"].ap, True, True,
                           r=[p0["pseltb"].spec3(t, 0, 128), p0["t1s"].spec()],
                           w=[ps_spec(3)], tag="e1")
                        mm(psf_bank(7), p0["rseltb"].ap, p0["t2s"].ap, True, True,
                           r=[p0["rseltb"].spec(), p0["t2s"].spec()], w=[ps_spec(7)], tag="e2")
                        dve(lambda e: e.tensor_copy(p0["e1s"].ap, psf_bank(3)),
                            r=[ps_spec(3)], w=[p0["e1s"].spec()], tag="e1e")
                        if c == 0:
                            dve(lambda e, tt=t, wf=wtf: e.tensor_tensor(
                                wf.ap[:, tt, :], p0["e1s"].ap, psf_bank(7), op=ALU.mult),
                                r=[p0["e1s"].spec(), ps_spec(7)], w=[wtf.spec3(t, 0, 512)], tag="wt0")
                        else:
                            dve(lambda e: e.tensor_tensor(p0["tmpw"].ap, p0["e1s"].ap,
                                                          psf_bank(7), op=ALU.mult),
                                r=[p0["e1s"].spec(), ps_spec(7)], w=[p0["tmpw"].spec()], tag="wtm")
                            dve(lambda e, tt=t, wf=wtf: e.tensor_tensor(
                                wf.ap[:, tt, :], wf.ap[:, tt, :], p0["tmpw"].ap, op=ALU.add),
                                r=[p0["tmpw"].spec(), wtf.spec3(t, 0, 512)],
                                w=[wtf.spec3(t, 0, 512)], tag="wta")
                    steps.append(stept)

        def stepfin():
            dma("qpool", wt_scr[0], p0["wtRf"].ap, r=[p0["wtRf"].spec()],
                w=[("hb_wt", 0, 1)], tag="wtRs")
            dma("qpool", wt_scr[1], p0["wtIf"].ap, r=[p0["wtIf"].spec()],
                w=[("hb_wt", 1, 2)], tag="wtIs")
            for v in p0.values():
                v.free()
        steps.append(stepfin)
        return steps

    prep0 = proj_steps(0) + v_steps(0) + norm_steps(0)
    ksm = 0
    for i_, stp in enumerate(prep0):
        stp()
        if i_ % 6 == 5 and ksm < len(sm_list):
            sm_list[ksm]()
            ksm += 1
    while ksm < len(sm_list):
        sm_list[ksm]()
        ksm += 1
    uh.free(); woS.free()
    for bb in ssms:
        bb.free()
    ATT = B("ATT", [128, 8, TOK], BF16)
    kta.append(B("kta1", [128, KV], BF16))
    ktb.append(B("ktb1", [128, KV], BF16))
    qta.append(B("qta1", [128, TOK], BF16))
    qtb.append(B("qtb1", [128, TOK], BF16))
    v2.append(B("v2_1", [128, 32, 130], BF16))
    ptbs = [B(f"ptb{i}", [128, 4, 512], BF16) for i in range(2)]
    tmpas = [B(f"tmpa{i}", [65, 512], F32) for i in range(2)]
    rz0s = [B(f"rz0{i}", [128, 512], F32) for i in range(2)]
    nrmb = B("nrmb", [64, 512], BF16)
    init_kqv(1)
    pending = []
    for hp in range(8):
        if hp < 7:
            steps = proj_steps(hp + 1) + v_steps(hp + 1) + norm_steps(hp + 1)
        else:
            NT.free()
            steps = make_p0_steps()
        len0 = len(steps)
        for qb in range(4):
            st_pv_qb2(hp, qb, steps, pending, len0)
        for stp in steps:
            stp()
    for pb in pending:
        pb()

    for bb in kta + ktb + qta + qtb + v2 + vts + wq2 + wk2 + wv2 + ptbs + tmpas + rz0s:
        bb.free()
    nrmb.free()
    qstg.free(); nsc.free(); nsq.free()

    ACC2 = [B(f"ACC{i}", [128, NS // 2, D], F32) for i in range(2)]

    def ACCa(s):
        return ACC2[s // 8].ap[:, s % 8, :]

    def ACCs(s):
        return ACC2[s // 8].spec3(s % 8, 0, D)
    MIT = B("MIT", [128, IC, TOK], BF16)
    woutS = B("woutS", [128, 8, D], BF16)
    ln2g = B("ln2g", [128, D], F32)
    ln2b = B("ln2b", [128, D], F32)
    ls1t = B("ls1t", [128, D], F32)
    ls2t = B("ls2t", [128, D], F32)
    dma("qpool", woutS.ap, din["Wout"].rearrange("(g p) n -> p g n", p=128),
        w=[woutS.spec()], tag="wout")
    bcast_load("qsp", ln2g, din["ln2_g"])
    bcast_load("qsp", ln2b, din["ln2_b"])
    bcast_load("qsp", ls1t, din["ls1"])
    bcast_load("qsp", ls2t, din["ls2"])
    gr = [B(f"gr{i}", [128, D], F32) for i in range(2)]
    sr = [B(f"sr{i}", [128, D], F32) for i in range(2)]
    nr = [B(f"nr{i}", [128, D], F32) for i in range(2)]
    agbs = [B(f"agb{i}", [128, D], F32) for i in range(2)]
    mints = [B(f"mint{i}", [128, D], BF16) for i in range(2)]
    prbs = [B(f"prb{i}", [128, NE], F32) for i in range(2)]
    t8bs = [B(f"t8b{i}", [128, NE], F32) for i in range(2)]
    mskbs = [B(f"mskb{i}", [128, NE], F32) for i in range(2)]

    def p3_tile_stages(s):
        tl = s * 128
        par = s % 2
        gb, sb_, nb = gr[par], sr[par], nr[par]
        agb = agbs[par]
        mint = mints[par]
        prb, t8b, mskb = prbs[par], t8bs[par], mskbs[par]
        apb = par * 2
        g0 = 40 + 6 * par

        def gp(a, b2):
            return stt.ap[:, g0 + a:g0 + b2], stt.spec(g0 + a, g0 + b2)

        def st_loads():
            dma("qact", gb.ap, sp_gate[tl:tl + 128, :], r=[("hb_gate", tl, tl + 128)],
                w=[gb.spec()], tag=f"grl_{s}")
            dma("qsp", sb_.ap, sp_ssm[tl:tl + 128, :], r=[("hb_ssm", tl, tl + 128)],
                w=[sb_.spec()], tag=f"srl_{s}")
            dma("qpool", nb.ap, sp_n0[tl:tl + 128, :], r=[("hb_n0", tl, tl + 128)],
                w=[nb.spec()], tag=f"nrl_{s}")

        def st_ap():
            for nh in range(2):
                bank = apb + nh
                for cc in range(8):
                    mm(psf_bank(bank), ATT.ap[:, cc, tl:tl + 128],
                       woutS.ap[:, cc, nh * 512:(nh + 1) * 512], cc == 0, cc == 7,
                       r=[ATT.spec3(cc, tl, tl + 128),
                          woutS.spec3(cc, nh * 512, (nh + 1) * 512)],
                       w=[ps_spec(bank)], tag=f"ap_{s}_{nh}")

        stages = [st_loads, st_ap]
        stages.append(lambda: dve(lambda e, g=gb, p=apb, a=agb: e.tensor_tensor(
            a.ap, PSF.ap()[:, p * 512:(p + 2) * 512], g.ap, op=ALU.mult),
            r=[ps_spec(apb, apb + 2), gb.spec()], w=[agb.spec()], tag=f"ag_{s}"))
        stages.append(lambda: dve(lambda e, b=sb_, a=agb: e.tensor_tensor(
            a.ap, a.ap, b.ap, op=ALU.subtract),
            r=[agb.spec(), sb_.spec()], w=[agb.spec()], tag=f"d_{s}"))
        stages.append(lambda: dve(lambda e, b=sb_, ss=s, a=agb: e.scalar_tensor_tensor(
            a.ap, a.ap, alpha.ap[:, ss:ss + 1], b.ap, op0=ALU.mult, op1=ALU.add),
            r=[agb.spec(), alpha.spec(s, s + 1), sb_.spec()], w=[agb.spec()],
            tag=f"bl_{s}"))
        stages.append(lambda: dve(lambda e, a=agb: e.tensor_tensor(
            a.ap, a.ap, ls1t.ap, op=ALU.mult),
            r=[agb.spec(), ls1t.spec()], w=[agb.spec()], tag=f"ls_{s}"))
        stages.append(lambda: dve(lambda e, n=nb, ss=s, a=agb: e.tensor_tensor(
            ACCa(ss), a.ap, n.ap, op=ALU.add),
            r=[agb.spec(), nb.spec()], w=[ACCs(s)], tag=f"acc_{s}"))

        c = 8 * par
        s6 = 16 * par

        def sc(a, b2):
            return stt.ap[:, c + a:c + b2], stt.spec(c + a, c + b2)
        stages.append(lambda: dve(lambda e: e.bn_stats(st6.ap[:, s6:s6 + 6],
                                                       ACCa(s)[:, 0:512]),
                                  r=[ACCs(s)], w=[st6.spec(s6, s6 + 6)], tag=f"mi_{s}b0"))
        stages.append(lambda: dve(lambda e: e.bn_stats(st6.ap[:, s6 + 6:s6 + 12],
                                                       ACCa(s)[:, 512:1024]),
                                  r=[ACCs(s)], w=[st6.spec(s6 + 6, s6 + 12)], tag=f"mi_{s}b1"))
        stages.append(lambda: dve(lambda e: e.bn_aggr(sc(4, 6)[0], st6.ap[:, s6:s6 + 12]),
                                  r=[st6.spec(s6, s6 + 12)], w=[sc(4, 6)[1]], tag=f"mi_{s}ag"))
        stages.append(lambda: act(AF.Sqrt, sc(2, 3)[0], sc(5, 6)[0], r=[sc(5, 6)[1]],
                                  w=[sc(2, 3)[1]], bias=float(EPS), tag=f"mi_{s}sd"))
        stages.append(lambda: dve(lambda e: e.reciprocal(sc(1, 2)[0], sc(2, 3)[0]),
                                  r=[sc(2, 3)[1]], w=[sc(1, 2)[1]], tag=f"mi_{s}rs"))
        stages.append(lambda: dve(lambda e: e.tensor_scalar(
            agb.ap, ACCa(s), sc(4, 5)[0], sc(1, 2)[0],
            op0=ALU.subtract, op1=ALU.mult),
            r=[ACCs(s), sc(4, 5)[1], sc(1, 2)[1]], w=[agb.spec()], tag=f"mi_{s}ap"))
        stages.append(lambda: dve(lambda e: e.tensor_tensor(agb.ap, agb.ap, ln2g.ap,
                                                            op=ALU.mult),
                                  r=[agb.spec(), ln2g.spec()], w=[agb.spec()], tag=f"mi_{s}gm"))
        stages.append(lambda: dve(lambda e, m=mint: e.tensor_tensor(m.ap, agb.ap, ln2b.ap,
                                                                    op=ALU.add),
                                  r=[agb.spec(), ln2b.spec()], w=[mint.spec()], tag=f"mi_{s}ba"))

        def st_tr():
            for ic in range(IC):
                tb_ = 6 + ic % 2
                transp(psbb(tb_, 0, 128, 0, 128), mint.ap[:, ic * 128:(ic + 1) * 128],
                       r=[mint.spec(ic * 128, (ic + 1) * 128)], w=[ps_spec(tb_)],
                       tag=f"mt_{s}_{ic}")
                dve(lambda e, i=ic, t=tl, b=tb_: e.tensor_copy(
                    MIT.ap[:, i, t:t + 128], psbb(b, 0, 128, 0, 128)),
                    r=[ps_spec(tb_)], w=[MIT.spec3(ic, tl, tl + 128)], tag=f"me_{s}_{ic}")
        stages.append(st_tr)

        wgb = 4 + par

        def st_wg():
            for i in range(IC):
                mm(psf_bank(wgb, 0, 128, 0, NE), MIT.ap[:, i, tl:tl + 128],
                   wg_s.ap[:, i, :], i == 0, i == IC - 1,
                   r=[MIT.spec3(i, tl, tl + 128), wg_s.spec3(i, 0, NE)],
                   w=[ps_spec(wgb)], tag=f"wg_{s}")
        stages.append(st_wg)
        stages.append(lambda: dve(lambda e: e.reduce_max(
            gp(0, 1)[0], psf_bank(wgb, 0, 128, 0, NE), axis=AX.X, negate=True),
            r=[ps_spec(wgb)], w=[gp(0, 1)[1]], tag=f"gm_{s}"))
        stages.append(lambda: P.op("act", lambda e: e.activation(
            prb.ap, psf_bank(wgb, 0, 128, 0, NE), AF.Exp, bias=gp(0, 1)[0],
            accum_out=gp(1, 2)[0]),
            r=[ps_spec(wgb), gp(0, 1)[1]], w=[prb.spec(), gp(1, 2)[1]], tag=f"ge_{s}"))
        stages.append(lambda: dve(lambda e: e.reciprocal(gp(2, 3)[0], gp(1, 2)[0]),
                                  r=[gp(1, 2)[1]], w=[gp(2, 3)[1]], tag=f"gr_{s}"))
        stages.append(lambda: dve(lambda e: e.tensor_scalar(
            prb.ap, prb.ap, gp(2, 3)[0], None, op0=ALU.mult),
            r=[prb.spec(), gp(2, 3)[1]], w=[prb.spec()], tag=f"gn_{s}"))
        stages.append(lambda: dve(lambda e: e.max(t8b.ap, prb.ap),
                                  r=[prb.spec()], w=[t8b.spec()], tag=f"g8_{s}"))
        stages.append(lambda: dve(lambda e: e.tensor_tensor(
            gp(3, 4)[0], t8b.ap[:, 0:1], t8b.ap[:, 1:2], op=ALU.add),
            r=[t8b.spec()], w=[gp(3, 4)[1]], tag=f"gd_{s}"))
        stages.append(lambda: dve(lambda e: e.tensor_scalar(
            gp(3, 4)[0], gp(3, 4)[0], 1e-8, None, op0=ALU.add),
            r=[gp(3, 4)[1]], w=[gp(3, 4)[1]], tag=f"gd2_{s}"))
        stages.append(lambda: dve(lambda e: e.reciprocal(gp(4, 5)[0], gp(3, 4)[0]),
                                  r=[gp(3, 4)[1]], w=[gp(4, 5)[1]], tag=f"gd3_{s}"))
        stages.append(lambda: dve(lambda e: e.tensor_scalar(
            mskb.ap, prb.ap, t8b.ap[:, 1:2], None, op0=ALU.is_ge),
            r=[prb.spec(), t8b.spec()], w=[mskb.spec()], tag=f"gk_{s}"))
        stages.append(lambda: dve(lambda e, ss=s: e.scalar_tensor_tensor(
            cwb.ap[:, ss, :], prb.ap, gp(4, 5)[0], mskb.ap,
            op0=ALU.mult, op1=ALU.mult),
            r=[prb.spec(), gp(4, 5)[1], mskb.spec()],
            w=[cwb.spec3(s, 0, NE)], tag=f"gc_{s}"))
        return stages

    for t_ in range(8):
        pair = [p3_tile_stages(2 * t_), p3_tile_stages(2 * t_ + 1)]
        for k in range(len(pair[0])):
            pair[0][k]()
            pair[1][k]()

    ATT.free(); woutS.free()
    for bb in gr + sr + agbs + mints + prbs + t8bs + mskbs:
        bb.free()
    ln2g.free(); ln2b.free(); ls1t.free()
    for bb in nr:
        bb.free()

    HT = B("HT", [128, 16, 1024], BF16)
    w2q = [B(f"w2q{i}", [128, 4, D], BF16) for i in range(4)]
    w1r = [B(f"w1r{i}", [128, 8, 256], BF16) for i in range(2)]
    w3r = [B(f"w3r{i}", [128, 8, 256], BF16) for i in range(2)]
    s1bs = [B(f"s1b{i}", [128, 512], F32) for i in range(2)]
    mtmp = B("mtmp", [128, D], F32)

    def moe_fc(e_, tb):
        def w2_load(ee=e_):
            for qq in range(4):
                dma("qpool", w2q[qq].ap,
                    din["W2"][ee][qq * 512:(qq + 1) * 512]
                    .rearrange("(g p) n -> p g n", p=128),
                    w=[w2q[qq].spec()], tag=f"w2_{ee}_{qq}")
        if True:
            tq0 = tb * 1024
            for fc in range(16):
                f0 = fc * 128
                gidx = ((tb * 16 + fc) // 2) % 2
                wb1, wb3 = w1r[gidx], w3r[gidx]
                off = (fc % 2) * 128
                if fc % 2 == 0:
                    dma("qpool", wb1.ap, din["W1"][e_][:, f0:f0 + 256]
                        .rearrange("(g p) n -> p g n", p=128), w=[wb1.spec()],
                        tag=f"w1_{e_}_{tb}_{fc}")
                    dma("qpool", wb3.ap, din["W3"][e_][:, f0:f0 + 256]
                        .rearrange("(g p) n -> p g n", p=128), w=[wb3.spec()],
                        tag=f"w3_{e_}_{tb}_{fc}")
                if tb == 0 and fc == 2:
                    w2_load()
                for tq in range(2):
                    t0 = tq0 + tq * 512
                    ba = tq % 2
                    for i in range(IC):
                        mm(psf_bank(ba * 2), wb1.ap[:, i, off:off + 128],
                           MIT.ap[:, i, t0:t0 + 512], i == 0, i == IC - 1,
                           r=[wb1.spec3(i, off, off + 128),
                              MIT.spec3(i, t0, t0 + 512)],
                           w=[ps_spec(ba * 2)], tag=f"a1_{e_}_{tb}_{fc}_{tq}")
                    for i in range(IC):
                        mm(psf_bank(ba * 2 + 1), wb3.ap[:, i, off:off + 128],
                           MIT.ap[:, i, t0:t0 + 512], i == 0, i == IC - 1,
                           r=[wb3.spec3(i, off, off + 128),
                              MIT.spec3(i, t0, t0 + 512)],
                           w=[ps_spec(ba * 2 + 1)], tag=f"a3_{e_}_{tb}_{fc}_{tq}")
                    s1x = s1bs[tq]
                    act(AF.Silu, s1x.ap, psf_bank(ba * 2),
                        r=[ps_spec(ba * 2)], w=[s1x.spec()],
                        tag=f"hs_{e_}_{tb}_{fc}_{tq}")
                    dve(lambda e, f=fc, t=tq * 512, b=ba, sx=s1x: e.tensor_tensor(
                        HT.ap[:, f, t:t + 512], sx.ap, psf_bank(b * 2 + 1),
                        op=ALU.mult),
                        r=[s1x.spec(), ps_spec(ba * 2 + 1)],
                        w=[HT.spec3(fc, tq * 512, tq * 512 + 512)],
                        tag=f"hm_{e_}_{tb}_{fc}_{tq}")
    def moe_out(e_, tb, pulls=None):
        if True:
            for j8 in range(8):
                s = tb * 8 + j8
                tl = j8 * 128
                ob_ = 4 if (e_ == NE - 1 and tb == 1) else 4 + (s % 2) * 2
                for nh in range(2):
                    for fc in range(16):
                        w2x = w2q[fc // 4]
                        mm(psf_bank(ob_ + nh), HT.ap[:, fc, tl:tl + 128],
                           w2x.ap[:, fc % 4, nh * 512:(nh + 1) * 512], fc == 0, fc == 15,
                           r=[HT.spec3(fc, tl, tl + 128),
                              w2x.spec3(fc % 4, nh * 512, (nh + 1) * 512)],
                           w=[ps_spec(ob_ + nh)], tag=f"o2_{e_}_{s}_{nh}")
                dve(lambda e, ss=s, ee=e_, p=ob_: e.scalar_tensor_tensor(
                    mtmp.ap, PSF.ap()[:, p * 512:(p + 2) * 512], cwb.ap[:, ss, ee:ee + 1],
                    ls2t.ap, op0=ALU.mult, op1=ALU.mult),
                    r=[ps_spec(ob_, ob_ + 2), cwb.spec3(s, e_, e_ + 1), ls2t.spec()],
                    w=[mtmp.spec()], tag=f"oc_{e_}_{s}")
                dve(lambda e, ss=s: e.tensor_tensor(ACCa(ss), ACCa(ss),
                                                    mtmp.ap, op=ALU.add),
                    r=[mtmp.spec(), ACCs(s)], w=[ACCs(s)],
                    tag=f"oa_{e_}_{s}")
                if pulls:
                    pulls.pop(0)()

    for e_ in range(NE - 1):
        moe_fc(e_, 0)
        moe_out(e_, 0)
        moe_fc(e_, 1)
        moe_out(e_, 1)
    moe_fc(NE - 1, 0)
    moe_out(NE - 1, 0)
    moe_fc(NE - 1, 1)
    MIT.free()
    for bb in w1r + w3r:
        bb.free()

    N1T = B("N1T", [128, IC, TOK], BF16)
    n0r2 = [B(f"n0q{i}", [128, D], F32) for i in range(2)]
    x0r2 = [B(f"x0q{i}", [128, D], F32) for i in range(2)]
    etmps = [B(f"etmp_{i}", [128, D], F32) for i in range(2)]
    n1bs = [B(f"n1b_{i}", [128, D], BF16) for i in range(2)]

    def lns(x_ap, x_spec, out_ap, out_spec, slot, tag):
        c = 8 * slot
        s6 = 16 * slot

        def sc(a, b2):
            return stt.ap[:, c + a:c + b2], stt.spec(c + a, c + b2)
        return [
            lambda: dve(lambda e: e.bn_stats(st6.ap[:, s6:s6 + 6], x_ap[:, 0:512]),
                        r=[x_spec], w=[st6.spec(s6, s6 + 6)], tag=tag + "b0"),
            lambda: dve(lambda e: e.bn_stats(st6.ap[:, s6 + 6:s6 + 12], x_ap[:, 512:1024]),
                        r=[x_spec], w=[st6.spec(s6 + 6, s6 + 12)], tag=tag + "b1"),
            lambda: dve(lambda e: e.bn_aggr(sc(4, 6)[0], st6.ap[:, s6:s6 + 12]),
                        r=[st6.spec(s6, s6 + 12)], w=[sc(4, 6)[1]], tag=tag + "ag"),
            lambda: act(AF.Sqrt, sc(2, 3)[0], sc(5, 6)[0], r=[sc(5, 6)[1]],
                        w=[sc(2, 3)[1]], bias=float(EPS), tag=tag + "sd"),
            lambda: dve(lambda e: e.reciprocal(sc(1, 2)[0], sc(2, 3)[0]),
                        r=[sc(2, 3)[1]], w=[sc(1, 2)[1]], tag=tag + "rs"),
            lambda: dve(lambda e: e.tensor_scalar(out_ap, x_ap, sc(4, 5)[0], sc(1, 2)[0],
                                                  op0=ALU.subtract, op1=ALU.mult),
                        r=[x_spec, sc(4, 5)[1], sc(1, 2)[1]], w=[out_spec], tag=tag + "ap"),
        ]

    def p6_combine_stages(s):
        tl = s * 128
        par = (s + s // 8) % 2
        nb, xb = n0r2[par], x0r2[par]
        etmp, n1b = etmps[par], n1bs[par]

        def st_loads():
            dma("qpool", nb.ap, sp_n0[tl:tl + 128, :], r=[("hb_n0", tl, tl + 128)],
                w=[nb.spec()], tag=f"n0q_{s}")
            dma("qsp", xb.ap, din["x0"][tl:tl + 128, :], w=[xb.spec()], tag=f"x0q_{s}")
        stages = [st_loads]
        stages.append(lambda: dve(lambda e, ss=s, n=nb, t_=etmp: e.scalar_tensor_tensor(
            t_.ap, ACCa(ss), wsum.ap[:, ss:ss + 1], n.ap,
            op0=ALU.mult, op1=ALU.add),
            r=[ACCs(s), wsum.spec(s, s + 1), nb.spec()],
            w=[etmp.spec()], tag=f"bo_{s}"))
        stages += lns(etmp.ap, etmp.spec(), nb.ap, nb.spec(), par, f"l1_{s}")
        stages.append(lambda: dve(lambda e, ss=s, x=xb, t_=nb: e.tensor_tensor(
            ACCa(ss), t_.ap, x.ap, op=ALU.add),
            r=[nb.spec(), xb.spec()], w=[ACCs(s)], tag=f"x1_{s}"))
        stages += lns(ACCa(s), ACCs(s), n1b.ap, n1b.spec(), 2 + par, f"l2_{s}")

        def st_tr():
            for ic in range(IC):
                tb_ = 6 + ic % 2
                transp(psbb(tb_, 0, 128, 0, 128), n1b.ap[:, ic * 128:(ic + 1) * 128],
                       r=[n1b.spec(ic * 128, (ic + 1) * 128)], w=[ps_spec(tb_)],
                       tag=f"n1t_{s}_{ic}")
                dve(lambda e, i=ic, t=tl, b=tb_: e.tensor_copy(
                    N1T.ap[:, i, t:t + 128], psbb(b, 0, 128, 0, 128)),
                    r=[ps_spec(tb_)], w=[N1T.spec3(ic, tl, tl + 128)], tag=f"n1e_{s}_{ic}")
        stages.append(st_tr)
        return stages

    def p6_combine_pair(s0, s1):
        ca_, cb2 = p6_combine_stages(s0), p6_combine_stages(s1)
        for k in range(len(ca_)):
            ca_[k]()
            cb2[k]()

    moe_out(NE - 1, 1, pulls=[(lambda j8=j8: p6_combine_pair(j8, 8 + j8))
                              for j8 in range(8)])

    HT.free(); mtmp.free()
    for bb in w2q:
        bb.free()
    for bb in s1bs:
        bb.free()

    wtRb = B("wtRb", [128, 4, 512], BF16)
    wtIb = B("wtIb", [128, 4, 512], BF16)
    wtInb = B("wtInb", [128, 4, 512], BF16)
    biasRt = B("biasRt", [128, 512], F32)
    biasIt = B("biasIt", [128, 512], F32)
    bcast_load("qsp", biasRt, din["bias_r"])
    bcast_load("qsp", biasIt, din["bias_i"])
    dma("qsp", wtRb.ap, wt_scr[0], r=[("hb_wt", 0, 1)], w=[wtRb.spec()], tag="wtRl")
    dma("qsp", wtIb.ap, wt_scr[1], r=[("hb_wt", 1, 2)], w=[wtIb.spec()], tag="wtIl")
    dve(lambda e: e.tensor_scalar(wtInb.ap, wtIb.ap, -1.0, None, op0=ALU.mult),
        r=[wtIb.spec()], w=[wtInb.spec()], tag="wtInb")

    ffss = [B(f"ffs_{i}", [128, D], F32) for i in range(3)]
    fts = [B(f"fts_{i}", [128, D], F32) for i in range(3)]
    outs = [B(f"outs{i}", [128, D], F32) for i in range(3)]

    def p6_ffn_stages(s, wide=2):
        tl = s * 128
        etmp, ffs = fts[s % wide], ffss[s % wide]
        fb0 = (s % wide) * 2

        def st_fr():
            for k, wt in ((0, wtRb), (4, wtInb)):
                for i4 in range(4):
                    mm(psf_bank(fb0), N1T.ap[:, k // 4 * 4 + i4, tl:tl + 128],
                       wt.ap[:, i4, :], (k + i4) == 0, (k + i4) == 7,
                       r=[N1T.spec3(k // 4 * 4 + i4, tl, tl + 128), wt.spec3(i4, 0, 512)],
                       w=[ps_spec(fb0)], tag=f"fr_{s}_{k}_{i4}")

        def st_fi():
            for k, wt in ((0, wtIb), (4, wtRb)):
                for i4 in range(4):
                    mm(psf_bank(fb0 + 1), N1T.ap[:, k // 4 * 4 + i4, tl:tl + 128],
                       wt.ap[:, i4, :], (k + i4) == 0, (k + i4) == 7,
                       r=[N1T.spec3(k // 4 * 4 + i4, tl, tl + 128), wt.spec3(i4, 0, 512)],
                       w=[ps_spec(fb0 + 1)], tag=f"fi_{s}_{k}_{i4}")
        stages = [st_fr, st_fi]
        stages.append(lambda: dve(lambda e, p=fb0, f_=ffs: e.tensor_tensor(
            f_.ap[:, 0:512], psf_bank(p), biasRt.ap, op=ALU.add),
            r=[ps_spec(fb0), biasRt.spec()], w=[ffs.spec(0, 512)], tag=f"fb_{s}"))
        stages.append(lambda: dve(lambda e, p=fb0, f_=ffs: e.tensor_tensor(
            f_.ap[:, 512:1024], psf_bank(p + 1), biasIt.ap, op=ALU.add),
            r=[ps_spec(fb0 + 1), biasIt.spec()], w=[ffs.spec(512, 1024)], tag=f"fb2_{s}"))
        stages += lns(ffs.ap, ffs.spec(), etmp.ap, etmp.spec(),
                      (4 + s % 2) if wide == 2 else (3 + s % 3), f"l3_{s}")

        def st_out():
            ob = outs[s % 3]
            dve(lambda e, ss=s, o=ob, t_=etmp: e.tensor_tensor(
                o.ap, t_.ap, ACCa(ss), op=ALU.add),
                r=[etmp.spec(), ACCs(s)], w=[ob.spec()], tag=f"o_{s}")
            dma("qact", out_d[tl:tl + 128, :], ob.ap, r=[ob.spec()],
                w=[("hb_out", tl, tl + 128)], tag=f"od_{s}")
        stages.append(st_out)
        return stages

    def p6_ffn_pair(s0, s1):
        fa, fb_ = p6_ffn_stages(s0), p6_ffn_stages(s1)
        for k in range(len(fa)):
            fa[k]()
            fb_[k]()

    def p6_ffn_multi(ss_list):
        chains = [p6_ffn_stages(s_, 3) for s_ in ss_list]
        for k in range(len(chains[0])):
            for ch in chains:
                ch[k]()

    for g0 in range(0, 16, 3):
        p6_ffn_multi(list(range(g0, min(g0 + 3, 16))))

    P.finalize()
    with nc.Block("init") as blk:
        @blk.gpsimd
        def _(e):
            for s in sems.values():
                e.sem_clear(s)
    with nc.Block("main") as blk:
        @blk.tensor
        def _(e):
            P.emit_engine("pe", e, sems)

        @blk.scalar
        def _(e):
            P.emit_engine("act", e, sems)
            for s, v in P.slot_total.items():
                if s.startswith("qact"):
                    e.wait_ge(sems[s], v)

        @blk.vector
        def _(e):
            P.emit_engine("dve", e, sems)

        @blk.gpsimd
        def _(e):
            P.emit_engine("pool", e, sems)
            for s, v in P.slot_total.items():
                if s.startswith("qpool"):
                    e.wait_ge(sems[s], v)

        @blk.sync
        def _(e):
            P.emit_engine("sp", e, sems)
            for s, v in P.slot_total.items():
                e.wait_ge(sems[s], v)
    st.close()
    return nc


def host_constants():
    ident = np.eye(128, dtype=BFDT)
    qsel = np.kron(np.eye(32, dtype=np.float32), np.ones((1, 16), np.float32))
    pselt = np.zeros((32, 512), np.float32)
    for t in range(4):
        blk = np.kron(np.eye(32, dtype=np.float32)[:, 8 * t:8 * t + 8],
                      np.ones((1, 16), np.float32))
        pselt[:, t * 128:(t + 1) * 128] = blk
    rselt = np.concatenate([np.eye(16, dtype=np.float32)] * 8, axis=1)
    ssel = np.tile(np.eye(16, dtype=np.float32), (1, 32)).astype(np.float32)
    return ident, qsel, pselt, rselt, ssel


_NC_CACHE = {}


def kernel(**inputs):
    x = np.ascontiguousarray(np.asarray(inputs["L0_attn_x"], dtype=np.float32))
    ident, qsel, pselt, rselt, ssel = host_constants()
    wmap = {
        "Wr": "L0_attn_Wr", "br": "L0_attn_br", "step_bias": "L0_attn_step_bias",
        "ln1_g": "L0_attn_ln1_g", "ln1_b": "L0_attn_ln1_b",
        "ln2_g": "L0_attn_ln2_g", "ln2_b": "L0_attn_ln2_b",
        "ls1": "L0_attn_ls1", "ls2": "L0_attn_ls2",
        "Wqkv": "L0_attn_Wqkv", "Wout": "L0_attn_Wout", "Wgate": "L0_attn_Wgate",
        "Win": "L0_attn_Win_ssm", "conv_w": "L0_attn_conv_w", "conv_b": "L0_attn_conv_b",
        "Wout_ssm": "L0_attn_Wout_ssm", "Wg": "L0_attn_Wg",
        "W1": "L0_attn_W1", "W2": "L0_attn_W2", "W3": "L0_attn_W3",
        "G1R": "L0_ffn_G1_R", "G2R": "L0_ffn_G2_R",
        "G1I": "L0_ffn_G1_I", "G2I": "L0_ffn_G2_I",
        "bias_r": "L0_ffn_bias_real", "bias_i": "L0_ffn_bias_imag",
    }
    weights = {k: np.ascontiguousarray(np.asarray(inputs[v], dtype=np.float32))
               for k, v in wmap.items()}
    in_maps = []
    for c in range(8):
        b, h = c // 2, c % 2
        x0 = np.ascontiguousarray(x[b, h * TOK:(h + 1) * TOK])
        xsib = np.ascontiguousarray(x[b, (1 - h) * TOK:(2 - h) * TOK])
        if h == 1:
            xp3 = np.ascontiguousarray(x[b, TOK - 3:TOK])
            cf = np.ones((1, 1), np.float32)
        else:
            xp3 = np.zeros((3, D), np.float32)
            cf = np.zeros((1, 1), np.float32)
        m = {"x0": x0, "xsib": xsib, "xp3": xp3, "cflag": cf,
             "ident": ident, "qsel": qsel, "pselt": pselt, "rselt": rselt,
             "ssel": ssel}
        m.update(weights)
        in_maps.append(m)
    if "nc" not in _NC_CACHE:
        _NC_CACHE["nc"] = build_nc()
    nc = _NC_CACHE["nc"]
    res = run_bass_kernel_spmd(nc, in_maps, core_ids=list(range(8)))
    out = np.empty((4, KV, D), np.float32)
    for c in range(8):
        b, h = c // 2, c % 2
        out[b, h * TOK:(h + 1) * TOK] = np.asarray(res.results[c]["out"],
                                                   dtype=np.float32)
    return out
